# version 01
# baseline (speedup 1.0000x reference)
import numpy as np
import concourse.bass as bass
import concourse.mybir as mybir

F32 = mybir.dt.float32
BF16 = mybir.dt.bfloat16
U32 = mybir.dt.uint32
I32 = mybir.dt.int32
AF = mybir.ActivationFunctionType
Alu = mybir.AluOpType
AX = mybir.AxisListType

L, D, H, DH, MLPD, NE, CAP, EH = 2048, 1024, 16, 64, 4096, 8, 4096, 2048
A3_STOP = None
NT, NC = L // 128, D // 128
EPS = 1e-5


class P:
    def __init__(self, nc):
        self.nc = nc
        self.prog = {e: [] for e in ("sync", "gpsimd", "vector", "scalar", "tensor")}
        self.sem = {}
        self.cnt = {}

    def new_sem(self, name):
        self.sem[name] = self.nc.semaphore(name).__enter__()
        self.cnt[name] = 0

    def emit(self, eng, fn, inc=None, inc_n=1):
        sem_h = self.sem[inc] if inc else None
        if inc:
            self.cnt[inc] += inc_n

        def run(e, fn=fn, sem_h=sem_h, inc_n=inc_n):
            r = fn(e)
            if sem_h is not None:
                r.then_inc(sem_h, inc_n)
        self.prog[eng].append(run)

    def wait(self, eng, sname, val=None):
        val = self.cnt[sname] if val is None else val
        h = self.sem[sname]
        self.prog[eng].append(lambda e, h=h, v=val: e.wait_ge(h, v))

    def val(self, sname):
        return self.cnt[sname]

    def barrier_all(self):
        for eng in ("sync", "gpsimd", "vector", "scalar", "tensor"):
            for s in ("ten", "vec", "scl", "sdma", "gdma", "odma", "qdma"):
                if self.cnt[s] > 0:
                    self.wait(eng, s)

    def fence(self, writers, sems=("ten", "vec", "scl")):
        for w in writers:
            for s in sems:
                if self.cnt[s] > 0:
                    self.wait(w, s)

    def finalize(self, block):
        for name, deco in (("sync", block.sync), ("gpsimd", block.gpsimd),
                           ("vector", block.vector), ("scalar", block.scalar),
                           ("tensor", block.tensor)):
            fns = list(self.prog[name])

            def make(fns):
                def _(eng):
                    for f in fns:
                        f(eng)
                return _
            deco(make(fns))


class B:

    def __init__(self, dbg=()):
        self.nc = bass.Bass("TRN2", target_bir_lowering=False)
        self.g = {}
        self.sb = {}
        self.cm = {}
        self.ps = {}
        self.p = P(self.nc)
        self.DBG = {}
        self.dbg_names = [d[0] for d in dbg]
        self.dbg_spec = dbg

    def dram_in(self, name, shape, dt=F32):
        self.g[name] = self.nc.dram_tensor(name, shape, dt, kind="ExternalInput").ap()

    def dram_out(self, name, shape, dt=F32):
        self.g[name] = self.nc.dram_tensor(name, shape, dt, kind="ExternalOutput").ap()

    def dram_tmp(self, name, shape, dt=F32):
        self.g[name] = self.nc.dram_tensor(name, shape, dt).ap()

    def sbuf(self, name, shape, dt=F32, side=None):
        self.cm[name] = self.nc.sbuf_tensor(name, shape, dt, side=side)
        self.sb[name] = self.cm[name].__enter__()
        return self.sb[name]

    def sfree(self, *names):
        for n in names:
            self.cm[n].__exit__(None, None, None)
            del self.cm[n]

    def psum(self, name, shape, dt=F32):
        self.ps[name] = self.nc.psum_tensor(name, shape, dt).__enter__()
        return self.ps[name]


def declare(b):
    b.dram_in("x", [L, D])
    for w in ("Wq", "Wk", "Wv", "Wfq", "Wfk"):
        b.dram_in(w, [D, D])
    for v in ("bq", "gq", "betaq", "bk", "gk", "betak", "bv", "gv", "betav",
              "ga", "ba", "bfq", "bfk", "b2"):
        b.dram_in(v, [1, D])
    b.dram_in("W1", [D, MLPD])
    b.dram_in("b1", [1, MLPD])
    b.dram_in("W2", [MLPD, D])
    b.dram_in("wg", [D, NE])
    b.dram_in("sel", [128, 16])
    b.dram_in("idrow", [128, 16])
    b.dram_in("eoff", [128, 1], U32)
    b.dram_in("ew1", [D, EH])
    b.dram_in("ew3", [D, EH])
    b.dram_in("ew2", [EH, D])
    if "mixer" in b.dbg_names:
        b.dram_out("mixer_dram", [L, D])
    else:
        b.dram_tmp("mixer_dram", [L, D])
    b.dram_tmp("y_dram", [L, D])
    b.dram_tmp("agy_dram", [L, D], BF16)
    b.dram_tmp("sag_dram", [NT, NE, 128])
    b.dram_out("y_out", [L, D])
    b.dram_out("s_out", [L, NE])
    for name, shape, dt in b.dbg_spec:
        if shape is None:
            continue
        b.dram_out("dbg_" + name, shape, dt)
        b.DBG[name] = b.g["dbg_" + name]
    for s in ("sdma", "gdma", "qdma", "vec", "scl", "ten", "odma"):
        b.p.new_sem(s)
    for n in ("pA", "pB", "pC", "pD", "pE", "pF", "pG", "pH"):
        b.psum(n, [128, 512], F32)


def stage_consts(b):
    p, sb, g, ps = b.p, b.sb, b.g, b.ps
    E, W = p.emit, p.wait
    b.sbuf("ident", [128, 128], F32)
    b.sbuf("identb", [128, 128], BF16)
    b.sbuf("onesr", [1, 128], F32)
    b.sbuf("onesc", [128, 1], BF16)
    b.sbuf("vt", [128, 48], F32)
    b.VT = {"gq": 0, "betaq": 8, "gk": 16, "betak": 24, "bfk": 32, "scr": 40}
    b.sbuf("b1t", [128, 32], F32)
    b.sbuf("rows", [1, D], F32)
    b.RW = {"b2": 0}
    b.sbuf("btile", [128, 5 * D], F32, side="right")
    b.sbuf("QnT", [128, NT * D], BF16, side="right")
    b.sbuf("KnT", [128, NT * D], BF16, side="right")
    b.BT = {"bq": 0, "bk": D, "bv": 2 * D, "gv": 3 * D, "betav": 4 * D}
    b.sbuf("epsc", [128, 1], F32)
    b.sbuf("qpbuf", [128, 64], F32)
    b.sbuf("selin", [128, 16], F32)
    b.sbuf("idrowin", [128, 16], F32)
    b.sbuf("lnscr", [128, 2 * 16], F32)
    b.sbuf("mvx", [128, 2 * 2], F32)
    b.sbuf("rstd", [128, 2], F32)
    b.sbuf("nbias", [128, 2], F32)
    b.sbuf("prescr", [128, 2 * D], F32)

    b.sbuf("ia", [128, 128], I32, side="right")
    b.sbuf("ib", [128, 128], I32, side="right")
    E("sync", lambda e: e.dma_start(out=sb["selin"][:, :], in_=g["sel"][:, :]), inc="sdma", inc_n=16)
    E("sync", lambda e: e.dma_start(out=sb["idrowin"][:, :], in_=g["idrow"][:, :]), inc="sdma", inc_n=16)
    E("gpsimd", lambda e: e.iota(sb["ia"][:, :], pattern=[[0, 128]], base=0, channel_multiplier=1))
    E("gpsimd", lambda e: e.iota(sb["ib"][:, :], pattern=[[1, 128]], base=0, channel_multiplier=0), inc="qdma")
    W("vector", "qdma")
    E("vector", lambda e: e.tensor_tensor(sb["ident"][:, :], sb["ia"][:, :], sb["ib"][:, :], Alu.is_equal), inc="vec")
    W("vector", "vec")
    E("vector", lambda e: e.tensor_copy(sb["identb"][:, :], sb["ident"][:, :]))
    E("vector", lambda e: e.memset(sb["onesr"][:, :], 1.0))
    E("vector", lambda e: e.memset(sb["onesc"][:, :], 1.0))
    E("vector", lambda e: e.memset(sb["epsc"][:, :], EPS), inc="vec")

    b.sbuf("vstage", [32, 128], F32, side="right")
    for nm in ("gq", "betaq", "gk", "betak", "bfk"):
        W("sync", "ten")
        E("sync", lambda e, nm=nm: e.dma_start(out=sb["vstage"][0:8, :],
                                               in_=g[nm].rearrange("o (a c) -> (o a) c", c=128)),
          inc="sdma", inc_n=16)
        W("tensor", "sdma")
        W("tensor", "vec")
        E("tensor", lambda e: e.transpose(ps["pG"][:, 0:8], sb["vstage"][0:8, :], sb["ident"][0:8, 0:8]), inc="ten")
        W("vector", "ten")
        E("vector", lambda e, c=b.VT[nm]: e.tensor_copy(sb["vt"][:, c:c + 8], ps["pG"][:, 0:8]), inc="vec")
        W("tensor", "vec")
    W("sync", "ten")
    E("sync", lambda e: e.dma_start(out=sb["vstage"][:, :],
                                    in_=g["b1"].rearrange("o (a c) -> (o a) c", c=128)), inc="sdma", inc_n=16)
    W("tensor", "sdma")
    E("tensor", lambda e: e.transpose(ps["pG"][:, 0:32], sb["vstage"][:, :], sb["ident"][0:32, 0:32]), inc="ten")
    W("vector", "ten")
    E("vector", lambda e: e.tensor_copy(sb["b1t"][:, :], ps["pG"][:, 0:32]), inc="vec")

    E("sync", lambda e: e.dma_start(out=sb["rows"][0:1, 0:D], in_=g["b2"][0:1, :]),
      inc="sdma", inc_n=16)
    b.sbuf("rowstage", [1, D], F32, side="right")
    W("tensor", "vec")

    def bcast_from(nm, dst_col):
        W("sync", "ten")
        E("sync", lambda e, nm=nm: e.dma_start(out=sb["rowstage"][0:1, :], in_=g[nm][0:1, :]),
          inc="sdma", inc_n=16)
        W("tensor", "sdma")
        for n in range(2):
            E("tensor", lambda e, n=n: e.matmul(ps["pG"][:, :], sb["onesr"][0:1, 0:128],
                                                sb["rowstage"][0:1, n * 512:(n + 1) * 512],
                                                start=True, stop=True), inc="ten")
            W("vector", "ten")
            E("vector", lambda e, n=n, c=dst_col: e.tensor_copy(
                sb["btile"][:, c + n * 512:c + (n + 1) * 512], ps["pG"][:, :]), inc="vec")
            W("tensor", "vec")

    for nm in ("bq", "bk", "bv", "gv", "betav"):
        bcast_from(nm, b.BT[nm])
    b.sfree("rowstage", "vstage", "ib", "ia")


def ln_tile(b, src_ap, out_ap, slot):
    p, sb = b.p, b.sb
    E, W = p.emit, p.wait
    s6, s2, s1 = slot * 16, slot * 2, slot
    E("vector", lambda e: e.bn_stats(sb["lnscr"][:, s6:s6 + 6], src_ap[:, 0:512]))
    E("vector", lambda e: e.bn_stats(sb["lnscr"][:, s6 + 6:s6 + 12], src_ap[:, 512:1024]), inc="vec")
    W("vector", "vec")
    E("vector", lambda e: e.bn_aggr(sb["mvx"][:, s2:s2 + 2], sb["lnscr"][:, s6:s6 + 12]), inc="vec")
    W("scalar", "vec")
    E("scalar", lambda e: e.activation(out=sb["rstd"][:, s1:s1 + 1], in_=sb["mvx"][:, s2 + 1:s2 + 2],
                                       func=AF.Sqrt, bias=sb["epsc"][:, 0:1]), inc="scl")
    W("vector", "scl")
    E("vector", lambda e: e.reciprocal(sb["rstd"][:, s1:s1 + 1], sb["rstd"][:, s1:s1 + 1]), inc="vec")
    W("vector", "vec")
    E("vector", lambda e: e.tensor_scalar(sb["nbias"][:, s1:s1 + 1], sb["mvx"][:, s2:s2 + 1],
                                          sb["rstd"][:, s1:s1 + 1], -1.0, op0=Alu.mult, op1=Alu.mult), inc="vec")
    W("scalar", "vec")
    E("scalar", lambda e: e.activation(out=out_ap, in_=src_ap, func=AF.Identity,
                                       scale=sb["rstd"][:, s1:s1 + 1], bias=sb["nbias"][:, s1:s1 + 1]), inc="scl")


def stage_a0(b):
    p, sb, g, ps = b.p, b.sb, b.g, b.ps
    E, W = p.emit, p.wait
    b.sbuf("xt", [128, 2 * D], F32, side="right")
    b.sbuf("x_fm", [128, NC * L], BF16, side="right")
    p.fence(("sync",))
    ten_after = {}
    for c in range(NT):
        buf = (c % 2) * D
        if c >= 2:
            W("sync", "ten", ten_after[c - 2])
        E("sync", lambda e, c=c, buf=buf: e.dma_start(out=sb["xt"][:, buf:buf + D],
                                                      in_=g["x"][c * 128:(c + 1) * 128, :]), inc="sdma", inc_n=16)
        W("tensor", "sdma")
        for k in range(NC):
            pk = ps["pG"] if k % 2 == 0 else ps["pH"]
            W("tensor", "vec")
            E("tensor", lambda e, k=k, buf=buf, pk=pk: e.transpose(
                pk[:, 0:128], sb["xt"][:, buf + k * 128:buf + (k + 1) * 128], sb["ident"][:, :]), inc="ten")
            W("vector", "ten")
            E("vector", lambda e, k=k, c=c, pk=pk: e.tensor_copy(
                sb["x_fm"][:, k * L + c * 128:k * L + (c + 1) * 128], pk[:, 0:128]), inc="vec")
        ten_after[c] = p.val("ten")
    W("tensor", "vec")


def stage_a1(b):
    p, sb, g, ps = b.p, b.sb, b.g, b.ps
    E, W = p.emit, p.wait
    b.sbuf("Ws", [128, 2 * NC * D], BF16, side="right")

    p.fence(("gpsimd",))

    def load_w(wi, wname):
        slot = wi % 2
        for k in range(NC):
            E("gpsimd", lambda e, slot=slot, k=k, wname=wname: e.dma_start(
                out=sb["Ws"][:, (slot * NC + k) * D:(slot * NC + k + 1) * D],
                in_=g[wname][k * 128:(k + 1) * 128, :]), inc="gdma", inc_n=16)
    load_w(0, "Wq")
    load_w(1, "Wk")
    b.sbuf("VfT", [128, NT * (D + 8)], BF16)
    b.sbuf("vscr", [128, D], F32, side="right")
    E("vector", lambda e: e.memset(
        sb["VfT"][:, :].rearrange("p (q c) -> p q c", c=129)[:, :, 128:129], 1.0), inc="vec")
    W("tensor", "gdma")

    PS6 = [(ps["pA"], ps["pB"]), (ps["pC"], ps["pD"]), (ps["pE"], ps["pF"])]
    evac_done = {}
    ln_done = {}
    mg = 0
    ten_after_w = {}
    for wi, outbuf in enumerate(("QnT", "KnT", "VfT")):
        bcol = [b.BT["bq"], b.BT["bk"], b.BT["bv"]][wi]
        W("tensor", "gdma", 16 * NC * (wi + 1))
        for m in range(NT):
            pp = PS6[mg % 3]
            if mg >= 3:
                W("tensor", "vec", evac_done[mg - 3])
            for k in range(NC):
                slot = wi % 2
                for n in range(2):
                    E("tensor", lambda e, m=m, k=k, n=n, slot=slot, pp=pp: e.matmul(
                        pp[n][:, :],
                        sb["x_fm"][:, k * L + m * 128:k * L + (m + 1) * 128],
                        sb["Ws"][:, (slot * NC + k) * D + n * 512:(slot * NC + k) * D + (n + 1) * 512],
                        start=(k == 0), stop=(k == NC - 1)),
                      inc="ten" if (k == NC - 1 and n == 1) else None)
            W("vector", "ten")
            if mg >= 2:
                W("vector", "scl", ln_done[mg - 2])
            slot = mg % 2
            pre = sb["prescr"][:, slot * D:(slot + 1) * D]
            for n in range(2):
                E("vector", lambda e, n=n, pp=pp, bcol=bcol, pre=pre: e.tensor_tensor(
                    pre[:, n * 512:(n + 1) * 512], pp[n][:, :],
                    sb["btile"][:, bcol + n * 512:bcol + (n + 1) * 512], Alu.add),
                  inc="vec" if n == 1 else None)
            evac_done[mg] = p.val("vec")
            W("vector", "vec")
            if wi == 2:
                out_ap = sb["VfT"][:, m * 1032:(m + 1) * 1032].rearrange(
                    "p (h c) -> p h c", c=129)[:, :, 0:128]
            else:
                out_ap = sb[outbuf][:, m * D:(m + 1) * D]
            if wi < 2:
                ln_tile(b, pre, out_ap, slot)
            else:
                ln_tile(b, pre, sb["vscr"][:, :], slot)
                W("vector", "scl")
                E("vector", lambda e: e.tensor_tensor(sb["vscr"][:, :], sb["vscr"][:, :],
                                                      sb["btile"][:, b.BT["gv"]:b.BT["gv"] + D], Alu.mult), inc="vec")
                W("vector", "vec")
                E("vector", lambda e, out_ap=out_ap: e.tensor_tensor(
                    out_ap, sb["vscr"][:, :], sb["btile"][:, b.BT["betav"]:b.BT["betav"] + D], Alu.add), inc="vec")
                evac_done[mg] = p.val("vec")
            ln_done[mg] = p.val("scl")
            mg += 1
        ten_after_w[wi] = p.val("ten")
        if wi == 0:
            W("gpsimd", "ten", ten_after_w[0])
            load_w(2, "Wv")
    W("vector", "scl")
    W("tensor", "scl")
    b.sfree("vscr", "Ws", "x_fm", "xt")


def dbg_dump_tokenmajor(b, name, buf, width=D, dt_rows=128):
    p, sb = b.p, b.sb
    E, W = p.emit, p.wait
    W("sync", "scl")
    W("sync", "vec")
    for m in range(NT):
        E("sync", lambda e, m=m: e.dma_start(out=b.DBG[name][m * 128:(m + 1) * 128, :],
                                             in_=sb[buf][:, m * width:(m + 1) * width]), inc="odma", inc_n=16)


def finish(b):
    blk = b.nc.Block()
    block = blk.__enter__()
    b.p.finalize(block)
    blk.__exit__(None, None, None)
    return b.nc


def stage_a2(b):
    p, sb, ps = b.p, b.sb, b.ps
    E, W = p.emit, p.wait
    b.sbuf("Kn_fm", [128, NC * L], BF16)
    b.sbuf("Qn_fm", [128, NC * L], BF16)
    b.QP = {"sumQ": 0, "qp": 8, "qpp": 16, "qpbk": 24, "tk0": 32, "scr": 40, "ks0": 48, "scr2": 56}
    for src, dst in (("KnT", "Kn_fm"), ("QnT", "Qn_fm")):
        for m in range(NT):
            for k in range(NC):
                pk = ps["pG"] if k % 2 == 0 else ps["pH"]
                W("tensor", "vec")
                E("tensor", lambda e, m=m, k=k, pk=pk, src=src: e.transpose(
                    pk[:, 0:64].bitcast(BF16), sb[src][:, m * D + k * 128:m * D + (k + 1) * 128],
                    sb["identb"][:, :]), inc="ten")
                W("vector", "ten")
                E("vector", lambda e, m=m, k=k, pk=pk, dst=dst: e.tensor_copy(
                    sb[dst][:, k * L + m * 128:k * L + (m + 1) * 128], pk[:, 0:64].bitcast(BF16)), inc="vec")
    for k in range(NC):
        for m in range(NT):
            E("tensor", lambda e, k=k, m=m: e.matmul(
                ps["pF"][:, k:k + 1], sb["QnT"][:, m * D + k * 128:m * D + (k + 1) * 128],
                sb["onesc"][:, 0:1], start=(m == 0), stop=(m == NT - 1)),
              inc="ten" if m == NT - 1 else None)
    W("vector", "ten")
    E("vector", lambda e: e.tensor_copy(sb["qpbuf"][:, 0:8], ps["pF"][:, 0:8]), inc="vec")
    b.sfree("KnT", "QnT", "btile")


def stage_a3(b):
    p, sb, g, ps, QP, VT = b.p, b.sb, b.g, b.ps, b.QP, b.VT
    E, W = p.emit, p.wait

    def SS(eng="vector", sem="vec"):
        W(eng, sem)

    b.sbuf("Wfbuf", [128, NC * D], BF16, side="right")
    b.sbuf("rows2", [1, 2 * D], F32, side="right")
    b.sbuf("lg", [16, 2052], F32, side="right")
    b.sbuf("qpbd", [128, 128], BF16, side="right")
    b.sbuf("qpbkbd", [128, 128], BF16, side="right")
    b.sbuf("score_t", [128, 256], F32, side="right")
    b.sbuf("KVd", [128, 8 * 130], BF16, side="right")
    b.sbuf("KsumF", [128, 8], F32, side="right")
    b.sbuf("s0f", [128, 8], F32, side="right")
    b.sbuf("vtb", [128, 16], BF16, side="right")
    b.sbuf("onesrb", [1, 128], BF16, side="right")
    b.sbuf("bfb", [1, D], BF16, side="right")
    b.sbuf("E1", [16, 128], BF16, side="right")
    b.sbuf("ch1", [16, 8], F32, side="right")
    b.sbuf("s0rhs", [16, 8], BF16, side="right")
    b.sbuf("smx", [16, 4], F32, side="right")
    b.sbuf("io1", [16, 128], I32, side="right")
    b.sbuf("io2", [16, 128], I32, side="right")
    b.sbuf("ioh", [16, 24], I32, side="right")
    b.sbuf("ioc", [16, 8], I32, side="right")
    b.sbuf("dall", [128, 16], F32, side="right")
    b.sbuf("phit", [128, D], BF16, side="right")
    b.sbuf("pKw", [128, NT * D], BF16, side="right")
    b.sbuf("kvscr", [128, 8 * 129], F32, side="right")
    b.sbuf("pqt", [128, 2 * D], BF16, side="right")

    W("vector", "vec")
    E("vector", lambda e: e.tensor_tensor(sb["qpbuf"][:, QP["scr"]:QP["scr"] + 8],
                                          sb["qpbuf"][:, 0:8], sb["vt"][:, VT["gq"]:VT["gq"] + 8], Alu.mult))
    E("vector", lambda e: e.tensor_scalar(sb["qpbuf"][:, QP["scr2"]:QP["scr2"] + 8],
                                          sb["vt"][:, VT["betaq"]:VT["betaq"] + 8], 0.125, None, op0=Alu.mult), inc="vec")
    SS()
    E("vector", lambda e: e.tensor_scalar(sb["qpbuf"][:, QP["scr"]:QP["scr"] + 8],
                                          sb["qpbuf"][:, QP["scr"]:QP["scr"] + 8], 1.0 / 16384.0, None, op0=Alu.mult), inc="vec")
    SS()
    E("vector", lambda e: e.tensor_tensor(sb["qpbuf"][:, QP["qp"]:QP["qp"] + 8],
                                          sb["qpbuf"][:, QP["scr"]:QP["scr"] + 8],
                                          sb["qpbuf"][:, QP["scr2"]:QP["scr2"] + 8], Alu.add), inc="vec")
    SS()
    E("vector", lambda e: e.tensor_tensor(sb["qpbuf"][:, QP["qpp"]:QP["qpp"] + 8],
                                          sb["qpbuf"][:, QP["qp"]:QP["qp"] + 8], sb["vt"][:, VT["gk"]:VT["gk"] + 8], Alu.mult))
    E("vector", lambda e: e.tensor_tensor(sb["qpbuf"][:, QP["qpbk"]:QP["qpbk"] + 8],
                                          sb["qpbuf"][:, QP["qp"]:QP["qp"] + 8], sb["vt"][:, VT["betak"]:VT["betak"] + 8], Alu.mult))
    E("vector", lambda e: e.memset(sb["qpbd"][:, :], 0.0))
    E("vector", lambda e: e.memset(sb["qpbkbd"][:, :], 0.0))
    E("vector", lambda e: e.memset(sb["onesrb"][0:1, :], 1.0), inc="vec")
    SS()
    for c in range(NC):
        E("vector", lambda e, c=c: e.tensor_copy(sb["qpbd"][0:64, c * 16 + 2 * c:c * 16 + 2 * c + 1],
                                                 sb["qpbuf"][0:64, QP["qpp"] + c:QP["qpp"] + c + 1]))
        E("vector", lambda e, c=c: e.tensor_copy(sb["qpbd"][64:128, c * 16 + 2 * c + 1:c * 16 + 2 * c + 2],
                                                 sb["qpbuf"][64:128, QP["qpp"] + c:QP["qpp"] + c + 1]))
        E("vector", lambda e, c=c: e.tensor_copy(sb["qpbkbd"][0:64, c * 16 + 2 * c:c * 16 + 2 * c + 1],
                                                 sb["qpbuf"][0:64, QP["qpbk"] + c:QP["qpbk"] + c + 1]))
        E("vector", lambda e, c=c: e.tensor_copy(sb["qpbkbd"][64:128, c * 16 + 2 * c + 1:c * 16 + 2 * c + 2],
                                                 sb["qpbuf"][64:128, QP["qpbk"] + c:QP["qpbk"] + c + 1]), inc="vec")

    qpbd_ready = p.val("vec")
    LB = [ps["pE"], ps["pF"], ps["pG"], ps["pH"]]
    W("tensor", "vec", qpbd_ready)
    for nt in range(4):
        for c in range(NC):
            E("tensor", lambda e, nt=nt, c=c: e.matmul(
                LB[nt][0:16, :], sb["qpbd"][:, c * 16:(c + 1) * 16],
                sb["Kn_fm"][:, c * L + nt * 512:c * L + (nt + 1) * 512],
                start=(c == 0), stop=(c == NC - 1)), inc="ten" if c == NC - 1 else None)
    for c in range(NC):
        E("tensor", lambda e, c=c: e.matmul(
            ps["pA"][0:16, 0:1], sb["qpbkbd"][:, c * 16:(c + 1) * 16], sb["onesc"][:, 0:1],
            start=(c == 0), stop=(c == NC - 1)), inc="ten" if c == NC - 1 else None)
    W("vector", "ten")
    for nt in range(4):
        E("vector", lambda e, nt=nt: e.tensor_copy(sb["lg"][0:16, 1 + nt * 512:1 + (nt + 1) * 512], LB[nt][0:16, :]))
    E("vector", lambda e: e.tensor_scalar(sb["lg"][0:16, 0:1], ps["pA"][0:16, 0:1], -1.0, None, op0=Alu.mult), inc="vec")
    SS()
    E("vector", lambda e: e.tensor_reduce(sb["smx"][0:16, 0:1], sb["lg"][0:16, 0:2049], axis=AX.X, op=Alu.max), inc="vec")
    SS()
    E("vector", lambda e: e.tensor_scalar(sb["smx"][0:16, 1:2], sb["smx"][0:16, 0:1], -1.0, None, op0=Alu.mult), inc="vec")
    W("scalar", "vec")
    E("scalar", lambda e: e.activation(out=sb["lg"][0:16, 0:2049], in_=sb["lg"][0:16, 0:2049],
                                       func=AF.Exp, bias=sb["smx"][0:16, 1:2]), inc="scl")
    W("vector", "scl")
    E("vector", lambda e: e.tensor_reduce(sb["smx"][0:16, 2:3], sb["lg"][0:16, 0:2049], axis=AX.X, op=Alu.add), inc="vec")
    SS()
    E("vector", lambda e: e.reciprocal(sb["smx"][0:16, 3:4], sb["smx"][0:16, 2:3]), inc="vec")
    W("scalar", "vec")
    E("scalar", lambda e: e.activation(out=sb["lg"][0:16, 0:2049], in_=sb["lg"][0:16, 0:2049],
                                       func=AF.Identity, scale=sb["smx"][0:16, 3:4]), inc="scl")
    score_done = p.val("scl")
    W("tensor", "scl", score_done)
    for c in range(NT):
        pk = ps["pG"] if c % 2 == 0 else ps["pH"]
        W("tensor", "vec")
        E("tensor", lambda e, c=c, pk=pk: e.transpose(pk[:, 0:16], sb["lg"][0:16, 1 + c * 128:1 + (c + 1) * 128],
                                                      sb["ident"][0:16, 0:16]), inc="ten")
        W("vector", "ten")
        E("vector", lambda e, c=c, pk=pk: e.tensor_copy(sb["score_t"][:, c * 16:(c + 1) * 16], pk[:, 0:16]), inc="vec")

    E("scalar", lambda e: e.activation(out=sb["qpbuf"][:, QP["tk0"]:QP["tk0"] + 8],
                                       in_=sb["vt"][:, VT["bfk"]:VT["bfk"] + 8], func=AF.Tanh), inc="scl")
    E("gpsimd", lambda e: e.iota(sb["io1"][0:16, 0:128], pattern=[[0, 128]], base=0, channel_multiplier=1))
    E("gpsimd", lambda e: e.iota(sb["io2"][0:16, 0:128], pattern=[[1, 2], [0, 64]], base=0, channel_multiplier=0))
    E("gpsimd", lambda e: e.iota(sb["ioh"][0:16, 0:8], pattern=[[0, 8]], base=0, channel_multiplier=1))
    E("gpsimd", lambda e: e.iota(sb["ioc"][0:16, 0:8], pattern=[[1, 8]], base=0, channel_multiplier=0), inc="qdma")
    W("vector", "qdma")
    E("vector", lambda e: e.tensor_scalar(sb["io1"][0:16, :], sb["io1"][0:16, :], 1, None, op0=Alu.bitwise_and),
      inc="vec")
    SS()
    E("vector", lambda e: e.tensor_tensor(sb["E1"][0:16, :], sb["io1"][0:16, :], sb["io2"][0:16, :], Alu.is_equal))
    E("vector", lambda e: e.tensor_scalar(sb["ioh"][0:16, 8:16], sb["ioh"][0:16, 0:8], 1, None,
                                          op0=Alu.bitwise_and), inc="vec")
    SS()
    E("vector", lambda e: e.tensor_tensor(sb["ioh"][0:16, 16:24], sb["ioh"][0:16, 0:8], sb["ioh"][0:16, 8:16],
                                          Alu.subtract), inc="vec")
    SS()
    E("vector", lambda e: e.tensor_scalar(sb["ch1"][0:16, :], sb["ioh"][0:16, 16:24], 0.5, None, op0=Alu.mult),
      inc="vec")
    SS()
    E("vector", lambda e: e.tensor_tensor(sb["ch1"][0:16, :], sb["ch1"][0:16, :], sb["ioc"][0:16, 0:8],
                                          Alu.is_equal), inc="vec")
    SS()
    W("vector", "scl", score_done)
    E("vector", lambda e: e.tensor_tensor(sb["s0rhs"][0:16, :], sb["ch1"][0:16, :],
                                          sb["lg"][0:16, 0:1].broadcast_to([16, 8]), Alu.mult), inc="vec")
    W("tensor", "vec")
    E("tensor", lambda e: e.matmul(ps["pA"][:, 0:8], sb["E1"][0:16, :], sb["s0rhs"][0:16, :],
                                   start=True, stop=True), inc="ten")
    W("vector", "ten")
    E("vector", lambda e: e.tensor_copy(sb["s0f"][:, 0:8], ps["pA"][:, 0:8]), inc="vec")

    def wf_load(wname, bname, vcol, gcol):
        W("gpsimd", "ten")
        for k in range(NC):
            E("gpsimd", lambda e, k=k, wname=wname: e.dma_start(
                out=sb["Wfbuf"][:, k * D:(k + 1) * D], in_=g[wname][k * 128:(k + 1) * 128, :]),
              inc="gdma", inc_n=16)
        E("sync", lambda e, bname=bname: e.dma_start(out=sb["rows2"][0:1, D:2 * D], in_=g[bname][0:1, :]),
          inc="sdma", inc_n=16)
        W("tensor", "gdma")
        W("tensor", "vec")
        for n in range(2):
            for c in range(NC):
                E("tensor", lambda e, n=n, c=c, vcol=vcol: e.matmul(
                    [ps["pA"], ps["pB"]][n][0:1, :], sb["vtb"][:, vcol + c:vcol + c + 1],
                    sb["Wfbuf"][:, c * D + n * 512:c * D + (n + 1) * 512],
                    start=(c == 0), stop=(c == NC - 1)), inc="ten" if c == NC - 1 else None)
        W("vector", "ten")
        W("vector", "sdma")
        for n in range(2):
            E("vector", lambda e, n=n: e.tensor_tensor(
                sb["rows2"][0:1, n * 512:(n + 1) * 512], [ps["pA"], ps["pB"]][n][0:1, :],
                sb["rows2"][0:1, D + n * 512:D + (n + 1) * 512], Alu.add), inc="vec")
        SS()
        E("vector", lambda e: e.tensor_copy(sb["bfb"][0:1, 0:D], sb["rows2"][0:1, 0:D]), inc="vec")
        bfb_done = p.val("vec")
        W("scalar", "ten")
        for k in range(NC):
            E("scalar", lambda e, k=k, gcol=gcol: e.activation(
                out=sb["Wfbuf"][:, k * D:(k + 1) * D], in_=sb["Wfbuf"][:, k * D:(k + 1) * D],
                func=AF.Identity, scale=sb["vt"][:, gcol + k:gcol + k + 1]), inc="scl")
        return bfb_done, p.val("scl")

    E("vector", lambda e: e.tensor_copy(sb["vtb"][:, 0:8], sb["vt"][:, VT["betaq"]:VT["betaq"] + 8]))
    E("vector", lambda e: e.tensor_copy(sb["vtb"][:, 8:16], sb["vt"][:, VT["betak"]:VT["betak"] + 8]), inc="vec")

    bfb_done, wf_scaled = wf_load("Wfk", "bfk", 8, VT["gk"])
    PP = [(ps["pA"], ps["pB"]), (ps["pC"], ps["pD"])]
    tanh_done = {}
    for m in range(NT):
        pp = PP[m % 2]
        W("tensor", "scl", wf_scaled)
        W("tensor", "vec", bfb_done)
        if m >= 2:
            W("tensor", "scl", tanh_done[m - 2])
        for k in range(NC):
            for n in range(2):
                E("tensor", lambda e, m=m, k=k, n=n, pp=pp: e.matmul(
                    pp[n][:, :], sb["Kn_fm"][:, k * L + m * 128:k * L + (m + 1) * 128],
                    sb["Wfbuf"][:, k * D + n * 512:k * D + (n + 1) * 512],
                    start=(k == 0), stop=False))
        for n in range(2):
            E("tensor", lambda e, n=n, pp=pp: e.matmul(
                pp[n][:, :], sb["onesrb"][0:1, 0:128], sb["bfb"][0:1, n * 512:(n + 1) * 512],
                start=False, stop=True), inc="ten")
        W("scalar", "ten")
        W("scalar", "vec")
        for n in range(2):
            E("scalar", lambda e, n=n, pp=pp: e.activation(
                out=sb["prescr"][:, n * 512:(n + 1) * 512], in_=pp[n][:, :], func=AF.Tanh), inc="scl")
        tanh_done[m] = p.val("scl")
        W("vector", "scl")
        for h in range(16):
            E("vector", lambda e, m=m, h=h: e.scalar_tensor_tensor(
                out=sb["pKw"][:, m * D + h * 64:m * D + (h + 1) * 64],
                in0=sb["prescr"][:, h * 64:(h + 1) * 64], scalar=1.0,
                in1=sb["score_t"][:, m * 16 + h:m * 16 + h + 1].broadcast_to([128, 64]),
                op0=Alu.add, op1=Alu.mult), inc="vec" if h == 15 else None)
    pkw_all = p.val("vec")

    W("tensor", "vec", pkw_all)
    kvdone = {}
    for hp in range(8):
        bank = ps["pE"] if hp % 2 == 0 else ps["pF"]
        if hp >= 2:
            W("tensor", "vec", kvdone[hp - 2])
        for m in range(NT):
            E("tensor", lambda e, m=m, hp=hp, bank=bank: e.matmul(
                bank[:, 0:129], sb["pKw"][:, m * D + hp * 128:m * D + (hp + 1) * 128],
                sb["VfT"][:, m * 1032 + hp * 129:m * 1032 + (hp + 1) * 129],
                start=(m == 0), stop=(m == NT - 1)), inc="ten" if m == NT - 1 else None)
        W("vector", "ten")
        E("vector", lambda e, hp=hp, bank=bank: e.tensor_copy(sb["kvscr"][:, hp * 129:(hp + 1) * 129],
                                                              bank[:, 0:129]),
          inc="vec")
        kvdone[hp] = p.val("vec")
    W("vector", "vec")
    E("vector", lambda e: e.tensor_tensor(sb["qpbuf"][:, QP["ks0"]:QP["ks0"] + 8],
                                          sb["qpbuf"][:, QP["tk0"]:QP["tk0"] + 8], sb["s0f"][:, 0:8], Alu.mult),
      inc="vec")
    SS()
    E("vector", lambda e: e.tensor_tensor(
        sb["KsumF"][:, 0:8],
        sb["kvscr"][:, :].rearrange("p (h c) -> p h c", c=129)[:, :, 128],
        sb["qpbuf"][:, QP["ks0"]:QP["ks0"] + 8], Alu.add), inc="vec")
    SS()
    E("vector", lambda e: e.tensor_tensor(sb["KsumF"][:, 0:8], sb["KsumF"][:, 0:8], sb["s0f"][:, 0:8], Alu.add),
      inc="vec")
    E("vector", lambda e: e.memset(sb["KVd"][:, :], 0.0), inc="vec")
    SS()
    for hp in range(8):
        E("vector", lambda e, hp=hp: e.tensor_copy(
            sb["KVd"][0:64, hp * 130:hp * 130 + 64], sb["kvscr"][0:64, hp * 129:hp * 129 + 64]))
        E("vector", lambda e, hp=hp: e.tensor_copy(
            sb["KVd"][64:128, hp * 130 + 64:hp * 130 + 128], sb["kvscr"][64:128, hp * 129 + 64:hp * 129 + 128]))
        E("vector", lambda e, hp=hp: e.tensor_copy(
            sb["KVd"][0:64, hp * 130 + 128:hp * 130 + 129], sb["KsumF"][0:64, hp:hp + 1]))
        E("vector", lambda e, hp=hp: e.tensor_copy(
            sb["KVd"][64:128, hp * 130 + 129:hp * 130 + 130], sb["KsumF"][64:128, hp:hp + 1]),
          inc="vec" if hp == 7 else None)
        kvd_ready = p.val("vec")
    if A3_STOP == "kvd":
        return

    bfb_done, wf_scaled = wf_load("Wfq", "bfq", 0, VT["gq"])
    mix_dma_done = {}
    evac_done2 = {}
    tanh_done = {}
    num_used = {}
    for m in range(NT):
        pp = PP[m % 2]
        W("tensor", "scl", wf_scaled)
        W("tensor", "vec", max(bfb_done, kvd_ready))
        if m >= 1:
            W("tensor", "vec", evac_done2[m - 1])
        if m >= 2:
            W("tensor", "scl", tanh_done[m - 2])
        for k in range(NC):
            for n in range(2):
                E("tensor", lambda e, m=m, k=k, n=n, pp=pp: e.matmul(
                    pp[n][:, :], sb["Qn_fm"][:, k * L + m * 128:k * L + (m + 1) * 128],
                    sb["Wfbuf"][:, k * D + n * 512:k * D + (n + 1) * 512],
                    start=(k == 0), stop=False))
        for n in range(2):
            E("tensor", lambda e, n=n, pp=pp: e.matmul(
                pp[n][:, :], sb["onesrb"][0:1, 0:128], sb["bfb"][0:1, n * 512:(n + 1) * 512],
                start=False, stop=True), inc="ten")
        W("scalar", "ten")
        for n in range(2):
            E("scalar", lambda e, n=n, pp=pp: e.activation(
                out=sb["phit"][:, n * 512:(n + 1) * 512], in_=pp[n][:, :], func=AF.Tanh), inc="scl")
        tanh_done[m] = p.val("scl")
        slot = m % 2
        W("tensor", "scl", tanh_done[m])
        if m >= 2:
            W("tensor", "ten", num_used[m - 2])
        for k in range(NC):
            pk = ps["pG"] if k % 2 == 0 else ps["pH"]
            W("tensor", "vec")
            E("tensor", lambda e, k=k, pk=pk: e.transpose(
                pk[:, 0:64].bitcast(BF16), sb["phit"][:, k * 128:(k + 1) * 128], sb["identb"][:, :]), inc="ten")
            W("vector", "ten")
            E("vector", lambda e, k=k, pk=pk, slot=slot: e.tensor_scalar(
                sb["pqt"][:, slot * D + k * 128:slot * D + (k + 1) * 128], pk[:, 0:64].bitcast(BF16),
                1.0, None, op0=Alu.add), inc="vec")
        pqt_done = p.val("vec")
        W("tensor", "vec", pqt_done)
        NSLOT = [(ps["pC"], 0), (ps["pC"], 130), (ps["pC"], 260), (ps["pD"], 0), (ps["pD"], 130),
                 (ps["pD"], 260), (ps["pE"], 0), (ps["pE"], 130)]
        for hp in range(8):
            bank, off = NSLOT[hp]
            E("tensor", lambda e, hp=hp, bank=bank, off=off, slot=slot: e.matmul(
                bank[:, off:off + 130], sb["pqt"][:, slot * D + hp * 128:slot * D + (hp + 1) * 128],
                sb["KVd"][:, hp * 130:(hp + 1) * 130], start=True, stop=True),
              inc="ten" if hp == 7 else None)
        num_used[m] = p.val("ten")
        W("vector", "ten")
        if m >= 2:
            W("vector", "odma", mix_dma_done[m - 2])
        mix = sb["prescr"][:, slot * D:(slot + 1) * D]
        for hp in range(8):
            bank, off = NSLOT[hp]
            E("vector", lambda e, hp=hp, bank=bank, off=off: e.tensor_scalar(
                sb["dall"][:, 2 * hp:2 * hp + 2], bank[:, off + 128:off + 130], 1e-6, None, op0=Alu.add),
              inc="vec" if hp == 7 else None)
        SS()
        E("vector", lambda e: e.reciprocal(sb["dall"][:, :], sb["dall"][:, :]), inc="vec")
        SS()
        for hp in range(8):
            bank, off = NSLOT[hp]
            for j in range(2):
                E("vector", lambda e, hp=hp, j=j, bank=bank, off=off, mix=mix: e.tensor_tensor(
                    mix[:, hp * 128 + j * 64:hp * 128 + (j + 1) * 64], bank[:, off + j * 64:off + (j + 1) * 64],
                    sb["dall"][:, 2 * hp + j:2 * hp + j + 1].broadcast_to([128, 64]), Alu.mult),
                  inc="vec" if (hp == 7 and j == 1) else None)
        evac_done2[m] = p.val("vec")
        W("scalar", "vec")
        E("scalar", lambda e, m=m, mix=mix: e.dma_start(out=g["mixer_dram"][m * 128:(m + 1) * 128, :], in_=mix),
          inc="odma", inc_n=16)
        mix_dma_done[m] = p.val("odma")
    W("vector", "odma")
    b.sfree("pqt", "kvscr", "pKw", "phit", "dall", "ioc", "ioh", "io2", "io1", "smx", "s0rhs", "ch1", "E1",
            "bfb", "onesrb", "vtb", "s0f", "KsumF", "KVd", "score_t", "qpbkbd", "qpbd", "lg", "rows2", "Wfbuf")
    b.sfree("Qn_fm", "Kn_fm", "VfT")




def stage_a4(b, dev_y=True):
    p, sb, g, ps = b.p, b.sb, b.g, b.ps
    E, W = p.emit, p.wait

    def SS(eng="vector", sem="vec"):
        W(eng, sem)

    b.sbuf("W1s", [128, NC * MLPD], BF16)
    b.sbuf("W2s", [128, 32 * D], BF16)
    b.sbuf("gabat", [128, 2 * D], F32, side="right")
    b.sbuf("aT", [128, 2 * D], F32, side="right")
    b.sbuf("a_fm", [128, NC * 256], BF16, side="right")
    b.sbuf("xbuf", [128, 2 * D], F32, side="right")
    b.sbuf("hscr", [128, D], F32, side="right")
    b.sbuf("yscr", [128, D], F32, side="right")
    b.sbuf("ybf", [128, D], BF16, side="right")
    b.sbuf("yfm", [128, NC * 128], F32, side="right")
    b.sbuf("sTt", [128, 8], F32, side="right")
    b.sbuf("wgs", [128, NC * 8], F32, side="right")
    b.sbuf("b2b", [1, D], BF16, side="right")
    b.sbuf("gslice", [128, 256], BF16, side="right")
    b.sbuf("smg", [128, 4], F32, side="right")
    b.sbuf("rowstage3", [1, D], F32, side="right")
    b.sbuf("onesrb4", [1, 128], BF16, side="right")

    p.fence(("gpsimd", "sync"))
    for k in range(NC):
        E("gpsimd", lambda e, k=k: e.dma_start(out=sb["W1s"][:, k * MLPD:(k + 1) * MLPD],
                                               in_=g["W1"][k * 128:(k + 1) * 128, :]), inc="gdma", inc_n=16)
    for k in range(32):
        E("gpsimd", lambda e, k=k: e.dma_start(out=sb["W2s"][:, k * D:(k + 1) * D],
                                               in_=g["W2"][k * 128:(k + 1) * 128, :]), inc="gdma", inc_n=16)
    for k in range(NC):
        E("sync", lambda e, k=k: e.dma_start(out=sb["wgs"][:, k * 8:(k + 1) * 8],
                                             in_=g["wg"][k * 128:(k + 1) * 128, :]), inc="sdma", inc_n=16)
    W("vector", "vec")
    E("vector", lambda e: e.memset(sb["onesrb4"][0:1, :], 1.0))
    E("vector", lambda e: e.tensor_copy(sb["b2b"][0:1, :], sb["rows"][0:1, 0:D]), inc="vec")
    for i, nm in enumerate(("ga", "ba")):
        W("sync", "ten")
        E("sync", lambda e, nm=nm: e.dma_start(out=sb["rowstage3"][0:1, :], in_=g[nm][0:1, :]), inc="sdma", inc_n=16)
        W("tensor", "sdma")
        for n in range(2):
            E("tensor", lambda e, n=n: e.matmul(ps["pG"][:, :], sb["onesr"][0:1, 0:128],
                                                sb["rowstage3"][0:1, n * 512:(n + 1) * 512],
                                                start=True, stop=True), inc="ten")
            W("vector", "ten")
            E("vector", lambda e, i=i, n=n: e.tensor_copy(
                sb["gabat"][:, i * D + n * 512:i * D + (n + 1) * 512], ps["pG"][:, :]), inc="vec")
            W("tensor", "vec")
    W("tensor", "gdma")
    W("sync", "odma")

    NCH = L // 256
    gelu_done = {}
    ydma_done = {}
    mixld_done = {}
    afm_done = {}
    evh_done = {}
    for tc in range(NCH):
        for t2 in range(2):
            ti = tc * 2 + t2
            slot = t2
            pre = sb["prescr"][:, slot * D:(slot + 1) * D]
            W("sync", "vec")
            W("sync", "scl")
            E("sync", lambda e, ti=ti, pre=pre: e.dma_start(out=pre, in_=g["mixer_dram"][ti * 128:(ti + 1) * 128, :]),
              inc="sdma", inc_n=16)
            E("sync", lambda e, ti=ti, slot=slot: e.dma_start(out=sb["xbuf"][:, slot * D:(slot + 1) * D],
                                                              in_=g["x"][ti * 128:(ti + 1) * 128, :]),
              inc="sdma", inc_n=16)
            W("vector", "sdma")
            ln_tile(b, pre, sb["hscr"][:, :], slot)
            W("vector", "scl")
            E("vector", lambda e: e.tensor_tensor(sb["hscr"][:, :], sb["hscr"][:, :],
                                                  sb["gabat"][:, 0:D], Alu.mult), inc="vec")
            SS()
            E("vector", lambda e, slot=slot: e.tensor_tensor(sb["aT"][:, slot * D:(slot + 1) * D],
                                                             sb["hscr"][:, :], sb["gabat"][:, D:2 * D],
                                                             Alu.add), inc="vec")
            at_done = p.val("vec")
            W("tensor", "vec", at_done)
            for k in range(NC):
                pk = ps["pG"] if k % 2 == 0 else ps["pH"]
                W("tensor", "vec")
                E("tensor", lambda e, k=k, pk=pk, slot=slot: e.transpose(
                    pk[:, 0:128], sb["aT"][:, slot * D + k * 128:slot * D + (k + 1) * 128],
                    sb["ident"][:, :]), inc="ten")
                W("vector", "ten")
                E("vector", lambda e, k=k, pk=pk, t2=t2: e.tensor_copy(
                    sb["a_fm"][:, k * 256 + t2 * 128:k * 256 + (t2 + 1) * 128], pk[:, 0:128]), inc="vec")
            W("vector", "ten")
            E("vector", lambda e, slot=slot: e.scalar_tensor_tensor(
                out=sb["aT"][:, slot * D:(slot + 1) * D], in0=sb["xbuf"][:, slot * D:(slot + 1) * D],
                scalar=2.0, in1=sb["aT"][:, slot * D:(slot + 1) * D], op0=Alu.mult, op1=Alu.add), inc="vec")
        afm_done[tc] = p.val("vec")

        W("tensor", "vec", afm_done[tc])
        W2B = [ps["pC"], ps["pD"], ps["pE"], ps["pF"]]
        for hc in range(32):
            pw = ps["pA"] if hc % 2 == 0 else ps["pB"]
            gi = tc * 32 + hc
            if gi >= 2:
                W("tensor", "scl", gelu_done[gi - 2])
            for k in range(NC):
                E("tensor", lambda e, hc=hc, k=k, pw=pw: e.matmul(
                    pw[:, 0:256], sb["W1s"][:, k * MLPD + hc * 128:k * MLPD + (hc + 1) * 128],
                    sb["a_fm"][:, k * 256:(k + 1) * 256], start=(k == 0), stop=(k == NC - 1)),
                  inc="ten" if k == NC - 1 else None)
            W("scalar", "ten")
            E("scalar", lambda e, hc=hc, pw=pw: e.activation(
                out=sb["gslice"][:, :], in_=pw[:, 0:256], func=AF.Gelu,
                bias=sb["b1t"][:, hc:hc + 1]), inc="scl")
            gelu_done[gi] = p.val("scl")
            W("tensor", "scl", gelu_done[gi])
            for ms in range(2):
                for n in range(2):
                    E("tensor", lambda e, hc=hc, ms=ms, n=n: e.matmul(
                        W2B[ms * 2 + n][:, :], sb["gslice"][:, ms * 128:(ms + 1) * 128],
                        sb["W2s"][:, hc * D + n * 512:hc * D + (n + 1) * 512],
                        start=(hc == 0), stop=False))
        for ms in range(2):
            for n in range(2):
                E("tensor", lambda e, ms=ms, n=n: e.matmul(
                    W2B[ms * 2 + n][:, :], sb["onesrb4"][0:1, 0:128], sb["b2b"][0:1, n * 512:(n + 1) * 512],
                    start=False, stop=True), inc="ten" if (ms == 1 and n == 1) else None)
        W("vector", "ten")
        for t2 in range(2):
            ti = tc * 2 + t2
            slot = t2
            W("vector", "scl")
            for n in range(2):
                E("vector", lambda e, t2=t2, n=n, slot=slot: e.tensor_tensor(
                    sb["hscr"][:, n * 512:(n + 1) * 512], W2B[t2 * 2 + n][:, :],
                    sb["aT"][:, slot * D + n * 512:slot * D + (n + 1) * 512], Alu.add),
                  inc="vec" if n == 1 else None)
            SS()
            W("scalar", "ten")
            if ti >= 1:
                W("scalar", "odma", ydma_done[ti - 1])
                W("vector", "odma", ydma_done[ti - 1])
            ln_tile(b, sb["hscr"][:, :], sb["yscr"][:, :], slot)
            W("vector", "scl")
            W("scalar", "scl")
            if dev_y:
                E("scalar", lambda e, ti=ti: e.dma_start(out=g["y_out"][ti * 128:(ti + 1) * 128, :],
                                                         in_=sb["yscr"][:, :]), inc="odma", inc_n=16)
            E("scalar", lambda e, ti=ti: e.dma_start(out=g["y_dram"][ti * 128:(ti + 1) * 128, :],
                                                     in_=sb["yscr"][:, :]), inc="odma", inc_n=16)
            E("vector", lambda e: e.tensor_copy(sb["ybf"][:, :], sb["yscr"][:, :]), inc="vec")
            W("scalar", "vec")
            E("scalar", lambda e, ti=ti: e.dma_start(out=g["agy_dram"][ti * 128:(ti + 1) * 128, :],
                                                     in_=sb["ybf"][:, :]), inc="odma", inc_n=16)
            for k in range(NC):
                pk = ps["pG"] if k % 2 == 0 else ps["pH"]
                W("tensor", "vec")
                W("tensor", "scl")
                E("tensor", lambda e, k=k, pk=pk: e.transpose(
                    pk[:, 0:128], sb["yscr"][:, k * 128:(k + 1) * 128], sb["ident"][:, :]), inc="ten")
                W("vector", "ten")
                E("vector", lambda e, k=k, pk=pk: e.tensor_copy(
                    sb["yfm"][:, k * 128:(k + 1) * 128], pk[:, 0:128]), inc="vec")
            yfm_done = p.val("vec")
            W("tensor", "vec", yfm_done)
            for k in range(NC):
                E("tensor", lambda e, k=k: e.matmul(
                    ps["pG"][:, 0:8], sb["yfm"][:, k * 128:(k + 1) * 128], sb["wgs"][:, k * 8:(k + 1) * 8],
                    start=(k == 0), stop=(k == NC - 1)), inc="ten" if k == NC - 1 else None)
            W("vector", "ten")
            E("vector", lambda e: e.tensor_reduce(sb["smg"][:, 0:1], ps["pG"][:, 0:8], axis=AX.X, op=Alu.max),
              inc="vec")
            SS()
            E("vector", lambda e: e.tensor_scalar(sb["smg"][:, 1:2], sb["smg"][:, 0:1], -1.0, None, op0=Alu.mult),
              inc="vec")
            W("scalar", "vec")
            E("scalar", lambda e: e.activation(out=sb["sTt"][:, :], in_=ps["pG"][:, 0:8], func=AF.Exp,
                                               bias=sb["smg"][:, 1:2]), inc="scl")
            W("vector", "scl")
            E("vector", lambda e: e.tensor_reduce(sb["smg"][:, 2:3], sb["sTt"][:, :], axis=AX.X, op=Alu.add),
              inc="vec")
            SS()
            E("vector", lambda e: e.reciprocal(sb["smg"][:, 3:4], sb["smg"][:, 2:3]), inc="vec")
            W("scalar", "vec")
            E("scalar", lambda e: e.activation(out=sb["sTt"][:, :], in_=sb["sTt"][:, :], func=AF.Identity,
                                               scale=sb["smg"][:, 3:4]), inc="scl")
            W("scalar", "scl")
            if dev_y:
                E("scalar", lambda e, ti=ti: e.dma_start(out=g["s_out"][ti * 128:(ti + 1) * 128, :],
                                                         in_=sb["sTt"][:, :]), inc="odma", inc_n=16)
            W("tensor", "scl")
            E("tensor", lambda e: e.transpose(ps["pH"][0:8, 0:128], sb["sTt"][:, :], sb["ident"][:, :]), inc="ten")
            W("vector", "ten")
            E("vector", lambda e: e.tensor_copy(sb["yfm"][0:8, 0:128], ps["pH"][0:8, 0:128]), inc="vec")
            W("scalar", "vec")
            E("scalar", lambda e, ti=ti: e.dma_start(out=g["sag_dram"][ti, :, :], in_=sb["yfm"][0:8, 0:128]),
              inc="odma", inc_n=16)
            ydma_done[ti] = p.val("odma")
    W("vector", "odma")
    W("scalar", "odma")
    b.sfree("onesrb4", "rowstage3", "smg", "gslice", "b2b", "wgs", "sTt", "yfm", "ybf", "yscr", "hscr",
            "xbuf", "a_fm", "aT", "gabat")
    b.sfree("W2s", "W1s")


def stage_r1(b):
    p, sb, g, ps = b.p, b.sb, b.g, b.ps
    E, W = p.emit, p.wait

    def SS(eng="vector", sem="vec"):
        W(eng, sem)

    b.dram_tmp("S2_dram", [128, NE, 128])
    b.dram_tmp("Yfull_dram", [NE * L, D], BF16)
    b.sbuf("S2", [128, NE * 128], F32)
    b.sbuf("selres", [8, 8], F32)
    b.sbuf("selmask", [128, NE * 128], F32)
    b.sbuf("sint", [8, 16], I32, side="right")
    b.sbuf("sbyte", [8, 4], F32, side="right")
    b.sbuf("ones4", [4, 128], F32, side="right")
    b.sbuf("cmpb", [128, NE * 128], F32, side="right")
    b.sbuf("red", [128, 32], F32, side="right")
    b.sbuf("tbits", [128, 8], I32, side="right")
    b.sbuf("tbscr", [128, 4 * 8], I32, side="right")
    b.sbuf("onescF", [128, 1], F32, side="right")
    b.sbuf("cntf", [8, 4], F32, side="right")
    b.sbuf("tvalf", [128, 128], F32, side="right")
    b.sbuf("eqm", [128, NE * 128], F32, side="right")
    b.sbuf("iot", [128, 128], I32, side="right")
    b.sbuf("mflt", [128, 8], F32, side="right")
    b.sbuf("xsp", [8, 8], F32, side="right")
    b.p.new_sem("cc")

    W("gpsimd", "odma")
    p.fence(("gpsimd", "sync"))
    E("gpsimd", lambda e: e.collective_compute(
        "AllGather", Alu.bypass, replica_groups=[list(range(8))],
        ins=[g["sag_dram"].opt()], outs=[g["S2_dram"].opt()]), inc="cc")
    E("gpsimd", lambda e: e.collective_compute(
        "AllGather", Alu.bypass, replica_groups=[list(range(8))],
        ins=[g["agy_dram"].opt()], outs=[g["Yfull_dram"].opt()]), inc="cc")
    W("sync", "cc", 1)
    E("sync", lambda e: e.dma_start(out=sb["S2"][:, :], in_=g["S2_dram"][:, :, :]), inc="sdma", inc_n=16)
    W("vector", "sdma")
    E("vector", lambda e: e.memset(sb["onescF"][:, :], 1.0))
    E("vector", lambda e: e.memset(sb["ones4"][:, :], 1.0))
    E("gpsimd", lambda e: e.iota(sb["iot"][:, :], pattern=[[1, 128]], base=0, channel_multiplier=128), inc="qdma")
    W("vector", "qdma")
    E("vector", lambda e: e.tensor_copy(sb["tvalf"][:, :], sb["iot"][:, :]), inc="vec")

    def bcast_exact(src_col):
        for j in range(4):
            E("vector", lambda e, j=j: e.tensor_scalar(
                sb["sint"][0:8, 8 + j:9 + j], sb["sint"][0:8, src_col:src_col + 1], 8 * j, None,
                op0=Alu.logical_shift_right), inc="vec" if j == 3 else None)
        SS()
        for j in range(4):
            E("vector", lambda e, j=j: e.tensor_scalar(
                sb["sint"][0:8, 8 + j:9 + j], sb["sint"][0:8, 8 + j:9 + j], 255, None,
                op0=Alu.bitwise_and), inc="vec" if j == 3 else None)
        SS()
        E("vector", lambda e: e.tensor_copy(sb["sbyte"][0:8, 0:4], sb["sint"][0:8, 8:12]), inc="vec")
        W("tensor", "vec")
        for j in range(4):
            E("tensor", lambda e, j=j: e.transpose(ps["pH"][0:1, 40 + j * 8:40 + (j + 1) * 8],
                                                   sb["sbyte"][0:8, j:j + 1], sb["ident"][0:8, 0:8]),
              inc="ten" if j == 3 else None)
        W("vector", "ten")
        E("vector", lambda e: e.tensor_copy(sb["red"][0:1, 0:32],
                                            ps["pH"][0:1, 40:72]), inc="vec")
        W("tensor", "vec")
        for j in range(4):
            E("tensor", lambda e, j=j: e.matmul(ps["pH"][:, j * 8:(j + 1) * 8], sb["onesr"][0:1, 0:128],
                                                sb["red"][0:1, j * 8:(j + 1) * 8], start=True, stop=True),
              inc="ten" if j == 3 else None)
        W("vector", "ten")
        for j in range(4):
            E("vector", lambda e, j=j: e.tensor_copy(sb["tbscr"][:, j * 8:(j + 1) * 8],
                                                     ps["pH"][:, j * 8:(j + 1) * 8]),
              inc="vec" if j == 3 else None)
        SS()
        for j in range(1, 4):
            E("vector", lambda e, j=j: e.tensor_scalar(
                sb["tbscr"][:, j * 8:(j + 1) * 8], sb["tbscr"][:, j * 8:(j + 1) * 8], 8 * j, None,
                op0=Alu.logical_shift_left), inc="vec" if j == 3 else None)
        SS()
        E("vector", lambda e: e.tensor_tensor(sb["tbits"][:, :], sb["tbscr"][:, 0:8], sb["tbscr"][:, 8:16],
                                              Alu.bitwise_or), inc="vec")
        SS()
        E("vector", lambda e: e.tensor_tensor(sb["tbits"][:, :], sb["tbits"][:, :], sb["tbscr"][:, 16:24],
                                              Alu.bitwise_or), inc="vec")
        SS()
        E("vector", lambda e: e.tensor_tensor(sb["tbits"][:, :], sb["tbits"][:, :], sb["tbscr"][:, 24:32],
                                              Alu.bitwise_or), inc="vec")
        SS()

    def count_gt(out_col):
        for e_ in range(NE):
            E("vector", lambda e, e_=e_: e.tensor_scalar(
                sb["cmpb"][:, e_ * 128:(e_ + 1) * 128], sb["S2"][:, e_ * 128:(e_ + 1) * 128],
                sb["tbits"].bitcast(F32)[:, e_:e_ + 1], None, op0=Alu.is_gt),
              inc="vec" if e_ == NE - 1 else None)
        SS()
        E("vector", lambda e: e.tensor_reduce(sb["red"][:, 0:8],
                                              sb["cmpb"][:, :].rearrange("p (e a) -> p e a", a=128),
                                              axis=AX.X, op=Alu.add), inc="vec")
        W("tensor", "vec")
        E("tensor", lambda e: e.matmul(ps["pH"][0:8, 32:33], sb["red"][:, 0:8], sb["onescF"][:, :],
                                       start=True, stop=True), inc="ten")
        W("vector", "ten")
        E("vector", lambda e, out_col=out_col: e.tensor_copy(sb["cntf"][0:8, out_col:out_col + 1],
                                                             ps["pH"][0:8, 32:33]), inc="vec")
        SS()

    E("vector", lambda e: e.memset(sb["xsp"][0:8, 0:8], 0.0), inc="vec")
    SS()
    for bit in range(30, -1, -1):
        hinc = float(1 << (bit - 16)) if bit >= 16 else 0.0
        linc = float(1 << bit) if bit < 16 else 0.0
        E("vector", lambda e, hinc=hinc: e.tensor_scalar(sb["xsp"][0:8, 2:3], sb["xsp"][0:8, 0:1],
                                                         hinc, None, op0=Alu.add))
        E("vector", lambda e, linc=linc: e.tensor_scalar(sb["xsp"][0:8, 3:4], sb["xsp"][0:8, 1:2],
                                                         linc, None, op0=Alu.add), inc="vec")
        SS()
        E("vector", lambda e: e.tensor_copy(sb["sint"][0:8, 2:3], sb["xsp"][0:8, 2:3]))
        E("vector", lambda e: e.tensor_copy(sb["sint"][0:8, 3:4], sb["xsp"][0:8, 3:4]), inc="vec")
        SS()
        E("vector", lambda e: e.tensor_scalar(sb["sint"][0:8, 2:3], sb["sint"][0:8, 2:3], 16, None,
                                              op0=Alu.logical_shift_left), inc="vec")
        SS()
        E("vector", lambda e: e.tensor_tensor(sb["sint"][0:8, 2:3], sb["sint"][0:8, 2:3],
                                              sb["sint"][0:8, 3:4], Alu.bitwise_or), inc="vec")
        SS()
        bcast_exact(2)
        count_gt(0)
        E("vector", lambda e: e.tensor_scalar(sb["xsp"][0:8, 4:5], sb["cntf"][0:8, 0:1], 4096.0, None,
                                              op0=Alu.is_ge), inc="vec")
        SS()
        E("vector", lambda e: e.tensor_copy(sb["sint"][0:8, 9:10], sb["xsp"][0:8, 4:5]), inc="vec")
        SS()
        E("vector", lambda e: e.select(sb["xsp"][0:8, 0:1], sb["sint"][0:8, 9:10],
                                       sb["xsp"][0:8, 2:3], sb["xsp"][0:8, 0:1]))
        E("vector", lambda e: e.select(sb["xsp"][0:8, 1:2], sb["sint"][0:8, 9:10],
                                       sb["xsp"][0:8, 3:4], sb["xsp"][0:8, 1:2]), inc="vec")
        SS()
    E("vector", lambda e: e.tensor_scalar(sb["xsp"][0:8, 1:2], sb["xsp"][0:8, 1:2], 1.0, None, op0=Alu.add),
      inc="vec")
    SS()
    E("vector", lambda e: e.tensor_scalar(sb["xsp"][0:8, 4:5], sb["xsp"][0:8, 1:2], 65535.0, None,
                                          op0=Alu.is_gt), inc="vec")
    SS()
    E("vector", lambda e: e.tensor_scalar(sb["xsp"][0:8, 5:6], sb["xsp"][0:8, 4:5], -65536.0, None,
                                          op0=Alu.mult), inc="vec")
    SS()
    E("vector", lambda e: e.tensor_tensor(sb["xsp"][0:8, 1:2], sb["xsp"][0:8, 1:2],
                                          sb["xsp"][0:8, 5:6], Alu.add))
    E("vector", lambda e: e.tensor_tensor(sb["xsp"][0:8, 0:1], sb["xsp"][0:8, 0:1],
                                          sb["xsp"][0:8, 4:5], Alu.add), inc="vec")
    SS()
    E("vector", lambda e: e.tensor_copy(sb["sint"][0:8, 1:2], sb["xsp"][0:8, 0:1]))
    E("vector", lambda e: e.tensor_copy(sb["sint"][0:8, 3:4], sb["xsp"][0:8, 1:2]), inc="vec")
    SS()
    E("vector", lambda e: e.tensor_scalar(sb["sint"][0:8, 1:2], sb["sint"][0:8, 1:2], 16, None,
                                          op0=Alu.logical_shift_left), inc="vec")
    SS()
    E("vector", lambda e: e.tensor_tensor(sb["sint"][0:8, 1:2], sb["sint"][0:8, 1:2],
                                          sb["sint"][0:8, 3:4], Alu.bitwise_or), inc="vec")
    SS()
    bcast_exact(1)
    count_gt(2)
    W("vector", "vec")
    for e_ in range(NE):
        E("vector", lambda e, e_=e_: e.tensor_scalar(
            sb["eqm"][:, e_ * 128:(e_ + 1) * 128], sb["S2"][:, e_ * 128:(e_ + 1) * 128],
            sb["tbits"].bitcast(F32)[:, e_:e_ + 1], None, op0=Alu.is_equal),
          inc="vec" if e_ == NE - 1 else None)
    SS()
    E("vector", lambda e: e.tensor_copy(sb["selres"][0:8, 0:1].bitcast(I32), sb["sint"][0:8, 1:2]))
    E("vector", lambda e: e.tensor_copy(sb["selres"][0:8, 2:3], sb["cntf"][0:8, 2:3]), inc="vec")
    E("vector", lambda e: e.memset(sb["xsp"][0:8, 5:6], 0.0), inc="vec")
    SS()
    for bit in range(14, -1, -1):
        E("vector", lambda e, v=float(1 << bit): e.tensor_scalar(sb["xsp"][0:8, 6:7], sb["xsp"][0:8, 5:6],
                                                                 v, None, op0=Alu.add), inc="vec")
        SS()
        E("vector", lambda e: e.tensor_copy(sb["sint"][0:8, 6:7], sb["xsp"][0:8, 6:7]), inc="vec")
        SS()
        bcast_exact(6)
        E("vector", lambda e: e.tensor_copy(sb["mflt"][:, :], sb["tbits"][:, :]), inc="vec")
        SS()
        for e_ in range(NE):
            E("vector", lambda e, e_=e_: e.tensor_scalar(
                sb["cmpb"][:, e_ * 128:(e_ + 1) * 128], sb["tvalf"][:, :],
                sb["mflt"][:, e_:e_ + 1], None, op0=Alu.is_lt),
              inc="vec" if e_ == NE - 1 else None)
        SS()
        E("vector", lambda e: e.tensor_tensor(sb["cmpb"][:, :], sb["cmpb"][:, :], sb["eqm"][:, :],
                                              Alu.mult), inc="vec")
        SS()
        E("vector", lambda e: e.tensor_reduce(sb["red"][:, 0:8],
                                              sb["cmpb"][:, :].rearrange("p (e a) -> p e a", a=128),
                                              axis=AX.X, op=Alu.add), inc="vec")
        W("tensor", "vec")
        E("tensor", lambda e: e.matmul(ps["pH"][0:8, 32:33], sb["red"][:, 0:8], sb["onescF"][:, :],
                                       start=True, stop=True), inc="ten")
        W("vector", "ten")
        E("vector", lambda e: e.tensor_copy(sb["cntf"][0:8, 0:1], ps["pH"][0:8, 32:33]), inc="vec")
        SS()
        E("vector", lambda e: e.tensor_tensor(sb["cntf"][0:8, 1:2], sb["cntf"][0:8, 0:1],
                                              sb["cntf"][0:8, 2:3], Alu.add), inc="vec")
        SS()
        E("vector", lambda e: e.tensor_scalar(sb["xsp"][0:8, 4:5], sb["cntf"][0:8, 1:2], 4096.0, None,
                                              op0=Alu.is_lt), inc="vec")
        SS()
        E("vector", lambda e: e.tensor_copy(sb["sint"][0:8, 9:10], sb["xsp"][0:8, 4:5]), inc="vec")
        SS()
        E("vector", lambda e: e.select(sb["xsp"][0:8, 5:6], sb["sint"][0:8, 9:10],
                                       sb["xsp"][0:8, 6:7], sb["xsp"][0:8, 5:6]), inc="vec")
        SS()
    E("vector", lambda e: e.tensor_scalar(sb["xsp"][0:8, 5:6], sb["xsp"][0:8, 5:6], 1.0, None, op0=Alu.add),
      inc="vec")
    SS()
    E("vector", lambda e: e.tensor_copy(sb["selres"][0:8, 1:2], sb["xsp"][0:8, 5:6]), inc="vec")
    SS()
    E("vector", lambda e: e.tensor_copy(sb["sint"][0:8, 6:7], sb["xsp"][0:8, 5:6]), inc="vec")
    SS()
    bcast_exact(6)
    E("vector", lambda e: e.tensor_copy(sb["mflt"][:, :], sb["tbits"][:, :]), inc="vec")
    SS()
    bcast_exact(1)
    for e_ in range(NE):
        E("vector", lambda e, e_=e_: e.tensor_scalar(
            sb["cmpb"][:, e_ * 128:(e_ + 1) * 128], sb["tvalf"][:, :],
            sb["mflt"][:, e_:e_ + 1], None, op0=Alu.is_lt), inc="vec" if e_ == NE - 1 else None)
    SS()
    E("vector", lambda e: e.tensor_tensor(sb["cmpb"][:, :], sb["cmpb"][:, :], sb["eqm"][:, :], Alu.mult),
      inc="vec")
    SS()
    for e_ in range(NE):
        E("vector", lambda e, e_=e_: e.tensor_scalar(
            sb["selmask"][:, e_ * 128:(e_ + 1) * 128], sb["S2"][:, e_ * 128:(e_ + 1) * 128],
            sb["tbits"].bitcast(F32)[:, e_:e_ + 1], None, op0=Alu.is_gt), inc="vec" if e_ == NE - 1 else None)
    SS()
    E("vector", lambda e: e.tensor_tensor(sb["selmask"][:, :], sb["selmask"][:, :], sb["cmpb"][:, :],
                                          Alu.add), inc="vec")
    SS()
    b.sfree("xsp", "mflt", "iot", "eqm", "tvalf", "cntf", "onescF", "tbscr", "tbits", "red", "cmpb",
            "ones4", "sbyte", "sint")


def stage_r2(b):
    p, sb, g, ps = b.p, b.sb, b.g, b.ps
    E, W = p.emit, p.wait
    from concourse.bass_core_rust import IndirectOffsetOnAxis

    def SS(eng="vector", sem="vec"):
        W(eng, sem)

    b.dram_tmp("IL_dram", [NE, 128, 32, 2])
    b.dram_tmp("ILfull_dram", [NE, 128, 32, 2])
    b.sbuf("LTu", [128, NE * 16], U32)
    b.sbuf("scanb", [128, 128], F32, side="right")
    b.sbuf("rank", [128, 128], F32, side="right")
    b.sbuf("rtot", [128, 4], F32, side="right")
    b.sbuf("tri", [128, 128], F32, side="right")
    b.sbuf("onesP", [128, 128], F32, side="right")
    b.sbuf("i1", [128, 128], I32, side="right")
    b.sbuf("i2", [128, 128], I32, side="right")
    b.sbuf("exi", [128, 8 * 16], F32, side="right")
    b.sbuf("exs", [128, 64], F32, side="right")
    b.sbuf("pairs", [128, 16 * 2], F32, side="right")
    b.sbuf("dsti", [128, 16], U32, side="right")
    b.sbuf("iscr", [128, 64], I32, side="right")
    b.sbuf("zz", [128, 512], F32, side="right")

    p.fence(("gpsimd", "sync"))
    E("gpsimd", lambda e: e.iota(sb["i1"][:, :], pattern=[[0, 128]], base=0, channel_multiplier=1))
    E("gpsimd", lambda e: e.iota(sb["i2"][:, :], pattern=[[1, 128]], base=0, channel_multiplier=0), inc="qdma")
    W("vector", "qdma")
    E("vector", lambda e: e.tensor_tensor(sb["tri"][:, :], sb["i1"][:, :], sb["i2"][:, :], Alu.is_lt))
    E("vector", lambda e: e.memset(sb["onesP"][:, :], 1.0))
    E("vector", lambda e: e.memset(sb["zz"][:, :], 0.0), inc="vec")
    W("sync", "vec")
    E("sync", lambda e: e.dma_start(out=g["IL_dram"][:, :, :, :], in_=sb["zz"][:, :]), inc="sdma", inc_n=16)
    zero_done = p.val("sdma")

    bcr = [None]

    def mk_bcr(e):
        if bcr[0] is None:
            bcr[0] = e.alloc_register("bcreg")
            e.reg_mov(bcr[0], 32767)
        return bcr[0]

    scat_cnt = [0]
    scat_after = {}
    for e_ in range(NE):
        if e_ > 0:
            W("vector", "gdma", scat_after[e_ - 1])
        msl = sb["selmask"][:, e_ * 128:(e_ + 1) * 128]
        W("vector", "vec")
        E("vector", lambda e, msl=msl: e.tensor_tensor_scan(sb["scanb"][:, :], sb["onesP"][:, :], msl,
                                                            op0=Alu.mult, op1=Alu.add, initial=0.0), inc="vec")
        SS()
        E("vector", lambda e: e.tensor_copy(sb["rtot"][:, 0:1], sb["scanb"][:, 127:128]), inc="vec")
        W("tensor", "vec")
        E("tensor", lambda e: e.matmul(ps["pG"][:, 0:1], sb["tri"][:, :], sb["rtot"][:, 0:1],
                                       start=True, stop=True), inc="ten")
        W("vector", "ten")
        E("vector", lambda e: e.tensor_copy(sb["rtot"][:, 1:2], ps["pG"][:, 0:1]), inc="vec")
        SS()
        E("vector", lambda e, msl=msl: e.tensor_tensor(sb["rank"][:, :], sb["scanb"][:, :], msl,
                                                       Alu.subtract), inc="vec")
        SS()
        E("vector", lambda e: e.tensor_scalar(sb["rank"][:, :], sb["rank"][:, :], sb["rtot"][:, 1:2], None,
                                              op0=Alu.add), inc="vec")
        SS()
        if "r2dbg" in b.DBG and e_ == 0:
            W("sync", "vec")
            E("sync", lambda e: e.dma_start(out=b.DBG["r2dbg"][0:128, 0:128], in_=sb["rank"][:, :]), inc="odma", inc_n=16)
            E("sync", lambda e: e.dma_start(out=b.DBG["r2dbg"][0:128, 128:132], in_=sb["rtot"][:, 0:4]), inc="odma", inc_n=16)
            E("sync", lambda e: e.dma_start(out=b.DBG["r2dbg"][0:128, 132:260], in_=sb["scanb"][:, :]), inc="odma", inc_n=16)
            E("sync", lambda e: e.dma_start(out=b.DBG["r2dbg"][0:128, 260:264], in_=sb["tri"][:, 40:44]), inc="odma", inc_n=16)
            W("vector", "odma")
        E("vector", lambda e: e.tensor_copy(sb["i1"][:, :], sb["rank"][:, :]), inc="vec")
        SS()
        E("vector", lambda e: e.tensor_scalar(sb["i2"][:, :], sb["i1"][:, :], 6, None,
                                              op0=Alu.logical_shift_right), inc="vec")
        SS()
        E("vector", lambda e: e.tensor_scalar(sb["i1"][:, :], sb["i1"][:, :], 63, None,
                                              op0=Alu.bitwise_and), inc="vec")
        SS()
        E("vector", lambda e: e.tensor_copy(sb["scanb"][:, :], sb["i2"][:, :]))
        E("vector", lambda e: e.tensor_copy(sb["rank"][:, :], sb["i1"][:, :]), inc="vec")
        ex_ready = p.val("vec")
        W("tensor", "vec", ex_ready)
        E("tensor", lambda e: e.matmul(ps["pG"][:, 0:16], sb["scanb"][:, :], g_sel(b), start=True, stop=True))
        E("tensor", lambda e: e.matmul(ps["pG"][:, 16:32], sb["rank"][:, :], g_sel(b), start=True, stop=True))
        E("tensor", lambda e, msl=msl: e.matmul(ps["pG"][:, 32:48], msl, g_sel(b), start=True, stop=True))
        E("tensor", lambda e, e_=e_: e.matmul(ps["pG"][:, 48:64], sb["S2"][:, e_ * 128:(e_ + 1) * 128],
                                              g_sel(b), start=True, stop=True), inc="ten")
        W("vector", "ten")
        E("vector", lambda e: e.tensor_copy(sb["exs"][:, 0:64], ps["pG"][:, 0:64]), inc="vec")
        SS()
        E("vector", lambda e: e.tensor_copy(sb["iscr"][:, 0:16], sb["exs"][:, 0:16]))
        E("vector", lambda e: e.tensor_copy(sb["iscr"][:, 16:32], sb["exs"][:, 16:32]), inc="vec")
        SS()
        E("vector", lambda e: e.tensor_scalar(sb["iscr"][:, 0:16], sb["iscr"][:, 0:16], 6, None,
                                              op0=Alu.logical_shift_left), inc="vec")
        SS()
        E("vector", lambda e: e.tensor_tensor(sb["iscr"][:, 0:16], sb["iscr"][:, 0:16], sb["iscr"][:, 16:32],
                                              Alu.bitwise_or), inc="vec")
        SS()
        E("vector", lambda e: e.tensor_copy(sb["iscr"][:, 32:48], sb["exs"][:, 32:48]), inc="vec")
        SS()
        E("vector", lambda e, e_=e_: e.memset(sb["LTu"][:, e_ * 16:(e_ + 1) * 16], 4096), inc="vec")
        SS()
        E("vector", lambda e, e_=e_: e.select(sb["LTu"][:, e_ * 16:(e_ + 1) * 16].bitcast(I32),
                                              sb["iscr"][:, 32:48], sb["iscr"][:, 0:16],
                                              sb["LTu"][:, e_ * 16:(e_ + 1) * 16].bitcast(I32)), inc="vec")
        SS()
        E("vector", lambda e: e.tensor_scalar(sb["iscr"][:, 48:64], sb["iscr"][:, 0:16], 127, None,
                                              op0=Alu.bitwise_and), inc="vec")
        SS()
        E("vector", lambda e: e.tensor_scalar(sb["iscr"][:, 48:64], sb["iscr"][:, 48:64], 5, None,
                                              op0=Alu.logical_shift_left), inc="vec")
        SS()
        E("vector", lambda e: e.tensor_scalar(sb["iscr"][:, 16:32], sb["iscr"][:, 0:16], 7, None,
                                              op0=Alu.logical_shift_right), inc="vec")
        SS()
        E("vector", lambda e: e.tensor_tensor(sb["iscr"][:, 48:64], sb["iscr"][:, 48:64], sb["iscr"][:, 16:32],
                                              Alu.bitwise_or), inc="vec")
        SS()
        E("vector", lambda e, e_=e_: e.tensor_scalar(sb["iscr"][:, 48:64], sb["iscr"][:, 48:64],
                                                     e_ << 12, None, op0=Alu.bitwise_or), inc="vec")
        SS()
        E("vector", lambda e: e.memset(sb["dsti"][:, :], 0x7FFFFFFF), inc="vec")
        SS()
        E("vector", lambda e: e.select(sb["dsti"][:, :].bitcast(I32), sb["iscr"][:, 32:48],
                                       sb["iscr"][:, 48:64], sb["dsti"][:, :].bitcast(I32)), inc="vec")
        SS()
        E("vector", lambda e: e.tensor_copy(sb["pairs"][:, :].rearrange("p (c t) -> p c t", t=2)[:, :, 0],
                                            g_idrow_s(b)))
        E("vector", lambda e: e.tensor_copy(sb["pairs"][:, :].rearrange("p (c t) -> p c t", t=2)[:, :, 1],
                                            sb["exs"][:, 48:64]), inc="vec")
        pair_done = p.val("vec")
        W("gpsimd", "vec", pair_done)
        if e_ == 0:
            W("gpsimd", "sdma", zero_done)
        for c in range(16):
            E("gpsimd", lambda e, e_=e_, c=c: e.indirect_dma_start(
                out=g["IL_dram"].rearrange("n p b t -> (n p b) t"),
                out_offset=IndirectOffsetOnAxis(sb["dsti"][:, c:c + 1], 0),
                in_=sb["pairs"][:, 2 * c:2 * c + 2], in_offset=None,
                bounds_check=mk_bcr(e), oob_is_err=False), inc="gdma", inc_n=16)
            scat_cnt[0] += 16
        scat_after[e_] = p.val("gdma")
    if "ILown" in b.DBG:
        W("sync", "gdma")
        E("sync", lambda e: e.dma_start(out=b.DBG["ILown"][:, :, :, :], in_=g["IL_dram"][:, :, :, :]),
          inc="odma", inc_n=16)
        W("sync", "vec")
        E("sync", lambda e: e.dma_start(out=b.DBG["dst0"][:, :], in_=sb["dsti"][:, :]), inc="odma", inc_n=16)
    W("gpsimd", "gdma")
    E("gpsimd", lambda e: e.collective_compute(
        "AllReduce", Alu.add, replica_groups=[list(range(8))],
        ins=[g["IL_dram"].opt()], outs=[g["ILfull_dram"].opt()]), inc="cc")
    b.sfree("zz", "iscr", "dsti", "pairs", "exs", "exi", "i2", "i1", "onesP", "tri", "rtot", "rank", "scanb")


def g_sel(b):
    return b.sb["selin"][:, :]


def g_idrow_s(b):
    return b.sb["idrowin"][:, :]


def stage_b(b):
    p, sb, g, ps = b.p, b.sb, b.g, b.ps
    E, W = p.emit, p.wait
    from concourse.bass_core_rust import IndirectOffsetOnAxis

    def SS(eng="vector", sem="vec"):
        W(eng, sem)

    b.dram_tmp("Zloc", [CAP + 1, D], BF16)
    b.dram_tmp("Zfull", [NE * (CAP + 1), D], BF16)
    b.sbuf("w13", [128, NC * 2 * EH], BF16)
    b.sbuf("w2s", [128, 16 * D], BF16)
    b.sbuf("ILs", [128, 64], F32, side="right")
    b.sbuf("idu", [128, 32], U32, side="right")
    b.sbuf("sclb", [128, 32], F32, side="right")
    b.sbuf("eoffs", [128, 1], U32, side="right")
    b.sbuf("yt", [128, 2 * D], BF16, side="right")
    b.sbuf("rt", [128, D], BF16, side="right")
    b.sbuf("r_fm", [128, NC * 512], BF16, side="right")
    b.sbuf("hh", [128, 16 * 512], BF16, side="right")
    b.sbuf("silb", [128, 512], BF16, side="right")
    b.sbuf("ztile", [128, 512], BF16, side="right")
    b.sbuf("zrow", [1, D], BF16, side="right")

    p.fence(("gpsimd", "sync"))
    for k in range(NC):
        E("gpsimd", lambda e, k=k: e.dma_start(out=sb["w13"][:, k * EH:(k + 1) * EH],
                                               in_=g["ew1"][k * 128:(k + 1) * 128, :]), inc="gdma", inc_n=16)
    for k in range(NC):
        E("gpsimd", lambda e, k=k: e.dma_start(out=sb["w13"][:, NC * EH + k * EH:NC * EH + (k + 1) * EH],
                                               in_=g["ew3"][k * 128:(k + 1) * 128, :]), inc="gdma", inc_n=16)
    for k in range(16):
        E("gpsimd", lambda e, k=k: e.dma_start(out=sb["w2s"][:, k * D:(k + 1) * D],
                                               in_=g["ew2"][k * 128:(k + 1) * 128, :]), inc="gdma", inc_n=16)
    E("sync", lambda e: e.dma_start(out=sb["eoffs"][:, :], in_=g["eoff"][:, :]), inc="sdma", inc_n=16)
    W("vector", "vec")
    E("vector", lambda e: e.memset(sb["zrow"][0:1, :], 0.0), inc="vec")
    W("sync", "vec")
    E("sync", lambda e: e.dma_start(out=g["Zloc"][CAP:CAP + 1, :], in_=sb["zrow"][0:1, :]), inc="sdma", inc_n=16)
    W("gpsimd", "cc", 3)
    W("gpsimd", "sdma")
    E("gpsimd", lambda e: e.indirect_dma_start(
        out=sb["ILs"][:, :], out_offset=None,
        in_=g["ILfull_dram"].rearrange("n p b t -> (n p) (b t)"),
        in_offset=IndirectOffsetOnAxis(sb["eoffs"][:, 0:1], 0)), inc="gdma", inc_n=16)
    W("vector", "gdma")
    E("vector", lambda e: e.tensor_copy(sb["idu"][:, :],
                                        sb["ILs"][:, :].rearrange("p (c t) -> p c t", t=2)[:, :, 0]))
    E("vector", lambda e: e.tensor_copy(sb["sclb"][:, :],
                                        sb["ILs"][:, :].rearrange("p (c t) -> p c t", t=2)[:, :, 1]), inc="vec")
    idu_done = p.val("vec")
    if "bdump" in b.DBG:
        W("sync", "vec")
        E("sync", lambda e: e.dma_start(out=b.DBG["bdump"][0:128, 0:32], in_=sb["idu"][:, :].bitcast(F32)),
          inc="odma", inc_n=16)
        E("sync", lambda e: e.dma_start(out=b.DBG["bdump"][0:128, 32:96], in_=sb["ILs"][:, :]), inc="odma", inc_n=16)
        W("vector", "odma")

    NCH = CAP // 512
    gat_done = {}
    rt_used = {}
    rfm_done = {}
    uv_done = {}
    z_dma_done = {}
    for ch in range(NCH):
        for j in range(4):
            bslot = ch * 4 + j
            ybuf = (bslot % 2) * D
            W("gpsimd", "vec", idu_done)
            if bslot >= 2:
                W("gpsimd", "scl", gat_done[bslot - 2])
            E("gpsimd", lambda e, bslot=bslot, ybuf=ybuf: e.indirect_dma_start(
                out=sb["yt"][:, ybuf:ybuf + D], out_offset=None,
                in_=g["Yfull_dram"][:, :],
                in_offset=IndirectOffsetOnAxis(sb["idu"][:, bslot:bslot + 1], 0)), inc="gdma", inc_n=16)
            W("scalar", "gdma")
            if bslot >= 1:
                W("scalar", "ten", rt_used[bslot - 1])
            E("scalar", lambda e, bslot=bslot, ybuf=ybuf: e.activation(
                out=sb["rt"][:, :], in_=sb["yt"][:, ybuf:ybuf + D], func=AF.Identity,
                scale=sb["sclb"][:, bslot:bslot + 1]), inc="scl")
            gat_done[bslot] = p.val("scl")
            if "bdump" in b.DBG and bslot == 0:
                W("sync", "scl")
                E("sync", lambda e, ybuf=ybuf: e.dma_start(out=b.DBG["bdump"][0:128, 96:96 + 512],
                                                           in_=sb["yt"][:, ybuf:ybuf + D].bitcast(F32)),
                  inc="odma", inc_n=16)
                W("vector", "odma")
                W("scalar", "odma")
            W("tensor", "scl", gat_done[bslot])
            for k in range(NC):
                pk = ps["pG"] if k % 2 == 0 else ps["pH"]
                W("tensor", "vec")
                E("tensor", lambda e, k=k, pk=pk: e.transpose(
                    pk[:, 0:64].bitcast(BF16), sb["rt"][:, k * 128:(k + 1) * 128], sb["identb"][:, :]),
                  inc="ten")
                W("vector", "ten")
                E("vector", lambda e, k=k, j=j, pk=pk: e.tensor_copy(
                    sb["r_fm"][:, k * 512 + j * 128:k * 512 + (j + 1) * 128], pk[:, 0:64].bitcast(BF16)),
                  inc="vec")
            rt_used[bslot] = p.val("ten")
        rfm_done[ch] = p.val("vec")
        W("tensor", "vec", rfm_done[ch])
        for hc in range(16):
            pu = ps["pA"] if hc % 2 == 0 else ps["pB"]
            pv = ps["pC"] if hc % 2 == 0 else ps["pD"]
            if hc >= 2:
                W("tensor", "vec", uv_done[(ch, hc - 2)])
            for k in range(NC):
                E("tensor", lambda e, hc=hc, k=k, pu=pu: e.matmul(
                    pu[:, :], sb["w13"][:, k * EH + hc * 128:k * EH + (hc + 1) * 128],
                    sb["r_fm"][:, k * 512:(k + 1) * 512], start=(k == 0), stop=(k == NC - 1)))
            for k in range(NC):
                E("tensor", lambda e, hc=hc, k=k, pv=pv: e.matmul(
                    pv[:, :], sb["w13"][:, NC * EH + k * EH + hc * 128:NC * EH + k * EH + (hc + 1) * 128],
                    sb["r_fm"][:, k * 512:(k + 1) * 512], start=(k == 0), stop=(k == NC - 1)), 
                  inc="ten" if k == NC - 1 else None)
            W("scalar", "ten")
            if hc >= 1:
                W("scalar", "vec", uv_done[(ch, hc - 1)])
            E("scalar", lambda e, pu=pu: e.activation(out=sb["silb"][:, :], in_=pu[:, :], func=AF.Silu),
              inc="scl")
            W("vector", "scl")
            E("vector", lambda e, hc=hc, pv=pv: e.tensor_tensor(
                sb["hh"][:, hc * 512:(hc + 1) * 512], sb["silb"][:, :], pv[:, :], Alu.mult), inc="vec")
            uv_done[(ch, hc)] = p.val("vec")
        W2B = [ps["pA"], ps["pB"], ps["pC"], ps["pD"], ps["pE"], ps["pF"], ps["pG"], ps["pH"]]
        W("tensor", "vec", uv_done[(ch, 15)])
        for ms in range(4):
            for n in range(2):
                bank = W2B[ms * 2 + n]
                for hc in range(16):
                    E("tensor", lambda e, ms=ms, n=n, hc=hc, bank=bank: e.matmul(
                        bank[:, :], sb["hh"][:, hc * 512 + ms * 128:hc * 512 + (ms + 1) * 128],
                        sb["w2s"][:, hc * D + n * 512:hc * D + (n + 1) * 512],
                        start=(hc == 0), stop=(hc == 15)),
                      inc="ten" if (ms == 3 and n == 1 and hc == 15) else None)
        W("vector", "ten")
        if ch >= 1:
            W("vector", "odma", z_dma_done[ch - 1])
        for ms in range(4):
            for n in range(2):
                E("vector", lambda e, ms=ms, n=n: e.tensor_copy(sb["ztile"][:, :], W2B[ms * 2 + n][:, :]),
                  inc="vec")
                SS()
                W("scalar", "vec")
                E("scalar", lambda e, ch=ch, ms=ms, n=n: e.dma_start(
                    out=g["Zloc"][ch * 512 + ms * 128:ch * 512 + (ms + 1) * 128, n * 512:(n + 1) * 512],
                    in_=sb["ztile"][:, :]), inc="odma", inc_n=16)
                W("vector", "odma")
        z_dma_done[ch] = p.val("odma")
    W("gpsimd", "odma")
    E("gpsimd", lambda e: e.collective_compute(
        "AllGather", Alu.bypass, replica_groups=[list(range(8))],
        ins=[g["Zloc"].opt()], outs=[g["Zfull"].opt()]), inc="cc")
    b.sfree("zrow", "ztile", "silb", "hh", "r_fm", "rt", "yt", "eoffs", "sclb", "idu", "ILs")
    b.sfree("w2s", "w13")


def stage_c(b):
    p, sb, g, ps = b.p, b.sb, b.g, b.ps
    E, W = p.emit, p.wait
    from concourse.bass_core_rust import IndirectOffsetOnAxis

    def SS(eng="vector", sem="vec"):
        W(eng, sem)

    b.sbuf("acc", [128, NT * D], F32)
    b.sbuf("zt2", [128, 2 * D], BF16, side="right")
    b.sbuf("cidx", [128, 16], U32, side="right")
    b.sbuf("oscr", [128, D], F32, side="right")
    p.fence(("gpsimd", "sync"))
    W("sync", "odma")
    E("sync", lambda e: e.dma_start(out=sb["acc"][:, :].rearrange("p (c d) -> p c d", d=D),
                                    in_=g["y_dram"].rearrange("(c p) d -> p c d", p=128)), inc="sdma", inc_n=16)
    W("vector", "sdma")
    W("gpsimd", "cc", 4)
    add_done = {}
    gath_done = {}
    for e_ in range(NE):
        W("vector", "gdma")
        E("vector", lambda e, e_=e_: e.tensor_scalar(sb["cidx"][:, :], sb["LTu"][:, e_ * 16:(e_ + 1) * 16],
                                                     float(e_ * (CAP + 1)), None, op0=Alu.add), inc="vec")
        cidx_done = p.val("vec")
        for c in range(NT):
            slot = ((e_ * NT + c) % 2) * D
            W("gpsimd", "vec", cidx_done)
            if e_ * NT + c >= 2:
                W("gpsimd", "vec", add_done[e_ * NT + c - 2])
            E("gpsimd", lambda e, c=c, slot=slot: e.indirect_dma_start(
                out=sb["zt2"][:, slot:slot + D], out_offset=None,
                in_=g["Zfull"][:, :],
                in_offset=IndirectOffsetOnAxis(sb["cidx"][:, c:c + 1], 0)), inc="gdma", inc_n=16)
            W("vector", "gdma")
            E("vector", lambda e, c=c, slot=slot: e.tensor_tensor(
                sb["acc"][:, c * D:(c + 1) * D], sb["acc"][:, c * D:(c + 1) * D],
                sb["zt2"][:, slot:slot + D], Alu.add), inc="vec")
            add_done[e_ * NT + c] = p.val("vec")
    out_dma = {}
    for c in range(NT):
        W("vector", "vec", add_done[7 * NT + c])
        slotl = c % 2
        if c >= 1:
            W("scalar", "odma", out_dma[c - 1])
        ln_tile(b, sb["acc"][:, c * D:(c + 1) * D], sb["oscr"][:, :], slotl)
        W("scalar", "scl")
        E("scalar", lambda e, c=c: e.dma_start(out=g["y_out"][c * 128:(c + 1) * 128, :],
                                               in_=sb["oscr"][:, :]), inc="odma", inc_n=16)
        out_dma[c] = p.val("odma")
    W("scalar", "odma")
    W("vector", "odma")
    W("sync", "odma")


from concourse.bass_utils import run_bass_kernel_spmd

_KEY = {
    "Wq": "L0_attn_Wq", "Wk": "L0_attn_Wk", "Wv": "L0_attn_Wv", "Wfq": "L0_attn_Wfq", "Wfk": "L0_attn_Wfk",
    "bq": "L0_attn_bq", "gq": "L0_attn_gq", "betaq": "L0_attn_betaq", "bk": "L0_attn_bk", "gk": "L0_attn_gk",
    "betak": "L0_attn_betak", "bv": "L0_attn_bv", "gv": "L0_attn_gv", "betav": "L0_attn_betav",
    "ga": "L0_attn_ga", "ba": "L0_attn_ba", "bfq": "L0_attn_bfq", "bfk": "L0_attn_bfk", "b2": "L0_attn_b2",
    "W1": "L0_attn_W1", "b1": "L0_attn_b1", "W2": "L0_attn_W2", "wg": "L0_ffn_wg",
}

_NC_CACHE = {}


def _build():
    if "nc" not in _NC_CACHE:
        b = B()
        declare(b)
        stage_consts(b)
        stage_a0(b)
        stage_a1(b)
        stage_a2(b)
        stage_a3(b)
        stage_a4(b, dev_y=False)
        stage_r1(b)
        stage_r2(b)
        stage_b(b)
        stage_c(b)
        _NC_CACHE["nc"] = finish(b)
    return _NC_CACHE["nc"]


def kernel(**inputs):
    nc = _build()
    inp = {k: np.ascontiguousarray(np.asarray(v)) for k, v in inputs.items()}
    shared = {}
    for mine, theirs in _KEY.items():
        a = inp[theirs]
        shared[mine] = a.reshape(1, -1).astype(np.float32) if a.ndim == 1 else a.astype(np.float32)
    x = inp["L0_attn_x"].astype(np.float32)
    w1f, w3f, w2f = inp["L0_ffn_w1"], inp["L0_ffn_w3"], inp["L0_ffn_w2"]
    in_maps = []
    for i in range(8):
        sel = np.zeros((128, 16), np.float32)
        for j in range(16):
            sel[16 * i + j, j] = 1.0
        idrow = ((16 * i + np.arange(16))[None, :] * 128 + np.arange(128)[:, None]).astype(np.float32)
        eoff = (i * 128 + np.arange(128))[:, None].astype(np.uint32)
        in_maps.append(dict(shared, x=np.ascontiguousarray(x[i]), sel=sel, idrow=idrow, eoff=eoff,
                            ew1=np.ascontiguousarray(w1f[i]).astype(np.float32),
                            ew3=np.ascontiguousarray(w3f[i]).astype(np.float32),
                            ew2=np.ascontiguousarray(w2f[i]).astype(np.float32)))
    res = run_bass_kernel_spmd(nc, in_maps, core_ids=list(range(8)))
    return np.stack([np.asarray(res.results[i]["y_out"]) for i in range(8)], 0)


# version 02
# speedup vs baseline: 1.0388x; 1.0388x over previous
import numpy as np
import concourse.bass as bass
import concourse.mybir as mybir

F32 = mybir.dt.float32
BF16 = mybir.dt.bfloat16
U32 = mybir.dt.uint32
I32 = mybir.dt.int32
AF = mybir.ActivationFunctionType
Alu = mybir.AluOpType
AX = mybir.AxisListType

L, D, H, DH, MLPD, NE, CAP, EH = 2048, 1024, 16, 64, 4096, 8, 4096, 2048
A3_STOP = None
NT, NC = L // 128, D // 128
EPS = 1e-5


class P:
    def __init__(self, nc):
        self.nc = nc
        self.prog = {e: [] for e in ("sync", "gpsimd", "vector", "scalar", "tensor")}
        self.sem = {}
        self.cnt = {}

    def new_sem(self, name):
        self.sem[name] = self.nc.semaphore(name).__enter__()
        self.cnt[name] = 0

    def emit(self, eng, fn, inc=None, inc_n=1):
        sem_h = self.sem[inc] if inc else None
        if inc:
            self.cnt[inc] += inc_n

        def run(e, fn=fn, sem_h=sem_h, inc_n=inc_n):
            r = fn(e)
            if sem_h is not None:
                r.then_inc(sem_h, inc_n)
        self.prog[eng].append(run)

    def wait(self, eng, sname, val=None):
        val = self.cnt[sname] if val is None else val
        h = self.sem[sname]
        self.prog[eng].append(lambda e, h=h, v=val: e.wait_ge(h, v))

    def val(self, sname):
        return self.cnt[sname]

    def barrier_all(self):
        for eng in ("sync", "gpsimd", "vector", "scalar", "tensor"):
            for s in ("ten", "vec", "scl", "sdma", "gdma", "odma", "qdma"):
                if self.cnt[s] > 0:
                    self.wait(eng, s)

    def fence(self, writers, sems=("ten", "vec", "scl")):
        for w in writers:
            for s in sems:
                if self.cnt[s] > 0:
                    self.wait(w, s)

    def finalize(self, block):
        for name, deco in (("sync", block.sync), ("gpsimd", block.gpsimd),
                           ("vector", block.vector), ("scalar", block.scalar),
                           ("tensor", block.tensor)):
            fns = list(self.prog[name])

            def make(fns):
                def _(eng):
                    for f in fns:
                        f(eng)
                return _
            deco(make(fns))


class B:

    def __init__(self, dbg=()):
        self.nc = bass.Bass("TRN2", target_bir_lowering=False)
        self.g = {}
        self.sb = {}
        self.cm = {}
        self.ps = {}
        self.p = P(self.nc)
        self.DBG = {}
        self.dbg_names = [d[0] for d in dbg]
        self.dbg_spec = dbg

    def dram_in(self, name, shape, dt=F32):
        self.g[name] = self.nc.dram_tensor(name, shape, dt, kind="ExternalInput").ap()

    def dram_out(self, name, shape, dt=F32):
        self.g[name] = self.nc.dram_tensor(name, shape, dt, kind="ExternalOutput").ap()

    def dram_tmp(self, name, shape, dt=F32, shared=False):
        if shared:
            self.g[name] = self.nc.dram_tensor(name, shape, dt, addr_space="Shared").ap()
        else:
            self.g[name] = self.nc.dram_tensor(name, shape, dt).ap()

    def sbuf(self, name, shape, dt=F32, side=None):
        self.cm[name] = self.nc.sbuf_tensor(name, shape, dt, side=side)
        self.sb[name] = self.cm[name].__enter__()
        return self.sb[name]

    def sfree(self, *names):
        for n in names:
            self.cm[n].__exit__(None, None, None)
            del self.cm[n]

    def psum(self, name, shape, dt=F32):
        self.ps[name] = self.nc.psum_tensor(name, shape, dt).__enter__()
        return self.ps[name]


def declare(b):
    b.dram_in("x", [L, D])
    for w in ("Wq", "Wk", "Wv", "Wfq", "Wfk"):
        b.dram_in(w, [D, D])
    for v in ("bq", "gq", "betaq", "bk", "gk", "betak", "bv", "gv", "betav",
              "ga", "ba", "bfq", "bfk", "b2"):
        b.dram_in(v, [1, D])
    b.dram_in("W1", [D, MLPD])
    b.dram_in("b1", [1, MLPD])
    b.dram_in("W2", [MLPD, D])
    b.dram_in("wg", [D, NE])
    b.dram_in("sel", [128, 16])
    b.dram_in("idrow", [128, 16])
    b.dram_in("eoff", [128, 1], U32)
    b.dram_in("ew1", [D, EH])
    b.dram_in("ew3", [D, EH])
    b.dram_in("ew2", [EH, D])
    if "mixer" in b.dbg_names:
        b.dram_out("mixer_dram", [L, D])
    else:
        b.dram_tmp("mixer_dram", [L, D])
    b.dram_tmp("y_dram", [L, D])
    b.dram_tmp("agy_dram", [L, D], BF16)
    b.dram_tmp("sag_dram", [NT, NE, 128])
    b.dram_out("y_out", [L, D])
    b.dram_out("s_out", [L, NE])
    for name, shape, dt in b.dbg_spec:
        if shape is None:
            continue
        b.dram_out("dbg_" + name, shape, dt)
        b.DBG[name] = b.g["dbg_" + name]
    for s in ("sdma", "gdma", "qdma", "vec", "scl", "ten", "odma"):
        b.p.new_sem(s)
    for n in ("pA", "pB", "pC", "pD", "pE", "pF", "pG", "pH"):
        b.psum(n, [128, 512], F32)


def stage_consts(b):
    p, sb, g, ps = b.p, b.sb, b.g, b.ps
    E, W = p.emit, p.wait
    b.sbuf("ident", [128, 128], F32)
    b.sbuf("identb", [128, 128], BF16)
    b.sbuf("onesr", [1, 128], F32)
    b.sbuf("onesc", [128, 1], BF16)
    b.sbuf("vt", [128, 48], F32)
    b.VT = {"gq": 0, "betaq": 8, "gk": 16, "betak": 24, "bfk": 32, "scr": 40}
    b.sbuf("b1t", [128, 32], F32)
    b.sbuf("rows", [1, D], F32)
    b.RW = {"b2": 0}
    b.sbuf("btile", [128, 5 * D], F32, side="right")
    b.sbuf("QnT", [128, NT * D], BF16, side="right")
    b.sbuf("KnT", [128, NT * D], BF16, side="right")
    b.BT = {"bq": 0, "bk": D, "bv": 2 * D, "gv": 3 * D, "betav": 4 * D}
    b.sbuf("epsc", [128, 1], F32)
    b.sbuf("qpbuf", [128, 64], F32)
    b.sbuf("selin", [128, 16], F32)
    b.sbuf("idrowin", [128, 16], F32)
    b.sbuf("lnscr", [128, 2 * 16], F32)
    b.sbuf("mvx", [128, 2 * 2], F32)
    b.sbuf("rstd", [128, 2], F32)
    b.sbuf("nbias", [128, 2], F32)
    b.sbuf("prescr", [128, 2 * D], F32)

    b.sbuf("ia", [128, 128], I32, side="right")
    b.sbuf("ib", [128, 128], I32, side="right")
    E("sync", lambda e: e.dma_start(out=sb["selin"][:, :], in_=g["sel"][:, :]), inc="sdma", inc_n=16)
    E("sync", lambda e: e.dma_start(out=sb["idrowin"][:, :], in_=g["idrow"][:, :]), inc="sdma", inc_n=16)
    E("gpsimd", lambda e: e.iota(sb["ia"][:, :], pattern=[[0, 128]], base=0, channel_multiplier=1))
    E("gpsimd", lambda e: e.iota(sb["ib"][:, :], pattern=[[1, 128]], base=0, channel_multiplier=0), inc="qdma")
    W("vector", "qdma")
    E("vector", lambda e: e.tensor_tensor(sb["ident"][:, :], sb["ia"][:, :], sb["ib"][:, :], Alu.is_equal), inc="vec")
    W("vector", "vec")
    E("vector", lambda e: e.tensor_copy(sb["identb"][:, :], sb["ident"][:, :]))
    E("vector", lambda e: e.memset(sb["onesr"][:, :], 1.0))
    E("vector", lambda e: e.memset(sb["onesc"][:, :], 1.0))
    E("vector", lambda e: e.memset(sb["epsc"][:, :], EPS), inc="vec")

    b.sbuf("vstage", [32, 128], F32, side="right")
    for nm in ("gq", "betaq", "gk", "betak", "bfk"):
        W("sync", "ten")
        E("sync", lambda e, nm=nm: e.dma_start(out=sb["vstage"][0:8, :],
                                               in_=g[nm].rearrange("o (a c) -> (o a) c", c=128)),
          inc="sdma", inc_n=16)
        W("tensor", "sdma")
        W("tensor", "vec")
        E("tensor", lambda e: e.transpose(ps["pG"][:, 0:8], sb["vstage"][0:8, :], sb["ident"][0:8, 0:8]), inc="ten")
        W("vector", "ten")
        E("vector", lambda e, c=b.VT[nm]: e.tensor_copy(sb["vt"][:, c:c + 8], ps["pG"][:, 0:8]), inc="vec")
        W("tensor", "vec")
    W("sync", "ten")
    E("sync", lambda e: e.dma_start(out=sb["vstage"][:, :],
                                    in_=g["b1"].rearrange("o (a c) -> (o a) c", c=128)), inc="sdma", inc_n=16)
    W("tensor", "sdma")
    E("tensor", lambda e: e.transpose(ps["pG"][:, 0:32], sb["vstage"][:, :], sb["ident"][0:32, 0:32]), inc="ten")
    W("vector", "ten")
    E("vector", lambda e: e.tensor_copy(sb["b1t"][:, :], ps["pG"][:, 0:32]), inc="vec")

    E("sync", lambda e: e.dma_start(out=sb["rows"][0:1, 0:D], in_=g["b2"][0:1, :]),
      inc="sdma", inc_n=16)
    b.sbuf("rowstage", [1, D], F32, side="right")
    W("tensor", "vec")

    def bcast_from(nm, dst_col):
        W("sync", "ten")
        E("sync", lambda e, nm=nm: e.dma_start(out=sb["rowstage"][0:1, :], in_=g[nm][0:1, :]),
          inc="sdma", inc_n=16)
        W("tensor", "sdma")
        for n in range(2):
            E("tensor", lambda e, n=n: e.matmul(ps["pG"][:, :], sb["onesr"][0:1, 0:128],
                                                sb["rowstage"][0:1, n * 512:(n + 1) * 512],
                                                start=True, stop=True), inc="ten")
            W("vector", "ten")
            E("vector", lambda e, n=n, c=dst_col: e.tensor_copy(
                sb["btile"][:, c + n * 512:c + (n + 1) * 512], ps["pG"][:, :]), inc="vec")
            W("tensor", "vec")

    for nm in ("bq", "bk", "bv", "gv", "betav"):
        bcast_from(nm, b.BT[nm])
    b.sfree("rowstage", "vstage", "ib", "ia")


def ln_tile(b, src_ap, out_ap, slot):
    p, sb = b.p, b.sb
    E, W = p.emit, p.wait
    s6, s2, s1 = slot * 16, slot * 2, slot
    E("vector", lambda e: e.bn_stats(sb["lnscr"][:, s6:s6 + 6], src_ap[:, 0:512]))
    E("vector", lambda e: e.bn_stats(sb["lnscr"][:, s6 + 6:s6 + 12], src_ap[:, 512:1024]), inc="vec")
    W("vector", "vec")
    E("vector", lambda e: e.bn_aggr(sb["mvx"][:, s2:s2 + 2], sb["lnscr"][:, s6:s6 + 12]), inc="vec")
    W("scalar", "vec")
    E("scalar", lambda e: e.activation(out=sb["rstd"][:, s1:s1 + 1], in_=sb["mvx"][:, s2 + 1:s2 + 2],
                                       func=AF.Sqrt, bias=sb["epsc"][:, 0:1]), inc="scl")
    W("vector", "scl")
    E("vector", lambda e: e.reciprocal(sb["rstd"][:, s1:s1 + 1], sb["rstd"][:, s1:s1 + 1]), inc="vec")
    W("vector", "vec")
    E("vector", lambda e: e.tensor_scalar(sb["nbias"][:, s1:s1 + 1], sb["mvx"][:, s2:s2 + 1],
                                          sb["rstd"][:, s1:s1 + 1], -1.0, op0=Alu.mult, op1=Alu.mult), inc="vec")
    W("scalar", "vec")
    E("scalar", lambda e: e.activation(out=out_ap, in_=src_ap, func=AF.Identity,
                                       scale=sb["rstd"][:, s1:s1 + 1], bias=sb["nbias"][:, s1:s1 + 1]), inc="scl")


def stage_a0(b):
    p, sb, g, ps = b.p, b.sb, b.g, b.ps
    E, W = p.emit, p.wait
    b.sbuf("xt", [128, 2 * D], F32, side="right")
    b.sbuf("x_fm", [128, NC * L], BF16, side="right")
    p.fence(("sync",))
    ten_after = {}
    for c in range(NT):
        buf = (c % 2) * D
        if c >= 2:
            W("sync", "ten", ten_after[c - 2])
        E("sync", lambda e, c=c, buf=buf: e.dma_start(out=sb["xt"][:, buf:buf + D],
                                                      in_=g["x"][c * 128:(c + 1) * 128, :]), inc="sdma", inc_n=16)
        W("tensor", "sdma")
        for k in range(NC):
            pk = ps["pG"] if k % 2 == 0 else ps["pH"]
            W("tensor", "vec")
            E("tensor", lambda e, k=k, buf=buf, pk=pk: e.transpose(
                pk[:, 0:128], sb["xt"][:, buf + k * 128:buf + (k + 1) * 128], sb["ident"][:, :]), inc="ten")
            W("vector", "ten")
            E("vector", lambda e, k=k, c=c, pk=pk: e.tensor_copy(
                sb["x_fm"][:, k * L + c * 128:k * L + (c + 1) * 128], pk[:, 0:128]), inc="vec")
        ten_after[c] = p.val("ten")
    W("tensor", "vec")


def stage_a1(b):
    p, sb, g, ps = b.p, b.sb, b.g, b.ps
    E, W = p.emit, p.wait
    b.sbuf("Ws", [128, 2 * NC * D], BF16, side="right")

    p.fence(("gpsimd",))

    def load_w(wi, wname):
        slot = wi % 2
        for k in range(NC):
            E("gpsimd", lambda e, slot=slot, k=k, wname=wname: e.dma_start(
                out=sb["Ws"][:, (slot * NC + k) * D:(slot * NC + k + 1) * D],
                in_=g[wname][k * 128:(k + 1) * 128, :]), inc="gdma", inc_n=16)
    load_w(0, "Wq")
    load_w(1, "Wk")
    b.sbuf("VfT", [128, NT * (D + 8)], BF16)
    b.sbuf("vscr", [128, D], F32, side="right")
    E("vector", lambda e: e.memset(
        sb["VfT"][:, :].rearrange("p (q c) -> p q c", c=129)[:, :, 128:129], 1.0), inc="vec")
    W("tensor", "gdma")

    PS6 = [(ps["pA"], ps["pB"]), (ps["pC"], ps["pD"]), (ps["pE"], ps["pF"])]
    evac_done = {}
    ln_done = {}
    mg = 0
    ten_after_w = {}
    for wi, outbuf in enumerate(("QnT", "KnT", "VfT")):
        bcol = [b.BT["bq"], b.BT["bk"], b.BT["bv"]][wi]
        W("tensor", "gdma", 16 * NC * (wi + 1))
        for m in range(NT):
            pp = PS6[mg % 3]
            if mg >= 3:
                W("tensor", "vec", evac_done[mg - 3])
            for k in range(NC):
                slot = wi % 2
                for n in range(2):
                    E("tensor", lambda e, m=m, k=k, n=n, slot=slot, pp=pp: e.matmul(
                        pp[n][:, :],
                        sb["x_fm"][:, k * L + m * 128:k * L + (m + 1) * 128],
                        sb["Ws"][:, (slot * NC + k) * D + n * 512:(slot * NC + k) * D + (n + 1) * 512],
                        start=(k == 0), stop=(k == NC - 1)),
                      inc="ten" if (k == NC - 1 and n == 1) else None)
            W("vector", "ten")
            if mg >= 2:
                W("vector", "scl", ln_done[mg - 2])
            slot = mg % 2
            pre = sb["prescr"][:, slot * D:(slot + 1) * D]
            for n in range(2):
                E("vector", lambda e, n=n, pp=pp, bcol=bcol, pre=pre: e.tensor_tensor(
                    pre[:, n * 512:(n + 1) * 512], pp[n][:, :],
                    sb["btile"][:, bcol + n * 512:bcol + (n + 1) * 512], Alu.add),
                  inc="vec" if n == 1 else None)
            evac_done[mg] = p.val("vec")
            W("vector", "vec")
            if wi == 2:
                out_ap = sb["VfT"][:, m * 1032:(m + 1) * 1032].rearrange(
                    "p (h c) -> p h c", c=129)[:, :, 0:128]
            else:
                out_ap = sb[outbuf][:, m * D:(m + 1) * D]
            if wi < 2:
                ln_tile(b, pre, out_ap, slot)
            else:
                ln_tile(b, pre, sb["vscr"][:, :], slot)
                W("vector", "scl")
                E("vector", lambda e: e.tensor_tensor(sb["vscr"][:, :], sb["vscr"][:, :],
                                                      sb["btile"][:, b.BT["gv"]:b.BT["gv"] + D], Alu.mult), inc="vec")
                W("vector", "vec")
                E("vector", lambda e, out_ap=out_ap: e.tensor_tensor(
                    out_ap, sb["vscr"][:, :], sb["btile"][:, b.BT["betav"]:b.BT["betav"] + D], Alu.add), inc="vec")
                evac_done[mg] = p.val("vec")
            ln_done[mg] = p.val("scl")
            mg += 1
        ten_after_w[wi] = p.val("ten")
        if wi == 0:
            W("gpsimd", "ten", ten_after_w[0])
            load_w(2, "Wv")
    W("vector", "scl")
    W("tensor", "scl")
    b.sfree("vscr", "Ws", "x_fm", "xt")


def dbg_dump_tokenmajor(b, name, buf, width=D, dt_rows=128):
    p, sb = b.p, b.sb
    E, W = p.emit, p.wait
    W("sync", "scl")
    W("sync", "vec")
    for m in range(NT):
        E("sync", lambda e, m=m: e.dma_start(out=b.DBG[name][m * 128:(m + 1) * 128, :],
                                             in_=sb[buf][:, m * width:(m + 1) * width]), inc="odma", inc_n=16)


def finish(b):
    blk = b.nc.Block()
    block = blk.__enter__()
    b.p.finalize(block)
    blk.__exit__(None, None, None)
    return b.nc


def stage_a2(b):
    p, sb, ps = b.p, b.sb, b.ps
    E, W = p.emit, p.wait
    b.sbuf("Kn_fm", [128, NC * L], BF16)
    b.sbuf("Qn_fm", [128, NC * L], BF16)
    b.QP = {"sumQ": 0, "qp": 8, "qpp": 16, "qpbk": 24, "tk0": 32, "scr": 40, "ks0": 48, "scr2": 56}
    for src, dst in (("KnT", "Kn_fm"), ("QnT", "Qn_fm")):
        for m in range(NT):
            for k in range(NC):
                pk = ps["pG"] if k % 2 == 0 else ps["pH"]
                W("tensor", "vec")
                E("tensor", lambda e, m=m, k=k, pk=pk, src=src: e.transpose(
                    pk[:, 0:64].bitcast(BF16), sb[src][:, m * D + k * 128:m * D + (k + 1) * 128],
                    sb["identb"][:, :]), inc="ten")
                W("vector", "ten")
                E("vector", lambda e, m=m, k=k, pk=pk, dst=dst: e.tensor_copy(
                    sb[dst][:, k * L + m * 128:k * L + (m + 1) * 128], pk[:, 0:64].bitcast(BF16)), inc="vec")
    for k in range(NC):
        for m in range(NT):
            E("tensor", lambda e, k=k, m=m: e.matmul(
                ps["pF"][:, k:k + 1], sb["QnT"][:, m * D + k * 128:m * D + (k + 1) * 128],
                sb["onesc"][:, 0:1], start=(m == 0), stop=(m == NT - 1)),
              inc="ten" if m == NT - 1 else None)
    W("vector", "ten")
    E("vector", lambda e: e.tensor_copy(sb["qpbuf"][:, 0:8], ps["pF"][:, 0:8]), inc="vec")
    b.sfree("KnT", "QnT", "btile")


def stage_a3(b):
    p, sb, g, ps, QP, VT = b.p, b.sb, b.g, b.ps, b.QP, b.VT
    E, W = p.emit, p.wait

    def SS(eng="vector", sem="vec"):
        W(eng, sem)

    b.sbuf("Wfbuf", [128, NC * D], BF16, side="right")
    b.sbuf("rows2", [1, 2 * D], F32, side="right")
    b.sbuf("lg", [16, 2052], F32, side="right")
    b.sbuf("qpbd", [128, 128], BF16, side="right")
    b.sbuf("qpbkbd", [128, 128], BF16, side="right")
    b.sbuf("score_t", [128, 256], F32, side="right")
    b.sbuf("KVd", [128, 8 * 130], BF16, side="right")
    b.sbuf("KsumF", [128, 8], F32, side="right")
    b.sbuf("s0f", [128, 8], F32, side="right")
    b.sbuf("vtb", [128, 16], BF16, side="right")
    b.sbuf("onesrb", [1, 128], BF16, side="right")
    b.sbuf("bfb", [1, D], BF16, side="right")
    b.sbuf("E1", [16, 128], BF16, side="right")
    b.sbuf("ch1", [16, 8], F32, side="right")
    b.sbuf("s0rhs", [16, 8], BF16, side="right")
    b.sbuf("smx", [16, 4], F32, side="right")
    b.sbuf("io1", [16, 128], I32, side="right")
    b.sbuf("io2", [16, 128], I32, side="right")
    b.sbuf("ioh", [16, 24], I32, side="right")
    b.sbuf("ioc", [16, 8], I32, side="right")
    b.sbuf("dall", [128, 16], F32, side="right")
    b.sbuf("phit", [128, D], BF16, side="right")
    b.sbuf("pKw", [128, NT * D], BF16, side="right")
    b.sbuf("kvscr", [128, 8 * 129], F32, side="right")
    b.sbuf("pqt", [128, 2 * D], BF16, side="right")

    W("vector", "vec")
    E("vector", lambda e: e.tensor_tensor(sb["qpbuf"][:, QP["scr"]:QP["scr"] + 8],
                                          sb["qpbuf"][:, 0:8], sb["vt"][:, VT["gq"]:VT["gq"] + 8], Alu.mult))
    E("vector", lambda e: e.tensor_scalar(sb["qpbuf"][:, QP["scr2"]:QP["scr2"] + 8],
                                          sb["vt"][:, VT["betaq"]:VT["betaq"] + 8], 0.125, None, op0=Alu.mult), inc="vec")
    SS()
    E("vector", lambda e: e.tensor_scalar(sb["qpbuf"][:, QP["scr"]:QP["scr"] + 8],
                                          sb["qpbuf"][:, QP["scr"]:QP["scr"] + 8], 1.0 / 16384.0, None, op0=Alu.mult), inc="vec")
    SS()
    E("vector", lambda e: e.tensor_tensor(sb["qpbuf"][:, QP["qp"]:QP["qp"] + 8],
                                          sb["qpbuf"][:, QP["scr"]:QP["scr"] + 8],
                                          sb["qpbuf"][:, QP["scr2"]:QP["scr2"] + 8], Alu.add), inc="vec")
    SS()
    E("vector", lambda e: e.tensor_tensor(sb["qpbuf"][:, QP["qpp"]:QP["qpp"] + 8],
                                          sb["qpbuf"][:, QP["qp"]:QP["qp"] + 8], sb["vt"][:, VT["gk"]:VT["gk"] + 8], Alu.mult))
    E("vector", lambda e: e.tensor_tensor(sb["qpbuf"][:, QP["qpbk"]:QP["qpbk"] + 8],
                                          sb["qpbuf"][:, QP["qp"]:QP["qp"] + 8], sb["vt"][:, VT["betak"]:VT["betak"] + 8], Alu.mult))
    E("vector", lambda e: e.memset(sb["qpbd"][:, :], 0.0))
    E("vector", lambda e: e.memset(sb["qpbkbd"][:, :], 0.0))
    E("vector", lambda e: e.memset(sb["onesrb"][0:1, :], 1.0), inc="vec")
    SS()
    for c in range(NC):
        E("vector", lambda e, c=c: e.tensor_copy(sb["qpbd"][0:64, c * 16 + 2 * c:c * 16 + 2 * c + 1],
                                                 sb["qpbuf"][0:64, QP["qpp"] + c:QP["qpp"] + c + 1]))
        E("vector", lambda e, c=c: e.tensor_copy(sb["qpbd"][64:128, c * 16 + 2 * c + 1:c * 16 + 2 * c + 2],
                                                 sb["qpbuf"][64:128, QP["qpp"] + c:QP["qpp"] + c + 1]))
        E("vector", lambda e, c=c: e.tensor_copy(sb["qpbkbd"][0:64, c * 16 + 2 * c:c * 16 + 2 * c + 1],
                                                 sb["qpbuf"][0:64, QP["qpbk"] + c:QP["qpbk"] + c + 1]))
        E("vector", lambda e, c=c: e.tensor_copy(sb["qpbkbd"][64:128, c * 16 + 2 * c + 1:c * 16 + 2 * c + 2],
                                                 sb["qpbuf"][64:128, QP["qpbk"] + c:QP["qpbk"] + c + 1]), inc="vec")

    qpbd_ready = p.val("vec")
    LB = [ps["pE"], ps["pF"], ps["pG"], ps["pH"]]
    W("tensor", "vec", qpbd_ready)
    for nt in range(4):
        for c in range(NC):
            E("tensor", lambda e, nt=nt, c=c: e.matmul(
                LB[nt][0:16, :], sb["qpbd"][:, c * 16:(c + 1) * 16],
                sb["Kn_fm"][:, c * L + nt * 512:c * L + (nt + 1) * 512],
                start=(c == 0), stop=(c == NC - 1)), inc="ten" if c == NC - 1 else None)
    for c in range(NC):
        E("tensor", lambda e, c=c: e.matmul(
            ps["pA"][0:16, 0:1], sb["qpbkbd"][:, c * 16:(c + 1) * 16], sb["onesc"][:, 0:1],
            start=(c == 0), stop=(c == NC - 1)), inc="ten" if c == NC - 1 else None)
    W("vector", "ten")
    for nt in range(4):
        E("vector", lambda e, nt=nt: e.tensor_copy(sb["lg"][0:16, 1 + nt * 512:1 + (nt + 1) * 512], LB[nt][0:16, :]))
    E("vector", lambda e: e.tensor_scalar(sb["lg"][0:16, 0:1], ps["pA"][0:16, 0:1], -1.0, None, op0=Alu.mult), inc="vec")
    SS()
    E("vector", lambda e: e.tensor_reduce(sb["smx"][0:16, 0:1], sb["lg"][0:16, 0:2049], axis=AX.X, op=Alu.max), inc="vec")
    SS()
    E("vector", lambda e: e.tensor_scalar(sb["smx"][0:16, 1:2], sb["smx"][0:16, 0:1], -1.0, None, op0=Alu.mult), inc="vec")
    W("scalar", "vec")
    E("scalar", lambda e: e.activation(out=sb["lg"][0:16, 0:2049], in_=sb["lg"][0:16, 0:2049],
                                       func=AF.Exp, bias=sb["smx"][0:16, 1:2]), inc="scl")
    W("vector", "scl")
    E("vector", lambda e: e.tensor_reduce(sb["smx"][0:16, 2:3], sb["lg"][0:16, 0:2049], axis=AX.X, op=Alu.add), inc="vec")
    SS()
    E("vector", lambda e: e.reciprocal(sb["smx"][0:16, 3:4], sb["smx"][0:16, 2:3]), inc="vec")
    W("scalar", "vec")
    E("scalar", lambda e: e.activation(out=sb["lg"][0:16, 0:2049], in_=sb["lg"][0:16, 0:2049],
                                       func=AF.Identity, scale=sb["smx"][0:16, 3:4]), inc="scl")
    score_done = p.val("scl")
    W("tensor", "scl", score_done)
    for c in range(NT):
        pk = ps["pG"] if c % 2 == 0 else ps["pH"]
        W("tensor", "vec")
        E("tensor", lambda e, c=c, pk=pk: e.transpose(pk[:, 0:16], sb["lg"][0:16, 1 + c * 128:1 + (c + 1) * 128],
                                                      sb["ident"][0:16, 0:16]), inc="ten")
        W("vector", "ten")
        E("vector", lambda e, c=c, pk=pk: e.tensor_copy(sb["score_t"][:, c * 16:(c + 1) * 16], pk[:, 0:16]), inc="vec")

    E("scalar", lambda e: e.activation(out=sb["qpbuf"][:, QP["tk0"]:QP["tk0"] + 8],
                                       in_=sb["vt"][:, VT["bfk"]:VT["bfk"] + 8], func=AF.Tanh), inc="scl")
    E("gpsimd", lambda e: e.iota(sb["io1"][0:16, 0:128], pattern=[[0, 128]], base=0, channel_multiplier=1))
    E("gpsimd", lambda e: e.iota(sb["io2"][0:16, 0:128], pattern=[[1, 2], [0, 64]], base=0, channel_multiplier=0))
    E("gpsimd", lambda e: e.iota(sb["ioh"][0:16, 0:8], pattern=[[0, 8]], base=0, channel_multiplier=1))
    E("gpsimd", lambda e: e.iota(sb["ioc"][0:16, 0:8], pattern=[[1, 8]], base=0, channel_multiplier=0), inc="qdma")
    W("vector", "qdma")
    E("vector", lambda e: e.tensor_scalar(sb["io1"][0:16, :], sb["io1"][0:16, :], 1, None, op0=Alu.bitwise_and),
      inc="vec")
    SS()
    E("vector", lambda e: e.tensor_tensor(sb["E1"][0:16, :], sb["io1"][0:16, :], sb["io2"][0:16, :], Alu.is_equal))
    E("vector", lambda e: e.tensor_scalar(sb["ioh"][0:16, 8:16], sb["ioh"][0:16, 0:8], 1, None,
                                          op0=Alu.bitwise_and), inc="vec")
    SS()
    E("vector", lambda e: e.tensor_tensor(sb["ioh"][0:16, 16:24], sb["ioh"][0:16, 0:8], sb["ioh"][0:16, 8:16],
                                          Alu.subtract), inc="vec")
    SS()
    E("vector", lambda e: e.tensor_scalar(sb["ch1"][0:16, :], sb["ioh"][0:16, 16:24], 0.5, None, op0=Alu.mult),
      inc="vec")
    SS()
    E("vector", lambda e: e.tensor_tensor(sb["ch1"][0:16, :], sb["ch1"][0:16, :], sb["ioc"][0:16, 0:8],
                                          Alu.is_equal), inc="vec")
    SS()
    W("vector", "scl", score_done)
    E("vector", lambda e: e.tensor_tensor(sb["s0rhs"][0:16, :], sb["ch1"][0:16, :],
                                          sb["lg"][0:16, 0:1].broadcast_to([16, 8]), Alu.mult), inc="vec")
    W("tensor", "vec")
    E("tensor", lambda e: e.matmul(ps["pA"][:, 0:8], sb["E1"][0:16, :], sb["s0rhs"][0:16, :],
                                   start=True, stop=True), inc="ten")
    W("vector", "ten")
    E("vector", lambda e: e.tensor_copy(sb["s0f"][:, 0:8], ps["pA"][:, 0:8]), inc="vec")

    def wf_load(wname, bname, vcol, gcol):
        W("gpsimd", "ten")
        for k in range(NC):
            E("gpsimd", lambda e, k=k, wname=wname: e.dma_start(
                out=sb["Wfbuf"][:, k * D:(k + 1) * D], in_=g[wname][k * 128:(k + 1) * 128, :]),
              inc="gdma", inc_n=16)
        E("sync", lambda e, bname=bname: e.dma_start(out=sb["rows2"][0:1, D:2 * D], in_=g[bname][0:1, :]),
          inc="sdma", inc_n=16)
        W("tensor", "gdma")
        W("tensor", "vec")
        for n in range(2):
            for c in range(NC):
                E("tensor", lambda e, n=n, c=c, vcol=vcol: e.matmul(
                    [ps["pA"], ps["pB"]][n][0:1, :], sb["vtb"][:, vcol + c:vcol + c + 1],
                    sb["Wfbuf"][:, c * D + n * 512:c * D + (n + 1) * 512],
                    start=(c == 0), stop=(c == NC - 1)), inc="ten" if c == NC - 1 else None)
        W("vector", "ten")
        W("vector", "sdma")
        for n in range(2):
            E("vector", lambda e, n=n: e.tensor_tensor(
                sb["rows2"][0:1, n * 512:(n + 1) * 512], [ps["pA"], ps["pB"]][n][0:1, :],
                sb["rows2"][0:1, D + n * 512:D + (n + 1) * 512], Alu.add), inc="vec")
        SS()
        E("vector", lambda e: e.tensor_copy(sb["bfb"][0:1, 0:D], sb["rows2"][0:1, 0:D]), inc="vec")
        bfb_done = p.val("vec")
        W("scalar", "ten")
        for k in range(NC):
            E("scalar", lambda e, k=k, gcol=gcol: e.activation(
                out=sb["Wfbuf"][:, k * D:(k + 1) * D], in_=sb["Wfbuf"][:, k * D:(k + 1) * D],
                func=AF.Identity, scale=sb["vt"][:, gcol + k:gcol + k + 1]), inc="scl")
        return bfb_done, p.val("scl")

    E("vector", lambda e: e.tensor_copy(sb["vtb"][:, 0:8], sb["vt"][:, VT["betaq"]:VT["betaq"] + 8]))
    E("vector", lambda e: e.tensor_copy(sb["vtb"][:, 8:16], sb["vt"][:, VT["betak"]:VT["betak"] + 8]), inc="vec")

    bfb_done, wf_scaled = wf_load("Wfk", "bfk", 8, VT["gk"])
    PP = [(ps["pA"], ps["pB"]), (ps["pC"], ps["pD"])]
    tanh_done = {}
    for m in range(NT):
        pp = PP[m % 2]
        W("tensor", "scl", wf_scaled)
        W("tensor", "vec", bfb_done)
        if m >= 2:
            W("tensor", "scl", tanh_done[m - 2])
        for k in range(NC):
            for n in range(2):
                E("tensor", lambda e, m=m, k=k, n=n, pp=pp: e.matmul(
                    pp[n][:, :], sb["Kn_fm"][:, k * L + m * 128:k * L + (m + 1) * 128],
                    sb["Wfbuf"][:, k * D + n * 512:k * D + (n + 1) * 512],
                    start=(k == 0), stop=False))
        for n in range(2):
            E("tensor", lambda e, n=n, pp=pp: e.matmul(
                pp[n][:, :], sb["onesrb"][0:1, 0:128], sb["bfb"][0:1, n * 512:(n + 1) * 512],
                start=False, stop=True), inc="ten")
        W("scalar", "ten")
        W("scalar", "vec")
        for n in range(2):
            E("scalar", lambda e, n=n, pp=pp: e.activation(
                out=sb["prescr"][:, n * 512:(n + 1) * 512], in_=pp[n][:, :], func=AF.Tanh), inc="scl")
        tanh_done[m] = p.val("scl")
        W("vector", "scl")
        for h in range(16):
            E("vector", lambda e, m=m, h=h: e.scalar_tensor_tensor(
                out=sb["pKw"][:, m * D + h * 64:m * D + (h + 1) * 64],
                in0=sb["prescr"][:, h * 64:(h + 1) * 64], scalar=1.0,
                in1=sb["score_t"][:, m * 16 + h:m * 16 + h + 1].broadcast_to([128, 64]),
                op0=Alu.add, op1=Alu.mult), inc="vec" if h == 15 else None)
    pkw_all = p.val("vec")

    W("tensor", "vec", pkw_all)
    kvdone = {}
    for hp in range(8):
        bank = ps["pE"] if hp % 2 == 0 else ps["pF"]
        if hp >= 2:
            W("tensor", "vec", kvdone[hp - 2])
        for m in range(NT):
            E("tensor", lambda e, m=m, hp=hp, bank=bank: e.matmul(
                bank[:, 0:129], sb["pKw"][:, m * D + hp * 128:m * D + (hp + 1) * 128],
                sb["VfT"][:, m * 1032 + hp * 129:m * 1032 + (hp + 1) * 129],
                start=(m == 0), stop=(m == NT - 1)), inc="ten" if m == NT - 1 else None)
        W("vector", "ten")
        E("vector", lambda e, hp=hp, bank=bank: e.tensor_copy(sb["kvscr"][:, hp * 129:(hp + 1) * 129],
                                                              bank[:, 0:129]),
          inc="vec")
        kvdone[hp] = p.val("vec")
    W("vector", "vec")
    E("vector", lambda e: e.tensor_tensor(sb["qpbuf"][:, QP["ks0"]:QP["ks0"] + 8],
                                          sb["qpbuf"][:, QP["tk0"]:QP["tk0"] + 8], sb["s0f"][:, 0:8], Alu.mult),
      inc="vec")
    SS()
    E("vector", lambda e: e.tensor_tensor(
        sb["KsumF"][:, 0:8],
        sb["kvscr"][:, :].rearrange("p (h c) -> p h c", c=129)[:, :, 128],
        sb["qpbuf"][:, QP["ks0"]:QP["ks0"] + 8], Alu.add), inc="vec")
    SS()
    E("vector", lambda e: e.tensor_tensor(sb["KsumF"][:, 0:8], sb["KsumF"][:, 0:8], sb["s0f"][:, 0:8], Alu.add),
      inc="vec")
    E("vector", lambda e: e.memset(sb["KVd"][:, :], 0.0), inc="vec")
    SS()
    for hp in range(8):
        E("vector", lambda e, hp=hp: e.tensor_copy(
            sb["KVd"][0:64, hp * 130:hp * 130 + 64], sb["kvscr"][0:64, hp * 129:hp * 129 + 64]))
        E("vector", lambda e, hp=hp: e.tensor_copy(
            sb["KVd"][64:128, hp * 130 + 64:hp * 130 + 128], sb["kvscr"][64:128, hp * 129 + 64:hp * 129 + 128]))
        E("vector", lambda e, hp=hp: e.tensor_copy(
            sb["KVd"][0:64, hp * 130 + 128:hp * 130 + 129], sb["KsumF"][0:64, hp:hp + 1]))
        E("vector", lambda e, hp=hp: e.tensor_copy(
            sb["KVd"][64:128, hp * 130 + 129:hp * 130 + 130], sb["KsumF"][64:128, hp:hp + 1]),
          inc="vec" if hp == 7 else None)
        kvd_ready = p.val("vec")
    if A3_STOP == "kvd":
        return

    bfb_done, wf_scaled = wf_load("Wfq", "bfq", 0, VT["gq"])
    mix_dma_done = {}
    evac_done2 = {}
    tanh_done = {}
    num_used = {}
    for m in range(NT):
        pp = PP[m % 2]
        W("tensor", "scl", wf_scaled)
        W("tensor", "vec", max(bfb_done, kvd_ready))
        if m >= 1:
            W("tensor", "vec", evac_done2[m - 1])
        if m >= 2:
            W("tensor", "scl", tanh_done[m - 2])
        for k in range(NC):
            for n in range(2):
                E("tensor", lambda e, m=m, k=k, n=n, pp=pp: e.matmul(
                    pp[n][:, :], sb["Qn_fm"][:, k * L + m * 128:k * L + (m + 1) * 128],
                    sb["Wfbuf"][:, k * D + n * 512:k * D + (n + 1) * 512],
                    start=(k == 0), stop=False))
        for n in range(2):
            E("tensor", lambda e, n=n, pp=pp: e.matmul(
                pp[n][:, :], sb["onesrb"][0:1, 0:128], sb["bfb"][0:1, n * 512:(n + 1) * 512],
                start=False, stop=True), inc="ten")
        W("scalar", "ten")
        for n in range(2):
            E("scalar", lambda e, n=n, pp=pp: e.activation(
                out=sb["phit"][:, n * 512:(n + 1) * 512], in_=pp[n][:, :], func=AF.Tanh), inc="scl")
        tanh_done[m] = p.val("scl")
        slot = m % 2
        W("tensor", "scl", tanh_done[m])
        if m >= 2:
            W("tensor", "ten", num_used[m - 2])
        for k in range(NC):
            pk = ps["pG"] if k % 2 == 0 else ps["pH"]
            W("tensor", "vec")
            E("tensor", lambda e, k=k, pk=pk: e.transpose(
                pk[:, 0:64].bitcast(BF16), sb["phit"][:, k * 128:(k + 1) * 128], sb["identb"][:, :]), inc="ten")
            W("vector", "ten")
            E("vector", lambda e, k=k, pk=pk, slot=slot: e.tensor_scalar(
                sb["pqt"][:, slot * D + k * 128:slot * D + (k + 1) * 128], pk[:, 0:64].bitcast(BF16),
                1.0, None, op0=Alu.add), inc="vec")
        pqt_done = p.val("vec")
        W("tensor", "vec", pqt_done)
        NSLOT = [(ps["pC"], 0), (ps["pC"], 130), (ps["pC"], 260), (ps["pD"], 0), (ps["pD"], 130),
                 (ps["pD"], 260), (ps["pE"], 0), (ps["pE"], 130)]
        for hp in range(8):
            bank, off = NSLOT[hp]
            E("tensor", lambda e, hp=hp, bank=bank, off=off, slot=slot: e.matmul(
                bank[:, off:off + 130], sb["pqt"][:, slot * D + hp * 128:slot * D + (hp + 1) * 128],
                sb["KVd"][:, hp * 130:(hp + 1) * 130], start=True, stop=True),
              inc="ten" if hp == 7 else None)
        num_used[m] = p.val("ten")
        W("vector", "ten")
        if m >= 2:
            W("vector", "odma", mix_dma_done[m - 2])
        mix = sb["prescr"][:, slot * D:(slot + 1) * D]
        for hp in range(8):
            bank, off = NSLOT[hp]
            E("vector", lambda e, hp=hp, bank=bank, off=off: e.tensor_scalar(
                sb["dall"][:, 2 * hp:2 * hp + 2], bank[:, off + 128:off + 130], 1e-6, None, op0=Alu.add),
              inc="vec" if hp == 7 else None)
        SS()
        E("vector", lambda e: e.reciprocal(sb["dall"][:, :], sb["dall"][:, :]), inc="vec")
        SS()
        for hp in range(8):
            bank, off = NSLOT[hp]
            for j in range(2):
                E("vector", lambda e, hp=hp, j=j, bank=bank, off=off, mix=mix: e.tensor_tensor(
                    mix[:, hp * 128 + j * 64:hp * 128 + (j + 1) * 64], bank[:, off + j * 64:off + (j + 1) * 64],
                    sb["dall"][:, 2 * hp + j:2 * hp + j + 1].broadcast_to([128, 64]), Alu.mult),
                  inc="vec" if (hp == 7 and j == 1) else None)
        evac_done2[m] = p.val("vec")
        W("scalar", "vec")
        E("scalar", lambda e, m=m, mix=mix: e.dma_start(out=g["mixer_dram"][m * 128:(m + 1) * 128, :], in_=mix),
          inc="odma", inc_n=16)
        mix_dma_done[m] = p.val("odma")
    W("vector", "odma")
    b.sfree("pqt", "kvscr", "pKw", "phit", "dall", "ioc", "ioh", "io2", "io1", "smx", "s0rhs", "ch1", "E1",
            "bfb", "onesrb", "vtb", "s0f", "KsumF", "KVd", "score_t", "qpbkbd", "qpbd", "lg", "rows2", "Wfbuf")
    b.sfree("Qn_fm", "Kn_fm", "VfT")




def stage_a4(b, dev_y=True):
    p, sb, g, ps = b.p, b.sb, b.g, b.ps
    E, W = p.emit, p.wait

    def SS(eng="vector", sem="vec"):
        W(eng, sem)

    b.sbuf("W1s", [128, NC * MLPD], BF16)
    b.sbuf("W2s", [128, 32 * D], BF16)
    b.sbuf("gabat", [128, 2 * D], F32, side="right")
    b.sbuf("aT", [128, 2 * D], F32, side="right")
    b.sbuf("a_fm", [128, NC * 256], BF16, side="right")
    b.sbuf("xbuf", [128, 2 * D], F32, side="right")
    b.sbuf("hscr", [128, D], F32, side="right")
    b.sbuf("yscr", [128, D], F32, side="right")
    b.sbuf("ybf", [128, D], BF16, side="right")
    b.sbuf("yfm", [128, NC * 128], F32, side="right")
    b.sbuf("sTt", [128, 8], F32, side="right")
    b.sbuf("wgs", [128, NC * 8], F32, side="right")
    b.sbuf("b2b", [1, D], BF16, side="right")
    b.sbuf("gslice", [128, 256], BF16, side="right")
    b.sbuf("smg", [128, 4], F32, side="right")
    b.sbuf("rowstage3", [1, D], F32, side="right")
    b.sbuf("onesrb4", [1, 128], BF16, side="right")

    p.fence(("gpsimd", "sync"))
    for k in range(NC):
        E("gpsimd", lambda e, k=k: e.dma_start(out=sb["W1s"][:, k * MLPD:(k + 1) * MLPD],
                                               in_=g["W1"][k * 128:(k + 1) * 128, :]), inc="gdma", inc_n=16)
    for k in range(32):
        E("gpsimd", lambda e, k=k: e.dma_start(out=sb["W2s"][:, k * D:(k + 1) * D],
                                               in_=g["W2"][k * 128:(k + 1) * 128, :]), inc="gdma", inc_n=16)
    for k in range(NC):
        E("sync", lambda e, k=k: e.dma_start(out=sb["wgs"][:, k * 8:(k + 1) * 8],
                                             in_=g["wg"][k * 128:(k + 1) * 128, :]), inc="sdma", inc_n=16)
    W("vector", "vec")
    E("vector", lambda e: e.memset(sb["onesrb4"][0:1, :], 1.0))
    E("vector", lambda e: e.tensor_copy(sb["b2b"][0:1, :], sb["rows"][0:1, 0:D]), inc="vec")
    for i, nm in enumerate(("ga", "ba")):
        W("sync", "ten")
        E("sync", lambda e, nm=nm: e.dma_start(out=sb["rowstage3"][0:1, :], in_=g[nm][0:1, :]), inc="sdma", inc_n=16)
        W("tensor", "sdma")
        for n in range(2):
            E("tensor", lambda e, n=n: e.matmul(ps["pG"][:, :], sb["onesr"][0:1, 0:128],
                                                sb["rowstage3"][0:1, n * 512:(n + 1) * 512],
                                                start=True, stop=True), inc="ten")
            W("vector", "ten")
            E("vector", lambda e, i=i, n=n: e.tensor_copy(
                sb["gabat"][:, i * D + n * 512:i * D + (n + 1) * 512], ps["pG"][:, :]), inc="vec")
            W("tensor", "vec")
    W("tensor", "gdma")
    W("sync", "odma")

    NCH = L // 256
    gelu_done = {}
    ydma_done = {}
    mixld_done = {}
    afm_done = {}
    evh_done = {}
    for tc in range(NCH):
        for t2 in range(2):
            ti = tc * 2 + t2
            slot = t2
            pre = sb["prescr"][:, slot * D:(slot + 1) * D]
            W("sync", "vec")
            W("sync", "scl")
            E("sync", lambda e, ti=ti, pre=pre: e.dma_start(out=pre, in_=g["mixer_dram"][ti * 128:(ti + 1) * 128, :]),
              inc="sdma", inc_n=16)
            E("sync", lambda e, ti=ti, slot=slot: e.dma_start(out=sb["xbuf"][:, slot * D:(slot + 1) * D],
                                                              in_=g["x"][ti * 128:(ti + 1) * 128, :]),
              inc="sdma", inc_n=16)
            W("vector", "sdma")
            ln_tile(b, pre, sb["hscr"][:, :], slot)
            W("vector", "scl")
            E("vector", lambda e: e.tensor_tensor(sb["hscr"][:, :], sb["hscr"][:, :],
                                                  sb["gabat"][:, 0:D], Alu.mult), inc="vec")
            SS()
            E("vector", lambda e, slot=slot: e.tensor_tensor(sb["aT"][:, slot * D:(slot + 1) * D],
                                                             sb["hscr"][:, :], sb["gabat"][:, D:2 * D],
                                                             Alu.add), inc="vec")
            at_done = p.val("vec")
            W("tensor", "vec", at_done)
            for k in range(NC):
                pk = ps["pG"] if k % 2 == 0 else ps["pH"]
                W("tensor", "vec")
                E("tensor", lambda e, k=k, pk=pk, slot=slot: e.transpose(
                    pk[:, 0:128], sb["aT"][:, slot * D + k * 128:slot * D + (k + 1) * 128],
                    sb["ident"][:, :]), inc="ten")
                W("vector", "ten")
                E("vector", lambda e, k=k, pk=pk, t2=t2: e.tensor_copy(
                    sb["a_fm"][:, k * 256 + t2 * 128:k * 256 + (t2 + 1) * 128], pk[:, 0:128]), inc="vec")
            W("vector", "ten")
            E("vector", lambda e, slot=slot: e.scalar_tensor_tensor(
                out=sb["aT"][:, slot * D:(slot + 1) * D], in0=sb["xbuf"][:, slot * D:(slot + 1) * D],
                scalar=2.0, in1=sb["aT"][:, slot * D:(slot + 1) * D], op0=Alu.mult, op1=Alu.add), inc="vec")
        afm_done[tc] = p.val("vec")

        W("tensor", "vec", afm_done[tc])
        W2B = [ps["pC"], ps["pD"], ps["pE"], ps["pF"]]
        for hc in range(32):
            pw = ps["pA"] if hc % 2 == 0 else ps["pB"]
            gi = tc * 32 + hc
            if gi >= 2:
                W("tensor", "scl", gelu_done[gi - 2])
            for k in range(NC):
                E("tensor", lambda e, hc=hc, k=k, pw=pw: e.matmul(
                    pw[:, 0:256], sb["W1s"][:, k * MLPD + hc * 128:k * MLPD + (hc + 1) * 128],
                    sb["a_fm"][:, k * 256:(k + 1) * 256], start=(k == 0), stop=(k == NC - 1)),
                  inc="ten" if k == NC - 1 else None)
            W("scalar", "ten")
            E("scalar", lambda e, hc=hc, pw=pw: e.activation(
                out=sb["gslice"][:, :], in_=pw[:, 0:256], func=AF.Gelu,
                bias=sb["b1t"][:, hc:hc + 1]), inc="scl")
            gelu_done[gi] = p.val("scl")
            W("tensor", "scl", gelu_done[gi])
            for ms in range(2):
                for n in range(2):
                    E("tensor", lambda e, hc=hc, ms=ms, n=n: e.matmul(
                        W2B[ms * 2 + n][:, :], sb["gslice"][:, ms * 128:(ms + 1) * 128],
                        sb["W2s"][:, hc * D + n * 512:hc * D + (n + 1) * 512],
                        start=(hc == 0), stop=False))
        for ms in range(2):
            for n in range(2):
                E("tensor", lambda e, ms=ms, n=n: e.matmul(
                    W2B[ms * 2 + n][:, :], sb["onesrb4"][0:1, 0:128], sb["b2b"][0:1, n * 512:(n + 1) * 512],
                    start=False, stop=True), inc="ten" if (ms == 1 and n == 1) else None)
        W("vector", "ten")
        for t2 in range(2):
            ti = tc * 2 + t2
            slot = t2
            W("vector", "scl")
            for n in range(2):
                E("vector", lambda e, t2=t2, n=n, slot=slot: e.tensor_tensor(
                    sb["hscr"][:, n * 512:(n + 1) * 512], W2B[t2 * 2 + n][:, :],
                    sb["aT"][:, slot * D + n * 512:slot * D + (n + 1) * 512], Alu.add),
                  inc="vec" if n == 1 else None)
            SS()
            W("scalar", "ten")
            if ti >= 1:
                W("scalar", "odma", ydma_done[ti - 1])
                W("vector", "odma", ydma_done[ti - 1])
            ln_tile(b, sb["hscr"][:, :], sb["yscr"][:, :], slot)
            W("vector", "scl")
            W("scalar", "scl")
            if dev_y:
                E("scalar", lambda e, ti=ti: e.dma_start(out=g["y_out"][ti * 128:(ti + 1) * 128, :],
                                                         in_=sb["yscr"][:, :]), inc="odma", inc_n=16)
            E("scalar", lambda e, ti=ti: e.dma_start(out=g["y_dram"][ti * 128:(ti + 1) * 128, :],
                                                     in_=sb["yscr"][:, :]), inc="odma", inc_n=16)
            E("vector", lambda e: e.tensor_copy(sb["ybf"][:, :], sb["yscr"][:, :]), inc="vec")
            W("scalar", "vec")
            E("scalar", lambda e, ti=ti: e.dma_start(out=g["agy_dram"][ti * 128:(ti + 1) * 128, :],
                                                     in_=sb["ybf"][:, :]), inc="odma", inc_n=16)
            for k in range(NC):
                pk = ps["pG"] if k % 2 == 0 else ps["pH"]
                W("tensor", "vec")
                W("tensor", "scl")
                E("tensor", lambda e, k=k, pk=pk: e.transpose(
                    pk[:, 0:128], sb["yscr"][:, k * 128:(k + 1) * 128], sb["ident"][:, :]), inc="ten")
                W("vector", "ten")
                E("vector", lambda e, k=k, pk=pk: e.tensor_copy(
                    sb["yfm"][:, k * 128:(k + 1) * 128], pk[:, 0:128]), inc="vec")
            yfm_done = p.val("vec")
            W("tensor", "vec", yfm_done)
            for k in range(NC):
                E("tensor", lambda e, k=k: e.matmul(
                    ps["pG"][:, 0:8], sb["yfm"][:, k * 128:(k + 1) * 128], sb["wgs"][:, k * 8:(k + 1) * 8],
                    start=(k == 0), stop=(k == NC - 1)), inc="ten" if k == NC - 1 else None)
            W("vector", "ten")
            E("vector", lambda e: e.tensor_reduce(sb["smg"][:, 0:1], ps["pG"][:, 0:8], axis=AX.X, op=Alu.max),
              inc="vec")
            SS()
            E("vector", lambda e: e.tensor_scalar(sb["smg"][:, 1:2], sb["smg"][:, 0:1], -1.0, None, op0=Alu.mult),
              inc="vec")
            W("scalar", "vec")
            E("scalar", lambda e: e.activation(out=sb["sTt"][:, :], in_=ps["pG"][:, 0:8], func=AF.Exp,
                                               bias=sb["smg"][:, 1:2]), inc="scl")
            W("vector", "scl")
            E("vector", lambda e: e.tensor_reduce(sb["smg"][:, 2:3], sb["sTt"][:, :], axis=AX.X, op=Alu.add),
              inc="vec")
            SS()
            E("vector", lambda e: e.reciprocal(sb["smg"][:, 3:4], sb["smg"][:, 2:3]), inc="vec")
            W("scalar", "vec")
            E("scalar", lambda e: e.activation(out=sb["sTt"][:, :], in_=sb["sTt"][:, :], func=AF.Identity,
                                               scale=sb["smg"][:, 3:4]), inc="scl")
            W("scalar", "scl")
            if dev_y:
                E("scalar", lambda e, ti=ti: e.dma_start(out=g["s_out"][ti * 128:(ti + 1) * 128, :],
                                                         in_=sb["sTt"][:, :]), inc="odma", inc_n=16)
            W("tensor", "scl")
            E("tensor", lambda e: e.transpose(ps["pH"][0:8, 0:128], sb["sTt"][:, :], sb["ident"][:, :]), inc="ten")
            W("vector", "ten")
            E("vector", lambda e: e.tensor_copy(sb["yfm"][0:8, 0:128], ps["pH"][0:8, 0:128]), inc="vec")
            W("scalar", "vec")
            E("scalar", lambda e, ti=ti: e.dma_start(out=g["sag_dram"][ti, :, :], in_=sb["yfm"][0:8, 0:128]),
              inc="odma", inc_n=16)
            ydma_done[ti] = p.val("odma")
    W("vector", "odma")
    W("scalar", "odma")
    b.sfree("onesrb4", "rowstage3", "smg", "gslice", "b2b", "wgs", "sTt", "yfm", "ybf", "yscr", "hscr",
            "xbuf", "a_fm", "aT", "gabat")
    b.sfree("W2s", "W1s")


def stage_r1(b):
    p, sb, g, ps = b.p, b.sb, b.g, b.ps
    E, W = p.emit, p.wait

    def SS(eng="vector", sem="vec"):
        W(eng, sem)

    b.dram_tmp("S2_dram", [128, NE, 128], shared=True)
    b.dram_tmp("Yfull_dram", [NE * L, D], BF16, shared=True)
    b.sbuf("S2", [128, NE * 128], F32)
    b.sbuf("selres", [8, 8], F32)
    b.sbuf("selmask", [128, NE * 128], F32)
    b.sbuf("sint", [8, 16], I32, side="right")
    b.sbuf("sbyte", [8, 4], F32, side="right")
    b.sbuf("ones4", [4, 128], F32, side="right")
    b.sbuf("cmpb", [128, NE * 128], F32, side="right")
    b.sbuf("red", [128, 32], F32, side="right")
    b.sbuf("tbits", [128, 8], I32, side="right")
    b.sbuf("tbscr", [128, 4 * 8], I32, side="right")
    b.sbuf("onescF", [128, 1], F32, side="right")
    b.sbuf("cntf", [8, 4], F32, side="right")
    b.sbuf("tvalf", [128, 128], F32, side="right")
    b.sbuf("eqm", [128, NE * 128], F32, side="right")
    b.sbuf("iot", [128, 128], I32, side="right")
    b.sbuf("mflt", [128, 8], F32, side="right")
    b.sbuf("xsp", [8, 8], F32, side="right")
    b.p.new_sem("cc")

    W("gpsimd", "odma")
    p.fence(("gpsimd", "sync"))
    E("gpsimd", lambda e: e.collective_compute(
        "AllGather", Alu.bypass, replica_groups=[list(range(8))],
        ins=[g["sag_dram"].opt()], outs=[g["S2_dram"].opt()]), inc="cc")
    E("gpsimd", lambda e: e.collective_compute(
        "AllGather", Alu.bypass, replica_groups=[list(range(8))],
        ins=[g["agy_dram"].opt()], outs=[g["Yfull_dram"].opt()]), inc="cc")
    W("sync", "cc", 1)
    E("sync", lambda e: e.dma_start(out=sb["S2"][:, :], in_=g["S2_dram"][:, :, :]), inc="sdma", inc_n=16)
    W("vector", "sdma")
    E("vector", lambda e: e.memset(sb["onescF"][:, :], 1.0))
    E("vector", lambda e: e.memset(sb["ones4"][:, :], 1.0))
    E("gpsimd", lambda e: e.iota(sb["iot"][:, :], pattern=[[1, 128]], base=0, channel_multiplier=128), inc="qdma")
    W("vector", "qdma")
    E("vector", lambda e: e.tensor_copy(sb["tvalf"][:, :], sb["iot"][:, :]), inc="vec")

    def bcast_exact(src_col):
        for j in range(4):
            E("vector", lambda e, j=j: e.tensor_scalar(
                sb["sint"][0:8, 8 + j:9 + j], sb["sint"][0:8, src_col:src_col + 1], 8 * j, None,
                op0=Alu.logical_shift_right), inc="vec" if j == 3 else None)
        SS()
        for j in range(4):
            E("vector", lambda e, j=j: e.tensor_scalar(
                sb["sint"][0:8, 8 + j:9 + j], sb["sint"][0:8, 8 + j:9 + j], 255, None,
                op0=Alu.bitwise_and), inc="vec" if j == 3 else None)
        SS()
        E("vector", lambda e: e.tensor_copy(sb["sbyte"][0:8, 0:4], sb["sint"][0:8, 8:12]), inc="vec")
        W("tensor", "vec")
        for j in range(4):
            E("tensor", lambda e, j=j: e.transpose(ps["pH"][0:1, 40 + j * 8:40 + (j + 1) * 8],
                                                   sb["sbyte"][0:8, j:j + 1], sb["ident"][0:8, 0:8]),
              inc="ten" if j == 3 else None)
        W("vector", "ten")
        E("vector", lambda e: e.tensor_copy(sb["red"][0:1, 0:32],
                                            ps["pH"][0:1, 40:72]), inc="vec")
        W("tensor", "vec")
        for j in range(4):
            E("tensor", lambda e, j=j: e.matmul(ps["pH"][:, j * 8:(j + 1) * 8], sb["onesr"][0:1, 0:128],
                                                sb["red"][0:1, j * 8:(j + 1) * 8], start=True, stop=True),
              inc="ten" if j == 3 else None)
        W("vector", "ten")
        for j in range(4):
            E("vector", lambda e, j=j: e.tensor_copy(sb["tbscr"][:, j * 8:(j + 1) * 8],
                                                     ps["pH"][:, j * 8:(j + 1) * 8]),
              inc="vec" if j == 3 else None)
        SS()
        for j in range(1, 4):
            E("vector", lambda e, j=j: e.tensor_scalar(
                sb["tbscr"][:, j * 8:(j + 1) * 8], sb["tbscr"][:, j * 8:(j + 1) * 8], 8 * j, None,
                op0=Alu.logical_shift_left), inc="vec" if j == 3 else None)
        SS()
        E("vector", lambda e: e.tensor_tensor(sb["tbits"][:, :], sb["tbscr"][:, 0:8], sb["tbscr"][:, 8:16],
                                              Alu.bitwise_or), inc="vec")
        SS()
        E("vector", lambda e: e.tensor_tensor(sb["tbits"][:, :], sb["tbits"][:, :], sb["tbscr"][:, 16:24],
                                              Alu.bitwise_or), inc="vec")
        SS()
        E("vector", lambda e: e.tensor_tensor(sb["tbits"][:, :], sb["tbits"][:, :], sb["tbscr"][:, 24:32],
                                              Alu.bitwise_or), inc="vec")
        SS()

    def count_gt(out_col):
        for e_ in range(NE):
            E("vector", lambda e, e_=e_: e.tensor_scalar(
                sb["cmpb"][:, e_ * 128:(e_ + 1) * 128], sb["S2"][:, e_ * 128:(e_ + 1) * 128],
                sb["tbits"].bitcast(F32)[:, e_:e_ + 1], None, op0=Alu.is_gt),
              inc="vec" if e_ == NE - 1 else None)
        SS()
        E("vector", lambda e: e.tensor_reduce(sb["red"][:, 0:8],
                                              sb["cmpb"][:, :].rearrange("p (e a) -> p e a", a=128),
                                              axis=AX.X, op=Alu.add), inc="vec")
        W("tensor", "vec")
        E("tensor", lambda e: e.matmul(ps["pH"][0:8, 32:33], sb["red"][:, 0:8], sb["onescF"][:, :],
                                       start=True, stop=True), inc="ten")
        W("vector", "ten")
        E("vector", lambda e, out_col=out_col: e.tensor_copy(sb["cntf"][0:8, out_col:out_col + 1],
                                                             ps["pH"][0:8, 32:33]), inc="vec")
        SS()

    E("vector", lambda e: e.memset(sb["xsp"][0:8, 0:8], 0.0), inc="vec")
    SS()
    for bit in range(30, -1, -1):
        hinc = float(1 << (bit - 16)) if bit >= 16 else 0.0
        linc = float(1 << bit) if bit < 16 else 0.0
        E("vector", lambda e, hinc=hinc: e.tensor_scalar(sb["xsp"][0:8, 2:3], sb["xsp"][0:8, 0:1],
                                                         hinc, None, op0=Alu.add))
        E("vector", lambda e, linc=linc: e.tensor_scalar(sb["xsp"][0:8, 3:4], sb["xsp"][0:8, 1:2],
                                                         linc, None, op0=Alu.add), inc="vec")
        SS()
        E("vector", lambda e: e.tensor_copy(sb["sint"][0:8, 2:3], sb["xsp"][0:8, 2:3]))
        E("vector", lambda e: e.tensor_copy(sb["sint"][0:8, 3:4], sb["xsp"][0:8, 3:4]), inc="vec")
        SS()
        E("vector", lambda e: e.tensor_scalar(sb["sint"][0:8, 2:3], sb["sint"][0:8, 2:3], 16, None,
                                              op0=Alu.logical_shift_left), inc="vec")
        SS()
        E("vector", lambda e: e.tensor_tensor(sb["sint"][0:8, 2:3], sb["sint"][0:8, 2:3],
                                              sb["sint"][0:8, 3:4], Alu.bitwise_or), inc="vec")
        SS()
        bcast_exact(2)
        count_gt(0)
        E("vector", lambda e: e.tensor_scalar(sb["xsp"][0:8, 4:5], sb["cntf"][0:8, 0:1], 4096.0, None,
                                              op0=Alu.is_ge), inc="vec")
        SS()
        E("vector", lambda e: e.tensor_copy(sb["sint"][0:8, 9:10], sb["xsp"][0:8, 4:5]), inc="vec")
        SS()
        E("vector", lambda e: e.select(sb["xsp"][0:8, 0:1], sb["sint"][0:8, 9:10],
                                       sb["xsp"][0:8, 2:3], sb["xsp"][0:8, 0:1]))
        E("vector", lambda e: e.select(sb["xsp"][0:8, 1:2], sb["sint"][0:8, 9:10],
                                       sb["xsp"][0:8, 3:4], sb["xsp"][0:8, 1:2]), inc="vec")
        SS()
    E("vector", lambda e: e.tensor_scalar(sb["xsp"][0:8, 1:2], sb["xsp"][0:8, 1:2], 1.0, None, op0=Alu.add),
      inc="vec")
    SS()
    E("vector", lambda e: e.tensor_scalar(sb["xsp"][0:8, 4:5], sb["xsp"][0:8, 1:2], 65535.0, None,
                                          op0=Alu.is_gt), inc="vec")
    SS()
    E("vector", lambda e: e.tensor_scalar(sb["xsp"][0:8, 5:6], sb["xsp"][0:8, 4:5], -65536.0, None,
                                          op0=Alu.mult), inc="vec")
    SS()
    E("vector", lambda e: e.tensor_tensor(sb["xsp"][0:8, 1:2], sb["xsp"][0:8, 1:2],
                                          sb["xsp"][0:8, 5:6], Alu.add))
    E("vector", lambda e: e.tensor_tensor(sb["xsp"][0:8, 0:1], sb["xsp"][0:8, 0:1],
                                          sb["xsp"][0:8, 4:5], Alu.add), inc="vec")
    SS()
    E("vector", lambda e: e.tensor_copy(sb["sint"][0:8, 1:2], sb["xsp"][0:8, 0:1]))
    E("vector", lambda e: e.tensor_copy(sb["sint"][0:8, 3:4], sb["xsp"][0:8, 1:2]), inc="vec")
    SS()
    E("vector", lambda e: e.tensor_scalar(sb["sint"][0:8, 1:2], sb["sint"][0:8, 1:2], 16, None,
                                          op0=Alu.logical_shift_left), inc="vec")
    SS()
    E("vector", lambda e: e.tensor_tensor(sb["sint"][0:8, 1:2], sb["sint"][0:8, 1:2],
                                          sb["sint"][0:8, 3:4], Alu.bitwise_or), inc="vec")
    SS()
    bcast_exact(1)
    count_gt(2)
    W("vector", "vec")
    for e_ in range(NE):
        E("vector", lambda e, e_=e_: e.tensor_scalar(
            sb["eqm"][:, e_ * 128:(e_ + 1) * 128], sb["S2"][:, e_ * 128:(e_ + 1) * 128],
            sb["tbits"].bitcast(F32)[:, e_:e_ + 1], None, op0=Alu.is_equal),
          inc="vec" if e_ == NE - 1 else None)
    SS()
    E("vector", lambda e: e.tensor_copy(sb["selres"][0:8, 0:1].bitcast(I32), sb["sint"][0:8, 1:2]))
    E("vector", lambda e: e.tensor_copy(sb["selres"][0:8, 2:3], sb["cntf"][0:8, 2:3]), inc="vec")
    E("vector", lambda e: e.memset(sb["xsp"][0:8, 5:6], 0.0), inc="vec")
    SS()
    for bit in range(14, -1, -1):
        E("vector", lambda e, v=float(1 << bit): e.tensor_scalar(sb["xsp"][0:8, 6:7], sb["xsp"][0:8, 5:6],
                                                                 v, None, op0=Alu.add), inc="vec")
        SS()
        E("vector", lambda e: e.tensor_copy(sb["sint"][0:8, 6:7], sb["xsp"][0:8, 6:7]), inc="vec")
        SS()
        bcast_exact(6)
        E("vector", lambda e: e.tensor_copy(sb["mflt"][:, :], sb["tbits"][:, :]), inc="vec")
        SS()
        for e_ in range(NE):
            E("vector", lambda e, e_=e_: e.tensor_scalar(
                sb["cmpb"][:, e_ * 128:(e_ + 1) * 128], sb["tvalf"][:, :],
                sb["mflt"][:, e_:e_ + 1], None, op0=Alu.is_lt),
              inc="vec" if e_ == NE - 1 else None)
        SS()
        E("vector", lambda e: e.tensor_tensor(sb["cmpb"][:, :], sb["cmpb"][:, :], sb["eqm"][:, :],
                                              Alu.mult), inc="vec")
        SS()
        E("vector", lambda e: e.tensor_reduce(sb["red"][:, 0:8],
                                              sb["cmpb"][:, :].rearrange("p (e a) -> p e a", a=128),
                                              axis=AX.X, op=Alu.add), inc="vec")
        W("tensor", "vec")
        E("tensor", lambda e: e.matmul(ps["pH"][0:8, 32:33], sb["red"][:, 0:8], sb["onescF"][:, :],
                                       start=True, stop=True), inc="ten")
        W("vector", "ten")
        E("vector", lambda e: e.tensor_copy(sb["cntf"][0:8, 0:1], ps["pH"][0:8, 32:33]), inc="vec")
        SS()
        E("vector", lambda e: e.tensor_tensor(sb["cntf"][0:8, 1:2], sb["cntf"][0:8, 0:1],
                                              sb["cntf"][0:8, 2:3], Alu.add), inc="vec")
        SS()
        E("vector", lambda e: e.tensor_scalar(sb["xsp"][0:8, 4:5], sb["cntf"][0:8, 1:2], 4096.0, None,
                                              op0=Alu.is_lt), inc="vec")
        SS()
        E("vector", lambda e: e.tensor_copy(sb["sint"][0:8, 9:10], sb["xsp"][0:8, 4:5]), inc="vec")
        SS()
        E("vector", lambda e: e.select(sb["xsp"][0:8, 5:6], sb["sint"][0:8, 9:10],
                                       sb["xsp"][0:8, 6:7], sb["xsp"][0:8, 5:6]), inc="vec")
        SS()
    E("vector", lambda e: e.tensor_scalar(sb["xsp"][0:8, 5:6], sb["xsp"][0:8, 5:6], 1.0, None, op0=Alu.add),
      inc="vec")
    SS()
    E("vector", lambda e: e.tensor_copy(sb["selres"][0:8, 1:2], sb["xsp"][0:8, 5:6]), inc="vec")
    SS()
    E("vector", lambda e: e.tensor_copy(sb["sint"][0:8, 6:7], sb["xsp"][0:8, 5:6]), inc="vec")
    SS()
    bcast_exact(6)
    E("vector", lambda e: e.tensor_copy(sb["mflt"][:, :], sb["tbits"][:, :]), inc="vec")
    SS()
    bcast_exact(1)
    for e_ in range(NE):
        E("vector", lambda e, e_=e_: e.tensor_scalar(
            sb["cmpb"][:, e_ * 128:(e_ + 1) * 128], sb["tvalf"][:, :],
            sb["mflt"][:, e_:e_ + 1], None, op0=Alu.is_lt), inc="vec" if e_ == NE - 1 else None)
    SS()
    E("vector", lambda e: e.tensor_tensor(sb["cmpb"][:, :], sb["cmpb"][:, :], sb["eqm"][:, :], Alu.mult),
      inc="vec")
    SS()
    for e_ in range(NE):
        E("vector", lambda e, e_=e_: e.tensor_scalar(
            sb["selmask"][:, e_ * 128:(e_ + 1) * 128], sb["S2"][:, e_ * 128:(e_ + 1) * 128],
            sb["tbits"].bitcast(F32)[:, e_:e_ + 1], None, op0=Alu.is_gt), inc="vec" if e_ == NE - 1 else None)
    SS()
    E("vector", lambda e: e.tensor_tensor(sb["selmask"][:, :], sb["selmask"][:, :], sb["cmpb"][:, :],
                                          Alu.add), inc="vec")
    SS()
    b.sfree("xsp", "mflt", "iot", "eqm", "tvalf", "cntf", "onescF", "tbscr", "tbits", "red", "cmpb",
            "ones4", "sbyte", "sint")


def stage_r2(b):
    p, sb, g, ps = b.p, b.sb, b.g, b.ps
    E, W = p.emit, p.wait
    from concourse.bass_core_rust import IndirectOffsetOnAxis

    def SS(eng="vector", sem="vec"):
        W(eng, sem)

    b.dram_tmp("IL_dram", [NE, 128, 32, 2])
    b.dram_tmp("ILfull_dram", [NE, 128, 32, 2])
    b.sbuf("LTu", [128, NE * 16], U32)
    b.sbuf("scanb", [128, 128], F32, side="right")
    b.sbuf("rank", [128, 128], F32, side="right")
    b.sbuf("rtot", [128, 4], F32, side="right")
    b.sbuf("tri", [128, 128], F32, side="right")
    b.sbuf("onesP", [128, 128], F32, side="right")
    b.sbuf("i1", [128, 128], I32, side="right")
    b.sbuf("i2", [128, 128], I32, side="right")
    b.sbuf("exi", [128, 8 * 16], F32, side="right")
    b.sbuf("exs", [128, 64], F32, side="right")
    b.sbuf("pairs", [128, 16 * 2], F32, side="right")
    b.sbuf("dsti", [128, 16], U32, side="right")
    b.sbuf("iscr", [128, 64], I32, side="right")
    b.sbuf("zz", [128, 512], F32, side="right")

    p.fence(("gpsimd", "sync"))
    E("gpsimd", lambda e: e.iota(sb["i1"][:, :], pattern=[[0, 128]], base=0, channel_multiplier=1))
    E("gpsimd", lambda e: e.iota(sb["i2"][:, :], pattern=[[1, 128]], base=0, channel_multiplier=0), inc="qdma")
    W("vector", "qdma")
    E("vector", lambda e: e.tensor_tensor(sb["tri"][:, :], sb["i1"][:, :], sb["i2"][:, :], Alu.is_lt))
    E("vector", lambda e: e.memset(sb["onesP"][:, :], 1.0))
    E("vector", lambda e: e.memset(sb["zz"][:, :], 0.0), inc="vec")
    W("sync", "vec")
    E("sync", lambda e: e.dma_start(out=g["IL_dram"][:, :, :, :], in_=sb["zz"][:, :]), inc="sdma", inc_n=16)
    zero_done = p.val("sdma")

    bcr = [None]

    def mk_bcr(e):
        if bcr[0] is None:
            bcr[0] = e.alloc_register("bcreg")
            e.reg_mov(bcr[0], 32767)
        return bcr[0]

    scat_cnt = [0]
    scat_after = {}
    for e_ in range(NE):
        if e_ > 0:
            W("vector", "gdma", scat_after[e_ - 1])
        msl = sb["selmask"][:, e_ * 128:(e_ + 1) * 128]
        W("vector", "vec")
        E("vector", lambda e, msl=msl: e.tensor_tensor_scan(sb["scanb"][:, :], sb["onesP"][:, :], msl,
                                                            op0=Alu.mult, op1=Alu.add, initial=0.0), inc="vec")
        SS()
        E("vector", lambda e: e.tensor_copy(sb["rtot"][:, 0:1], sb["scanb"][:, 127:128]), inc="vec")
        W("tensor", "vec")
        E("tensor", lambda e: e.matmul(ps["pG"][:, 0:1], sb["tri"][:, :], sb["rtot"][:, 0:1],
                                       start=True, stop=True), inc="ten")
        W("vector", "ten")
        E("vector", lambda e: e.tensor_copy(sb["rtot"][:, 1:2], ps["pG"][:, 0:1]), inc="vec")
        SS()
        E("vector", lambda e, msl=msl: e.tensor_tensor(sb["rank"][:, :], sb["scanb"][:, :], msl,
                                                       Alu.subtract), inc="vec")
        SS()
        E("vector", lambda e: e.tensor_scalar(sb["rank"][:, :], sb["rank"][:, :], sb["rtot"][:, 1:2], None,
                                              op0=Alu.add), inc="vec")
        SS()
        if "r2dbg" in b.DBG and e_ == 0:
            W("sync", "vec")
            E("sync", lambda e: e.dma_start(out=b.DBG["r2dbg"][0:128, 0:128], in_=sb["rank"][:, :]), inc="odma", inc_n=16)
            E("sync", lambda e: e.dma_start(out=b.DBG["r2dbg"][0:128, 128:132], in_=sb["rtot"][:, 0:4]), inc="odma", inc_n=16)
            E("sync", lambda e: e.dma_start(out=b.DBG["r2dbg"][0:128, 132:260], in_=sb["scanb"][:, :]), inc="odma", inc_n=16)
            E("sync", lambda e: e.dma_start(out=b.DBG["r2dbg"][0:128, 260:264], in_=sb["tri"][:, 40:44]), inc="odma", inc_n=16)
            W("vector", "odma")
        E("vector", lambda e: e.tensor_copy(sb["i1"][:, :], sb["rank"][:, :]), inc="vec")
        SS()
        E("vector", lambda e: e.tensor_scalar(sb["i2"][:, :], sb["i1"][:, :], 6, None,
                                              op0=Alu.logical_shift_right), inc="vec")
        SS()
        E("vector", lambda e: e.tensor_scalar(sb["i1"][:, :], sb["i1"][:, :], 63, None,
                                              op0=Alu.bitwise_and), inc="vec")
        SS()
        E("vector", lambda e: e.tensor_copy(sb["scanb"][:, :], sb["i2"][:, :]))
        E("vector", lambda e: e.tensor_copy(sb["rank"][:, :], sb["i1"][:, :]), inc="vec")
        ex_ready = p.val("vec")
        W("tensor", "vec", ex_ready)
        E("tensor", lambda e: e.matmul(ps["pG"][:, 0:16], sb["scanb"][:, :], g_sel(b), start=True, stop=True))
        E("tensor", lambda e: e.matmul(ps["pG"][:, 16:32], sb["rank"][:, :], g_sel(b), start=True, stop=True))
        E("tensor", lambda e, msl=msl: e.matmul(ps["pG"][:, 32:48], msl, g_sel(b), start=True, stop=True))
        E("tensor", lambda e, e_=e_: e.matmul(ps["pG"][:, 48:64], sb["S2"][:, e_ * 128:(e_ + 1) * 128],
                                              g_sel(b), start=True, stop=True), inc="ten")
        W("vector", "ten")
        E("vector", lambda e: e.tensor_copy(sb["exs"][:, 0:64], ps["pG"][:, 0:64]), inc="vec")
        SS()
        E("vector", lambda e: e.tensor_copy(sb["iscr"][:, 0:16], sb["exs"][:, 0:16]))
        E("vector", lambda e: e.tensor_copy(sb["iscr"][:, 16:32], sb["exs"][:, 16:32]), inc="vec")
        SS()
        E("vector", lambda e: e.tensor_scalar(sb["iscr"][:, 0:16], sb["iscr"][:, 0:16], 6, None,
                                              op0=Alu.logical_shift_left), inc="vec")
        SS()
        E("vector", lambda e: e.tensor_tensor(sb["iscr"][:, 0:16], sb["iscr"][:, 0:16], sb["iscr"][:, 16:32],
                                              Alu.bitwise_or), inc="vec")
        SS()
        E("vector", lambda e: e.tensor_copy(sb["iscr"][:, 32:48], sb["exs"][:, 32:48]), inc="vec")
        SS()
        E("vector", lambda e, e_=e_: e.memset(sb["LTu"][:, e_ * 16:(e_ + 1) * 16], 4096), inc="vec")
        SS()
        E("vector", lambda e, e_=e_: e.select(sb["LTu"][:, e_ * 16:(e_ + 1) * 16].bitcast(I32),
                                              sb["iscr"][:, 32:48], sb["iscr"][:, 0:16],
                                              sb["LTu"][:, e_ * 16:(e_ + 1) * 16].bitcast(I32)), inc="vec")
        SS()
        E("vector", lambda e: e.tensor_scalar(sb["iscr"][:, 48:64], sb["iscr"][:, 0:16], 127, None,
                                              op0=Alu.bitwise_and), inc="vec")
        SS()
        E("vector", lambda e: e.tensor_scalar(sb["iscr"][:, 48:64], sb["iscr"][:, 48:64], 5, None,
                                              op0=Alu.logical_shift_left), inc="vec")
        SS()
        E("vector", lambda e: e.tensor_scalar(sb["iscr"][:, 16:32], sb["iscr"][:, 0:16], 7, None,
                                              op0=Alu.logical_shift_right), inc="vec")
        SS()
        E("vector", lambda e: e.tensor_tensor(sb["iscr"][:, 48:64], sb["iscr"][:, 48:64], sb["iscr"][:, 16:32],
                                              Alu.bitwise_or), inc="vec")
        SS()
        E("vector", lambda e, e_=e_: e.tensor_scalar(sb["iscr"][:, 48:64], sb["iscr"][:, 48:64],
                                                     e_ << 12, None, op0=Alu.bitwise_or), inc="vec")
        SS()
        E("vector", lambda e: e.memset(sb["dsti"][:, :], 0x7FFFFFFF), inc="vec")
        SS()
        E("vector", lambda e: e.select(sb["dsti"][:, :].bitcast(I32), sb["iscr"][:, 32:48],
                                       sb["iscr"][:, 48:64], sb["dsti"][:, :].bitcast(I32)), inc="vec")
        SS()
        E("vector", lambda e: e.tensor_copy(sb["pairs"][:, :].rearrange("p (c t) -> p c t", t=2)[:, :, 0],
                                            g_idrow_s(b)))
        E("vector", lambda e: e.tensor_copy(sb["pairs"][:, :].rearrange("p (c t) -> p c t", t=2)[:, :, 1],
                                            sb["exs"][:, 48:64]), inc="vec")
        pair_done = p.val("vec")
        W("gpsimd", "vec", pair_done)
        if e_ == 0:
            W("gpsimd", "sdma", zero_done)
        for c in range(16):
            E("gpsimd", lambda e, e_=e_, c=c: e.indirect_dma_start(
                out=g["IL_dram"].rearrange("n p b t -> (n p b) t"),
                out_offset=IndirectOffsetOnAxis(sb["dsti"][:, c:c + 1], 0),
                in_=sb["pairs"][:, 2 * c:2 * c + 2], in_offset=None,
                bounds_check=mk_bcr(e), oob_is_err=False), inc="gdma", inc_n=16)
            scat_cnt[0] += 16
        scat_after[e_] = p.val("gdma")
    if "ILown" in b.DBG:
        W("sync", "gdma")
        E("sync", lambda e: e.dma_start(out=b.DBG["ILown"][:, :, :, :], in_=g["IL_dram"][:, :, :, :]),
          inc="odma", inc_n=16)
        W("sync", "vec")
        E("sync", lambda e: e.dma_start(out=b.DBG["dst0"][:, :], in_=sb["dsti"][:, :]), inc="odma", inc_n=16)
    W("gpsimd", "gdma")
    E("gpsimd", lambda e: e.collective_compute(
        "AllReduce", Alu.add, replica_groups=[list(range(8))],
        ins=[g["IL_dram"].opt()], outs=[g["ILfull_dram"].opt()]), inc="cc")
    b.sfree("zz", "iscr", "dsti", "pairs", "exs", "exi", "i2", "i1", "onesP", "tri", "rtot", "rank", "scanb")


def g_sel(b):
    return b.sb["selin"][:, :]


def g_idrow_s(b):
    return b.sb["idrowin"][:, :]


def stage_b(b):
    p, sb, g, ps = b.p, b.sb, b.g, b.ps
    E, W = p.emit, p.wait
    from concourse.bass_core_rust import IndirectOffsetOnAxis

    def SS(eng="vector", sem="vec"):
        W(eng, sem)

    b.dram_tmp("Zloc", [CAP + 1, D], BF16)
    b.dram_tmp("Zfull", [NE * (CAP + 1), D], BF16, shared=True)
    b.sbuf("w13", [128, NC * 2 * EH], BF16)
    b.sbuf("w2s", [128, 16 * D], BF16)
    b.sbuf("ILs", [128, 64], F32, side="right")
    b.sbuf("idu", [128, 32], U32, side="right")
    b.sbuf("sclb", [128, 32], F32, side="right")
    b.sbuf("eoffs", [128, 1], U32, side="right")
    b.sbuf("yt", [128, 2 * D], BF16, side="right")
    b.sbuf("rt", [128, D], BF16, side="right")
    b.sbuf("r_fm", [128, NC * 512], BF16, side="right")
    b.sbuf("hh", [128, 16 * 512], BF16, side="right")
    b.sbuf("silb", [128, 512], BF16, side="right")
    b.sbuf("ztile", [128, 512], BF16, side="right")
    b.sbuf("zrow", [1, D], BF16, side="right")

    p.fence(("gpsimd", "sync"))
    for k in range(NC):
        E("gpsimd", lambda e, k=k: e.dma_start(out=sb["w13"][:, k * EH:(k + 1) * EH],
                                               in_=g["ew1"][k * 128:(k + 1) * 128, :]), inc="gdma", inc_n=16)
    for k in range(NC):
        E("gpsimd", lambda e, k=k: e.dma_start(out=sb["w13"][:, NC * EH + k * EH:NC * EH + (k + 1) * EH],
                                               in_=g["ew3"][k * 128:(k + 1) * 128, :]), inc="gdma", inc_n=16)
    for k in range(16):
        E("gpsimd", lambda e, k=k: e.dma_start(out=sb["w2s"][:, k * D:(k + 1) * D],
                                               in_=g["ew2"][k * 128:(k + 1) * 128, :]), inc="gdma", inc_n=16)
    E("sync", lambda e: e.dma_start(out=sb["eoffs"][:, :], in_=g["eoff"][:, :]), inc="sdma", inc_n=16)
    W("vector", "vec")
    E("vector", lambda e: e.memset(sb["zrow"][0:1, :], 0.0), inc="vec")
    W("sync", "vec")
    E("sync", lambda e: e.dma_start(out=g["Zloc"][CAP:CAP + 1, :], in_=sb["zrow"][0:1, :]), inc="sdma", inc_n=16)
    W("gpsimd", "cc", 3)
    W("gpsimd", "sdma")
    E("gpsimd", lambda e: e.indirect_dma_start(
        out=sb["ILs"][:, :], out_offset=None,
        in_=g["ILfull_dram"].rearrange("n p b t -> (n p) (b t)"),
        in_offset=IndirectOffsetOnAxis(sb["eoffs"][:, 0:1], 0)), inc="gdma", inc_n=16)
    W("vector", "gdma")
    E("vector", lambda e: e.tensor_copy(sb["idu"][:, :],
                                        sb["ILs"][:, :].rearrange("p (c t) -> p c t", t=2)[:, :, 0]))
    E("vector", lambda e: e.tensor_copy(sb["sclb"][:, :],
                                        sb["ILs"][:, :].rearrange("p (c t) -> p c t", t=2)[:, :, 1]), inc="vec")
    idu_done = p.val("vec")
    if "bdump" in b.DBG:
        W("sync", "vec")
        E("sync", lambda e: e.dma_start(out=b.DBG["bdump"][0:128, 0:32], in_=sb["idu"][:, :].bitcast(F32)),
          inc="odma", inc_n=16)
        E("sync", lambda e: e.dma_start(out=b.DBG["bdump"][0:128, 32:96], in_=sb["ILs"][:, :]), inc="odma", inc_n=16)
        W("vector", "odma")

    NCH = CAP // 512
    gat_done = {}
    rt_used = {}
    rfm_done = {}
    uv_done = {}
    z_dma_done = {}
    for ch in range(NCH):
        for j in range(4):
            bslot = ch * 4 + j
            ybuf = (bslot % 2) * D
            W("gpsimd", "vec", idu_done)
            if bslot >= 2:
                W("gpsimd", "scl", gat_done[bslot - 2])
            E("gpsimd", lambda e, bslot=bslot, ybuf=ybuf: e.indirect_dma_start(
                out=sb["yt"][:, ybuf:ybuf + D], out_offset=None,
                in_=g["Yfull_dram"][:, :],
                in_offset=IndirectOffsetOnAxis(sb["idu"][:, bslot:bslot + 1], 0)), inc="gdma", inc_n=16)
            W("scalar", "gdma")
            if bslot >= 1:
                W("scalar", "ten", rt_used[bslot - 1])
            E("scalar", lambda e, bslot=bslot, ybuf=ybuf: e.activation(
                out=sb["rt"][:, :], in_=sb["yt"][:, ybuf:ybuf + D], func=AF.Identity,
                scale=sb["sclb"][:, bslot:bslot + 1]), inc="scl")
            gat_done[bslot] = p.val("scl")
            if "bdump" in b.DBG and bslot == 0:
                W("sync", "scl")
                E("sync", lambda e, ybuf=ybuf: e.dma_start(out=b.DBG["bdump"][0:128, 96:96 + 512],
                                                           in_=sb["yt"][:, ybuf:ybuf + D].bitcast(F32)),
                  inc="odma", inc_n=16)
                W("vector", "odma")
                W("scalar", "odma")
            W("tensor", "scl", gat_done[bslot])
            for k in range(NC):
                pk = ps["pG"] if k % 2 == 0 else ps["pH"]
                W("tensor", "vec")
                E("tensor", lambda e, k=k, pk=pk: e.transpose(
                    pk[:, 0:64].bitcast(BF16), sb["rt"][:, k * 128:(k + 1) * 128], sb["identb"][:, :]),
                  inc="ten")
                W("vector", "ten")
                E("vector", lambda e, k=k, j=j, pk=pk: e.tensor_copy(
                    sb["r_fm"][:, k * 512 + j * 128:k * 512 + (j + 1) * 128], pk[:, 0:64].bitcast(BF16)),
                  inc="vec")
            rt_used[bslot] = p.val("ten")
        rfm_done[ch] = p.val("vec")
        W("tensor", "vec", rfm_done[ch])
        for hc in range(16):
            pu = ps["pA"] if hc % 2 == 0 else ps["pB"]
            pv = ps["pC"] if hc % 2 == 0 else ps["pD"]
            if hc >= 2:
                W("tensor", "vec", uv_done[(ch, hc - 2)])
            for k in range(NC):
                E("tensor", lambda e, hc=hc, k=k, pu=pu: e.matmul(
                    pu[:, :], sb["w13"][:, k * EH + hc * 128:k * EH + (hc + 1) * 128],
                    sb["r_fm"][:, k * 512:(k + 1) * 512], start=(k == 0), stop=(k == NC - 1)))
            for k in range(NC):
                E("tensor", lambda e, hc=hc, k=k, pv=pv: e.matmul(
                    pv[:, :], sb["w13"][:, NC * EH + k * EH + hc * 128:NC * EH + k * EH + (hc + 1) * 128],
                    sb["r_fm"][:, k * 512:(k + 1) * 512], start=(k == 0), stop=(k == NC - 1)), 
                  inc="ten" if k == NC - 1 else None)
            W("scalar", "ten")
            if hc >= 1:
                W("scalar", "vec", uv_done[(ch, hc - 1)])
            E("scalar", lambda e, pu=pu: e.activation(out=sb["silb"][:, :], in_=pu[:, :], func=AF.Silu),
              inc="scl")
            W("vector", "scl")
            E("vector", lambda e, hc=hc, pv=pv: e.tensor_tensor(
                sb["hh"][:, hc * 512:(hc + 1) * 512], sb["silb"][:, :], pv[:, :], Alu.mult), inc="vec")
            uv_done[(ch, hc)] = p.val("vec")
        W2B = [ps["pA"], ps["pB"], ps["pC"], ps["pD"], ps["pE"], ps["pF"], ps["pG"], ps["pH"]]
        W("tensor", "vec", uv_done[(ch, 15)])
        for ms in range(4):
            for n in range(2):
                bank = W2B[ms * 2 + n]
                for hc in range(16):
                    E("tensor", lambda e, ms=ms, n=n, hc=hc, bank=bank: e.matmul(
                        bank[:, :], sb["hh"][:, hc * 512 + ms * 128:hc * 512 + (ms + 1) * 128],
                        sb["w2s"][:, hc * D + n * 512:hc * D + (n + 1) * 512],
                        start=(hc == 0), stop=(hc == 15)),
                      inc="ten" if (ms == 3 and n == 1 and hc == 15) else None)
        W("vector", "ten")
        if ch >= 1:
            W("vector", "odma", z_dma_done[ch - 1])
        for ms in range(4):
            for n in range(2):
                E("vector", lambda e, ms=ms, n=n: e.tensor_copy(sb["ztile"][:, :], W2B[ms * 2 + n][:, :]),
                  inc="vec")
                SS()
                W("scalar", "vec")
                E("scalar", lambda e, ch=ch, ms=ms, n=n: e.dma_start(
                    out=g["Zloc"][ch * 512 + ms * 128:ch * 512 + (ms + 1) * 128, n * 512:(n + 1) * 512],
                    in_=sb["ztile"][:, :]), inc="odma", inc_n=16)
                W("vector", "odma")
        z_dma_done[ch] = p.val("odma")
    W("gpsimd", "odma")
    E("gpsimd", lambda e: e.collective_compute(
        "AllGather", Alu.bypass, replica_groups=[list(range(8))],
        ins=[g["Zloc"].opt()], outs=[g["Zfull"].opt()]), inc="cc")
    b.sfree("zrow", "ztile", "silb", "hh", "r_fm", "rt", "yt", "eoffs", "sclb", "idu", "ILs")
    b.sfree("w2s", "w13")


def stage_c(b):
    p, sb, g, ps = b.p, b.sb, b.g, b.ps
    E, W = p.emit, p.wait
    from concourse.bass_core_rust import IndirectOffsetOnAxis

    def SS(eng="vector", sem="vec"):
        W(eng, sem)

    b.sbuf("acc", [128, NT * D], F32)
    b.sbuf("zt2", [128, 2 * D], BF16, side="right")
    b.sbuf("cidx", [128, 16], U32, side="right")
    b.sbuf("oscr", [128, D], F32, side="right")
    p.fence(("gpsimd", "sync"))
    W("sync", "odma")
    E("sync", lambda e: e.dma_start(out=sb["acc"][:, :].rearrange("p (c d) -> p c d", d=D),
                                    in_=g["y_dram"].rearrange("(c p) d -> p c d", p=128)), inc="sdma", inc_n=16)
    W("vector", "sdma")
    W("gpsimd", "cc", 4)
    add_done = {}
    gath_done = {}
    for e_ in range(NE):
        W("vector", "gdma")
        E("vector", lambda e, e_=e_: e.tensor_scalar(sb["cidx"][:, :], sb["LTu"][:, e_ * 16:(e_ + 1) * 16],
                                                     float(e_ * (CAP + 1)), None, op0=Alu.add), inc="vec")
        cidx_done = p.val("vec")
        for c in range(NT):
            slot = ((e_ * NT + c) % 2) * D
            W("gpsimd", "vec", cidx_done)
            if e_ * NT + c >= 2:
                W("gpsimd", "vec", add_done[e_ * NT + c - 2])
            E("gpsimd", lambda e, c=c, slot=slot: e.indirect_dma_start(
                out=sb["zt2"][:, slot:slot + D], out_offset=None,
                in_=g["Zfull"][:, :],
                in_offset=IndirectOffsetOnAxis(sb["cidx"][:, c:c + 1], 0)), inc="gdma", inc_n=16)
            W("vector", "gdma")
            E("vector", lambda e, c=c, slot=slot: e.tensor_tensor(
                sb["acc"][:, c * D:(c + 1) * D], sb["acc"][:, c * D:(c + 1) * D],
                sb["zt2"][:, slot:slot + D], Alu.add), inc="vec")
            add_done[e_ * NT + c] = p.val("vec")
    out_dma = {}
    for c in range(NT):
        W("vector", "vec", add_done[7 * NT + c])
        slotl = c % 2
        if c >= 1:
            W("scalar", "odma", out_dma[c - 1])
        ln_tile(b, sb["acc"][:, c * D:(c + 1) * D], sb["oscr"][:, :], slotl)
        W("scalar", "scl")
        E("scalar", lambda e, c=c: e.dma_start(out=g["y_out"][c * 128:(c + 1) * 128, :],
                                               in_=sb["oscr"][:, :]), inc="odma", inc_n=16)
        out_dma[c] = p.val("odma")
    W("scalar", "odma")
    W("vector", "odma")
    W("sync", "odma")


from concourse.bass_utils import run_bass_kernel_spmd

_KEY = {
    "Wq": "L0_attn_Wq", "Wk": "L0_attn_Wk", "Wv": "L0_attn_Wv", "Wfq": "L0_attn_Wfq", "Wfk": "L0_attn_Wfk",
    "bq": "L0_attn_bq", "gq": "L0_attn_gq", "betaq": "L0_attn_betaq", "bk": "L0_attn_bk", "gk": "L0_attn_gk",
    "betak": "L0_attn_betak", "bv": "L0_attn_bv", "gv": "L0_attn_gv", "betav": "L0_attn_betav",
    "ga": "L0_attn_ga", "ba": "L0_attn_ba", "bfq": "L0_attn_bfq", "bfk": "L0_attn_bfk", "b2": "L0_attn_b2",
    "W1": "L0_attn_W1", "b1": "L0_attn_b1", "W2": "L0_attn_W2", "wg": "L0_ffn_wg",
}

_NC_CACHE = {}


def _build():
    if "nc" not in _NC_CACHE:
        b = B()
        declare(b)
        stage_consts(b)
        stage_a0(b)
        stage_a1(b)
        stage_a2(b)
        stage_a3(b)
        stage_a4(b, dev_y=False)
        stage_r1(b)
        stage_r2(b)
        stage_b(b)
        stage_c(b)
        _NC_CACHE["nc"] = finish(b)
    return _NC_CACHE["nc"]


def kernel(**inputs):
    nc = _build()
    inp = {k: np.ascontiguousarray(np.asarray(v)) for k, v in inputs.items()}
    shared = {}
    for mine, theirs in _KEY.items():
        a = inp[theirs]
        shared[mine] = a.reshape(1, -1).astype(np.float32) if a.ndim == 1 else a.astype(np.float32)
    x = inp["L0_attn_x"].astype(np.float32)
    w1f, w3f, w2f = inp["L0_ffn_w1"], inp["L0_ffn_w3"], inp["L0_ffn_w2"]
    in_maps = []
    for i in range(8):
        sel = np.zeros((128, 16), np.float32)
        for j in range(16):
            sel[16 * i + j, j] = 1.0
        idrow = ((16 * i + np.arange(16))[None, :] * 128 + np.arange(128)[:, None]).astype(np.float32)
        eoff = (i * 128 + np.arange(128))[:, None].astype(np.uint32)
        in_maps.append(dict(shared, x=np.ascontiguousarray(x[i]), sel=sel, idrow=idrow, eoff=eoff,
                            ew1=np.ascontiguousarray(w1f[i]).astype(np.float32),
                            ew3=np.ascontiguousarray(w3f[i]).astype(np.float32),
                            ew2=np.ascontiguousarray(w2f[i]).astype(np.float32)))
    res = run_bass_kernel_spmd(nc, in_maps, core_ids=list(range(8)))
    return np.stack([np.asarray(res.results[i]["y_out"]) for i in range(8)], 0)


# version 09
# speedup vs baseline: 1.1096x; 1.0682x over previous
import numpy as np
import concourse.bass as bass
import concourse.mybir as mybir

F32 = mybir.dt.float32
BF16 = mybir.dt.bfloat16
U32 = mybir.dt.uint32
I32 = mybir.dt.int32
AF = mybir.ActivationFunctionType
Alu = mybir.AluOpType
AX = mybir.AxisListType

L, D, H, DH, MLPD, NE, CAP, EH = 2048, 1024, 16, 64, 4096, 8, 4096, 2048
A3_STOP = None
NT, NC = L // 128, D // 128
EPS = 1e-5


class P:
    def __init__(self, nc):
        self.nc = nc
        self.prog = {e: [] for e in ("sync", "gpsimd", "vector", "scalar", "tensor")}
        self.sem = {}
        self.cnt = {}

    def new_sem(self, name):
        self.sem[name] = self.nc.semaphore(name).__enter__()
        self.cnt[name] = 0

    def emit(self, eng, fn, inc=None, inc_n=1):
        sem_h = self.sem[inc] if inc else None
        if inc:
            self.cnt[inc] += inc_n

        def run(e, fn=fn, sem_h=sem_h, inc_n=inc_n):
            r = fn(e)
            if sem_h is not None:
                r.then_inc(sem_h, inc_n)
        self.prog[eng].append(run)

    def wait(self, eng, sname, val=None):
        val = self.cnt[sname] if val is None else val
        h = self.sem[sname]
        self.prog[eng].append(lambda e, h=h, v=val: e.wait_ge(h, v))

    def val(self, sname):
        return self.cnt[sname]

    def barrier_all(self):
        for eng in ("sync", "gpsimd", "vector", "scalar", "tensor"):
            for s in ("ten", "vec", "scl", "sdma", "gdma", "odma", "qdma"):
                if self.cnt[s] > 0:
                    self.wait(eng, s)

    def fence(self, writers, sems=("ten", "vec", "scl")):
        for w in writers:
            for s in sems:
                if self.cnt[s] > 0:
                    self.wait(w, s)

    def finalize(self, block):
        for name, deco in (("sync", block.sync), ("gpsimd", block.gpsimd),
                           ("vector", block.vector), ("scalar", block.scalar),
                           ("tensor", block.tensor)):
            fns = list(self.prog[name])

            def make(fns):
                def _(eng):
                    for f in fns:
                        f(eng)
                return _
            deco(make(fns))


class B:

    def __init__(self, dbg=()):
        self.nc = bass.Bass("TRN2", target_bir_lowering=False)
        self.g = {}
        self.sb = {}
        self.cm = {}
        self.ps = {}
        self.p = P(self.nc)
        self.DBG = {}
        self.dbg_names = [d[0] for d in dbg]
        self.dbg_spec = dbg

    def dram_in(self, name, shape, dt=F32):
        self.g[name] = self.nc.dram_tensor(name, shape, dt, kind="ExternalInput").ap()

    def dram_out(self, name, shape, dt=F32):
        self.g[name] = self.nc.dram_tensor(name, shape, dt, kind="ExternalOutput").ap()

    def dram_tmp(self, name, shape, dt=F32, shared=False):
        if shared:
            self.g[name] = self.nc.dram_tensor(name, shape, dt, addr_space="Shared").ap()
        else:
            self.g[name] = self.nc.dram_tensor(name, shape, dt).ap()

    def sbuf(self, name, shape, dt=F32, side=None):
        self.cm[name] = self.nc.sbuf_tensor(name, shape, dt, side=side)
        self.sb[name] = self.cm[name].__enter__()
        return self.sb[name]

    def sfree(self, *names):
        for n in names:
            self.cm[n].__exit__(None, None, None)
            del self.cm[n]

    def psum(self, name, shape, dt=F32):
        self.ps[name] = self.nc.psum_tensor(name, shape, dt).__enter__()
        return self.ps[name]


def declare(b):
    b.dram_in("x", [L, D])
    for w in ("Wq", "Wk", "Wv", "Wfq", "Wfk"):
        b.dram_in(w, [D, D])
    for v in ("bq", "gq", "betaq", "bk", "gk", "betak", "bv", "gv", "betav",
              "ga", "ba", "bfq", "bfk", "b2"):
        b.dram_in(v, [1, D])
    b.dram_in("W1", [D, MLPD])
    b.dram_in("b1", [1, MLPD])
    b.dram_in("W2", [MLPD, D])
    b.dram_in("wg", [D, NE])
    b.dram_in("sel", [128, 16])
    b.dram_in("idrow", [128, 16])
    b.dram_in("eoff", [128, 1], U32)
    b.dram_in("ew1", [D, EH])
    b.dram_in("ew3", [D, EH])
    b.dram_in("ew2", [EH, D])
    if "mixer" in b.dbg_names:
        b.dram_out("mixer_dram", [L, D])
    else:
        b.dram_tmp("mixer_dram", [L, D])
    b.dram_tmp("y_dram", [L, D])
    b.dram_tmp("agy_dram", [L, D], BF16)
    b.dram_tmp("sag_dram", [NT, NE, 128])
    b.dram_out("y_out", [L, D])
    b.dram_out("s_out", [L, NE])
    for name, shape, dt in b.dbg_spec:
        if shape is None:
            continue
        b.dram_out("dbg_" + name, shape, dt)
        b.DBG[name] = b.g["dbg_" + name]
    for s in ("sdma", "gdma", "qdma", "vec", "scl", "ten", "odma"):
        b.p.new_sem(s)
    for n in ("pA", "pB", "pC", "pD", "pE", "pF", "pG", "pH"):
        b.psum(n, [128, 512], F32)


def stage_consts(b):
    p, sb, g, ps = b.p, b.sb, b.g, b.ps
    E, W = p.emit, p.wait
    b.sbuf("ident", [128, 128], F32)
    b.sbuf("identb", [128, 128], BF16)
    b.sbuf("onesr", [1, 128], F32)
    b.sbuf("onesc", [128, 1], BF16)
    b.sbuf("vt", [128, 48], F32)
    b.VT = {"gq": 0, "betaq": 8, "gk": 16, "betak": 24, "bfk": 32, "scr": 40}
    b.sbuf("b1t", [128, 32], F32)
    b.sbuf("rows", [1, D], F32)
    b.RW = {"b2": 0}
    b.sbuf("btile", [128, 5 * D], F32, side="right")
    b.sbuf("QnT", [128, NT * D], BF16, side="right")
    b.sbuf("KnT", [128, NT * D], BF16, side="right")
    b.BT = {"bq": 0, "bk": D, "bv": 2 * D, "gv": 3 * D, "betav": 4 * D}
    b.sbuf("epsc", [128, 1], F32)
    b.sbuf("onef", [128, 1], F32)
    b.sbuf("qpbuf", [128, 64], F32)
    b.sbuf("selin", [128, 16], F32)
    b.sbuf("idrowin", [128, 16], F32)
    b.sbuf("lnscr", [128, 2 * 16], F32)
    b.sbuf("mvx", [128, 2 * 2], F32)
    b.sbuf("rstd", [128, 2], F32)
    b.sbuf("nbias", [128, 2], F32)
    b.sbuf("prescr", [128, 2 * D], F32)

    b.sbuf("ia", [128, 128], I32, side="right")
    b.sbuf("ib", [128, 128], I32, side="right")
    E("sync", lambda e: e.dma_start(out=sb["selin"][:, :], in_=g["sel"][:, :]), inc="sdma", inc_n=16)
    E("sync", lambda e: e.dma_start(out=sb["idrowin"][:, :], in_=g["idrow"][:, :]), inc="sdma", inc_n=16)
    E("gpsimd", lambda e: e.iota(sb["ia"][:, :], pattern=[[0, 128]], base=0, channel_multiplier=1))
    E("gpsimd", lambda e: e.iota(sb["ib"][:, :], pattern=[[1, 128]], base=0, channel_multiplier=0), inc="qdma")
    W("vector", "qdma")
    E("vector", lambda e: e.tensor_tensor(sb["ident"][:, :], sb["ia"][:, :], sb["ib"][:, :], Alu.is_equal), inc="vec")
    W("vector", "vec")
    E("vector", lambda e: e.tensor_copy(sb["identb"][:, :], sb["ident"][:, :]))
    E("vector", lambda e: e.memset(sb["onesr"][:, :], 1.0))
    E("vector", lambda e: e.memset(sb["onesc"][:, :], 1.0))
    E("vector", lambda e: e.memset(sb["epsc"][:, :], EPS))
    E("vector", lambda e: e.memset(sb["onef"][:, :], 1.0), inc="vec")

    b.sbuf("vstage", [32, 128], F32, side="right")
    for nm in ("gq", "betaq", "gk", "betak", "bfk"):
        W("sync", "ten")
        E("sync", lambda e, nm=nm: e.dma_start(out=sb["vstage"][0:8, :],
                                               in_=g[nm].rearrange("o (a c) -> (o a) c", c=128)),
          inc="sdma", inc_n=16)
        W("tensor", "sdma")
        W("tensor", "vec")
        E("tensor", lambda e: e.transpose(ps["pG"][:, 0:8], sb["vstage"][0:8, :], sb["ident"][0:8, 0:8]), inc="ten")
        W("vector", "ten")
        E("vector", lambda e, c=b.VT[nm]: e.tensor_copy(sb["vt"][:, c:c + 8], ps["pG"][:, 0:8]), inc="vec")
        W("tensor", "vec")
    W("sync", "ten")
    E("sync", lambda e: e.dma_start(out=sb["vstage"][:, :],
                                    in_=g["b1"].rearrange("o (a c) -> (o a) c", c=128)), inc="sdma", inc_n=16)
    W("tensor", "sdma")
    E("tensor", lambda e: e.transpose(ps["pG"][:, 0:32], sb["vstage"][:, :], sb["ident"][0:32, 0:32]), inc="ten")
    W("vector", "ten")
    E("vector", lambda e: e.tensor_copy(sb["b1t"][:, :], ps["pG"][:, 0:32]), inc="vec")

    E("sync", lambda e: e.dma_start(out=sb["rows"][0:1, 0:D], in_=g["b2"][0:1, :]),
      inc="sdma", inc_n=16)
    b.sbuf("rowstage", [1, D], F32, side="right")
    W("tensor", "vec")

    def bcast_from(nm, dst_col):
        W("sync", "ten")
        E("sync", lambda e, nm=nm: e.dma_start(out=sb["rowstage"][0:1, :], in_=g[nm][0:1, :]),
          inc="sdma", inc_n=16)
        W("tensor", "sdma")
        for n in range(2):
            E("tensor", lambda e, n=n: e.matmul(ps["pG"][:, :], sb["onesr"][0:1, 0:128],
                                                sb["rowstage"][0:1, n * 512:(n + 1) * 512],
                                                start=True, stop=True), inc="ten")
            W("vector", "ten")
            E("vector", lambda e, n=n, c=dst_col: e.tensor_copy(
                sb["btile"][:, c + n * 512:c + (n + 1) * 512], ps["pG"][:, :]), inc="vec")
            W("tensor", "vec")

    for nm in ("bq", "bk", "bv", "gv", "betav"):
        bcast_from(nm, b.BT[nm])
    b.sfree("rowstage", "vstage", "ib", "ia")


def ln_tile(b, src_ap, out_ap, slot):
    p, sb = b.p, b.sb
    E, W = p.emit, p.wait
    s6, s2, s1 = slot * 16, slot * 2, slot
    E("vector", lambda e: e.bn_stats(sb["lnscr"][:, s6:s6 + 6], src_ap[:, 0:512]))
    E("vector", lambda e: e.bn_stats(sb["lnscr"][:, s6 + 6:s6 + 12], src_ap[:, 512:1024]), inc="vec")
    W("vector", "vec")
    E("vector", lambda e: e.bn_aggr(sb["mvx"][:, s2:s2 + 2], sb["lnscr"][:, s6:s6 + 12]), inc="vec")
    W("scalar", "vec")
    E("scalar", lambda e: e.activation(out=sb["rstd"][:, s1:s1 + 1], in_=sb["mvx"][:, s2 + 1:s2 + 2],
                                       func=AF.Sqrt, bias=sb["epsc"][:, 0:1]), inc="scl")
    W("vector", "scl")
    E("vector", lambda e: e.reciprocal(sb["rstd"][:, s1:s1 + 1], sb["rstd"][:, s1:s1 + 1]), inc="vec")
    W("vector", "vec")
    E("vector", lambda e: e.tensor_scalar(sb["nbias"][:, s1:s1 + 1], sb["mvx"][:, s2:s2 + 1],
                                          sb["rstd"][:, s1:s1 + 1], -1.0, op0=Alu.mult, op1=Alu.mult), inc="vec")
    W("scalar", "vec")
    E("scalar", lambda e: e.activation(out=out_ap, in_=src_ap, func=AF.Identity,
                                       scale=sb["rstd"][:, s1:s1 + 1], bias=sb["nbias"][:, s1:s1 + 1]), inc="scl")


def stage_a0(b):
    p, sb, g, ps = b.p, b.sb, b.g, b.ps
    E, W = p.emit, p.wait
    b.sbuf("xt", [128, 2 * D], F32, side="right")
    b.sbuf("x_fm", [128, NC * L], BF16, side="right")
    p.fence(("sync",))
    ten_after = {}
    tev_a0 = {}
    W("tensor", "vec")
    W("tensor", "scl")
    for c in range(NT):
        buf = (c % 2) * D
        if c >= 2:
            W("sync", "ten", ten_after[c - 2])
        E("sync", lambda e, c=c, buf=buf: e.dma_start(out=sb["xt"][:, buf:buf + D],
                                                      in_=g["x"][c * 128:(c + 1) * 128, :]), inc="sdma", inc_n=16)
        W("tensor", "sdma")
        for k in range(NC):
            pk = ps["pG"] if k % 2 == 0 else ps["pH"]
            gi = c * NC + k
            if gi >= 2:
                W("tensor", "scl", tev_a0[gi - 2])
            E("tensor", lambda e, k=k, buf=buf, pk=pk: e.transpose(
                pk[:, 0:128], sb["xt"][:, buf + k * 128:buf + (k + 1) * 128], sb["ident"][:, :]), inc="ten")
            W("scalar", "ten")
            E("scalar", lambda e, k=k, c=c, pk=pk: e.activation(
                out=sb["x_fm"][:, k * L + c * 128:k * L + (c + 1) * 128], in_=pk[:, 0:128],
                func=AF.Identity), inc="scl")
            tev_a0[gi] = p.val("scl")
        ten_after[c] = p.val("ten")
    W("tensor", "vec")


def stage_a1(b):
    p, sb, g, ps = b.p, b.sb, b.g, b.ps
    E, W = p.emit, p.wait
    b.sbuf("Ws", [128, 2 * NC * D], BF16, side="right")

    p.fence(("gpsimd",))

    def load_w(wi, wname):
        slot = wi % 2
        for k in range(NC):
            E("gpsimd", lambda e, slot=slot, k=k, wname=wname: e.dma_start(
                out=sb["Ws"][:, (slot * NC + k) * D:(slot * NC + k + 1) * D],
                in_=g[wname][k * 128:(k + 1) * 128, :]), inc="gdma", inc_n=16)
    load_w(0, "Wq")
    load_w(1, "Wk")
    b.sbuf("VfT", [128, NT * (D + 8)], BF16)
    b.sbuf("vscr", [128, D], F32, side="right")
    W("tensor", "scl")
    E("vector", lambda e: e.memset(
        sb["VfT"][:, :].rearrange("p (q c) -> p q c", c=129)[:, :, 128:129], 1.0), inc="vec")
    W("tensor", "gdma")

    PS6 = [(ps["pA"], ps["pB"]), (ps["pC"], ps["pD"]), (ps["pE"], ps["pF"])]
    evac_done = {}
    ln_done = {}
    mg = 0
    ten_after_w = {}
    for wi, outbuf in enumerate(("QnT", "KnT", "VfT")):
        bcol = [b.BT["bq"], b.BT["bk"], b.BT["bv"]][wi]
        W("tensor", "gdma", 16 * NC * (wi + 1))
        for m in range(NT):
            pp = PS6[mg % 3]
            if mg >= 3:
                W("tensor", "vec", evac_done[mg - 3])
            for k in range(NC):
                slot = wi % 2
                for n in range(2):
                    E("tensor", lambda e, m=m, k=k, n=n, slot=slot, pp=pp: e.matmul(
                        pp[n][:, :],
                        sb["x_fm"][:, k * L + m * 128:k * L + (m + 1) * 128],
                        sb["Ws"][:, (slot * NC + k) * D + n * 512:(slot * NC + k) * D + (n + 1) * 512],
                        start=(k == 0), stop=(k == NC - 1)),
                      inc="ten" if (k == NC - 1 and n == 1) else None)
            W("vector", "ten")
            if mg >= 2:
                W("vector", "scl", ln_done[mg - 2])
            slot = mg % 2
            pre = sb["prescr"][:, slot * D:(slot + 1) * D]
            for n in range(2):
                E("vector", lambda e, n=n, pp=pp, bcol=bcol, pre=pre: e.tensor_tensor(
                    pre[:, n * 512:(n + 1) * 512], pp[n][:, :],
                    sb["btile"][:, bcol + n * 512:bcol + (n + 1) * 512], Alu.add),
                  inc="vec" if n == 1 else None)
            evac_done[mg] = p.val("vec")
            W("vector", "vec")
            if wi == 2:
                out_ap = sb["VfT"][:, m * 1032:(m + 1) * 1032].rearrange(
                    "p (h c) -> p h c", c=129)[:, :, 0:128]
            else:
                out_ap = sb[outbuf][:, m * D:(m + 1) * D]
            if wi < 2:
                ln_tile(b, pre, out_ap, slot)
            else:
                ln_tile(b, pre, sb["vscr"][:, :], slot)
                W("vector", "scl")
                E("vector", lambda e: e.tensor_tensor(sb["vscr"][:, :], sb["vscr"][:, :],
                                                      sb["btile"][:, b.BT["gv"]:b.BT["gv"] + D], Alu.mult), inc="vec")
                W("vector", "vec")
                E("vector", lambda e, out_ap=out_ap: e.tensor_tensor(
                    out_ap, sb["vscr"][:, :], sb["btile"][:, b.BT["betav"]:b.BT["betav"] + D], Alu.add), inc="vec")
                evac_done[mg] = p.val("vec")
            ln_done[mg] = p.val("scl")
            mg += 1
        ten_after_w[wi] = p.val("ten")
        if wi == 0:
            W("gpsimd", "ten", ten_after_w[0])
            load_w(2, "Wv")
    W("vector", "scl")
    W("tensor", "scl")
    b.sfree("vscr", "Ws", "x_fm", "xt")


def dbg_dump_tokenmajor(b, name, buf, width=D, dt_rows=128):
    p, sb = b.p, b.sb
    E, W = p.emit, p.wait
    W("sync", "scl")
    W("sync", "vec")
    for m in range(NT):
        E("sync", lambda e, m=m: e.dma_start(out=b.DBG[name][m * 128:(m + 1) * 128, :],
                                             in_=sb[buf][:, m * width:(m + 1) * width]), inc="odma", inc_n=16)


def finish(b):
    blk = b.nc.Block()
    block = blk.__enter__()
    b.p.finalize(block)
    blk.__exit__(None, None, None)
    return b.nc


def stage_a2(b):
    p, sb, ps = b.p, b.sb, b.ps
    E, W = p.emit, p.wait
    b.sbuf("Kn_fm", [128, NC * L], BF16)
    b.sbuf("Qn_fm", [128, NC * L], BF16)
    b.QP = {"sumQ": 0, "qp": 8, "qpp": 16, "qpbk": 24, "tk0": 32, "scr": 40, "ks0": 48, "scr2": 56}
    tev_a2 = {}
    gi = 0
    W("tensor", "vec")
    W("tensor", "scl")
    for src, dst in (("KnT", "Kn_fm"), ("QnT", "Qn_fm")):
        for m in range(NT):
            for k in range(NC):
                pk = ps["pG"] if k % 2 == 0 else ps["pH"]
                if gi >= 2:
                    W("tensor", "scl", tev_a2[gi - 2])
                E("tensor", lambda e, m=m, k=k, pk=pk, src=src: e.transpose(
                    pk[:, 0:64].bitcast(BF16), sb[src][:, m * D + k * 128:m * D + (k + 1) * 128],
                    sb["identb"][:, :]), inc="ten")
                W("scalar", "ten")
                E("scalar", lambda e, m=m, k=k, pk=pk, dst=dst: e.activation(
                    out=sb[dst][:, k * L + m * 128:k * L + (m + 1) * 128], in_=pk[:, 0:64].bitcast(BF16),
                    func=AF.Identity), inc="scl")
                tev_a2[gi] = p.val("scl")
                gi += 1
    for k in range(NC):
        for m in range(NT):
            E("tensor", lambda e, k=k, m=m: e.matmul(
                ps["pF"][:, k:k + 1], sb["QnT"][:, m * D + k * 128:m * D + (k + 1) * 128],
                sb["onesc"][:, 0:1], start=(m == 0), stop=(m == NT - 1)),
              inc="ten" if m == NT - 1 else None)
    W("vector", "ten")
    E("vector", lambda e: e.tensor_copy(sb["qpbuf"][:, 0:8], ps["pF"][:, 0:8]), inc="vec")
    b.sfree("KnT", "QnT", "btile")


def stage_a3(b):
    p, sb, g, ps, QP, VT = b.p, b.sb, b.g, b.ps, b.QP, b.VT
    E, W = p.emit, p.wait

    def SS(eng="vector", sem="vec"):
        W(eng, sem)

    b.sbuf("Wfbuf", [128, NC * D], BF16, side="right")
    b.sbuf("rows2", [1, 2 * D], F32, side="right")
    b.sbuf("lg", [16, 2052], F32, side="right")
    b.sbuf("qpbd", [128, 128], BF16, side="right")
    b.sbuf("qpbkbd", [128, 128], BF16, side="right")
    b.sbuf("score_t", [128, 256], F32, side="right")
    b.sbuf("KVd", [128, 8 * 130], BF16, side="right")
    b.sbuf("KsumF", [128, 8], F32, side="right")
    b.sbuf("s0f", [128, 8], F32, side="right")
    b.sbuf("vtb", [128, 16], BF16, side="right")
    b.sbuf("onesrb", [1, 128], BF16, side="right")
    b.sbuf("bfb", [1, D], BF16, side="right")
    b.sbuf("E1", [16, 128], BF16, side="right")
    b.sbuf("ch1", [16, 8], F32, side="right")
    b.sbuf("s0rhs", [16, 8], BF16, side="right")
    b.sbuf("smx", [16, 4], F32, side="right")
    b.sbuf("io1", [16, 128], I32, side="right")
    b.sbuf("io2", [16, 128], I32, side="right")
    b.sbuf("ioh", [16, 24], I32, side="right")
    b.sbuf("ioc", [16, 8], I32, side="right")
    b.sbuf("dall", [128, 16], F32, side="right")
    b.sbuf("phit", [128, D], BF16, side="right")
    b.sbuf("pKw", [128, NT * D], BF16, side="right")
    b.sbuf("kvscr", [128, 8 * 129], F32, side="right")
    b.sbuf("pqt", [128, 2 * D], BF16, side="right")

    W("vector", "vec")
    E("vector", lambda e: e.tensor_tensor(sb["qpbuf"][:, QP["scr"]:QP["scr"] + 8],
                                          sb["qpbuf"][:, 0:8], sb["vt"][:, VT["gq"]:VT["gq"] + 8], Alu.mult))
    E("vector", lambda e: e.tensor_scalar(sb["qpbuf"][:, QP["scr2"]:QP["scr2"] + 8],
                                          sb["vt"][:, VT["betaq"]:VT["betaq"] + 8], 0.125, None, op0=Alu.mult), inc="vec")
    SS()
    E("vector", lambda e: e.tensor_scalar(sb["qpbuf"][:, QP["scr"]:QP["scr"] + 8],
                                          sb["qpbuf"][:, QP["scr"]:QP["scr"] + 8], 1.0 / 16384.0, None, op0=Alu.mult), inc="vec")
    SS()
    E("vector", lambda e: e.tensor_tensor(sb["qpbuf"][:, QP["qp"]:QP["qp"] + 8],
                                          sb["qpbuf"][:, QP["scr"]:QP["scr"] + 8],
                                          sb["qpbuf"][:, QP["scr2"]:QP["scr2"] + 8], Alu.add), inc="vec")
    SS()
    E("vector", lambda e: e.tensor_tensor(sb["qpbuf"][:, QP["qpp"]:QP["qpp"] + 8],
                                          sb["qpbuf"][:, QP["qp"]:QP["qp"] + 8], sb["vt"][:, VT["gk"]:VT["gk"] + 8], Alu.mult))
    E("vector", lambda e: e.tensor_tensor(sb["qpbuf"][:, QP["qpbk"]:QP["qpbk"] + 8],
                                          sb["qpbuf"][:, QP["qp"]:QP["qp"] + 8], sb["vt"][:, VT["betak"]:VT["betak"] + 8], Alu.mult))
    E("vector", lambda e: e.memset(sb["qpbd"][:, :], 0.0))
    E("vector", lambda e: e.memset(sb["qpbkbd"][:, :], 0.0))
    E("vector", lambda e: e.memset(sb["onesrb"][0:1, :], 1.0), inc="vec")
    SS()
    for c in range(NC):
        E("vector", lambda e, c=c: e.tensor_copy(sb["qpbd"][0:64, c * 16 + 2 * c:c * 16 + 2 * c + 1],
                                                 sb["qpbuf"][0:64, QP["qpp"] + c:QP["qpp"] + c + 1]))
        E("vector", lambda e, c=c: e.tensor_copy(sb["qpbd"][64:128, c * 16 + 2 * c + 1:c * 16 + 2 * c + 2],
                                                 sb["qpbuf"][64:128, QP["qpp"] + c:QP["qpp"] + c + 1]))
        E("vector", lambda e, c=c: e.tensor_copy(sb["qpbkbd"][0:64, c * 16 + 2 * c:c * 16 + 2 * c + 1],
                                                 sb["qpbuf"][0:64, QP["qpbk"] + c:QP["qpbk"] + c + 1]))
        E("vector", lambda e, c=c: e.tensor_copy(sb["qpbkbd"][64:128, c * 16 + 2 * c + 1:c * 16 + 2 * c + 2],
                                                 sb["qpbuf"][64:128, QP["qpbk"] + c:QP["qpbk"] + c + 1]), inc="vec")

    qpbd_ready = p.val("vec")
    LB = [ps["pE"], ps["pF"], ps["pG"], ps["pH"]]
    W("tensor", "vec", qpbd_ready)
    W("tensor", "scl")
    for nt in range(4):
        for c in range(NC):
            E("tensor", lambda e, nt=nt, c=c: e.matmul(
                LB[nt][0:16, :], sb["qpbd"][:, c * 16:(c + 1) * 16],
                sb["Kn_fm"][:, c * L + nt * 512:c * L + (nt + 1) * 512],
                start=(c == 0), stop=(c == NC - 1)), inc="ten" if c == NC - 1 else None)
    for c in range(NC):
        E("tensor", lambda e, c=c: e.matmul(
            ps["pA"][0:16, 0:1], sb["qpbkbd"][:, c * 16:(c + 1) * 16], sb["onesc"][:, 0:1],
            start=(c == 0), stop=(c == NC - 1)), inc="ten" if c == NC - 1 else None)
    W("vector", "ten")
    for nt in range(4):
        E("vector", lambda e, nt=nt: e.tensor_copy(sb["lg"][0:16, 1 + nt * 512:1 + (nt + 1) * 512], LB[nt][0:16, :]))
    E("vector", lambda e: e.tensor_scalar(sb["lg"][0:16, 0:1], ps["pA"][0:16, 0:1], -1.0, None, op0=Alu.mult), inc="vec")
    SS()
    E("vector", lambda e: e.tensor_reduce(sb["smx"][0:16, 0:1], sb["lg"][0:16, 0:2049], axis=AX.X, op=Alu.max), inc="vec")
    SS()
    E("vector", lambda e: e.tensor_scalar(sb["smx"][0:16, 1:2], sb["smx"][0:16, 0:1], -1.0, None, op0=Alu.mult), inc="vec")
    W("scalar", "vec")
    E("scalar", lambda e: e.activation(out=sb["lg"][0:16, 0:2049], in_=sb["lg"][0:16, 0:2049],
                                       func=AF.Exp, bias=sb["smx"][0:16, 1:2]), inc="scl")
    W("vector", "scl")
    E("vector", lambda e: e.tensor_reduce(sb["smx"][0:16, 2:3], sb["lg"][0:16, 0:2049], axis=AX.X, op=Alu.add), inc="vec")
    SS()
    E("vector", lambda e: e.reciprocal(sb["smx"][0:16, 3:4], sb["smx"][0:16, 2:3]), inc="vec")
    W("scalar", "vec")
    E("scalar", lambda e: e.activation(out=sb["lg"][0:16, 0:2049], in_=sb["lg"][0:16, 0:2049],
                                       func=AF.Identity, scale=sb["smx"][0:16, 3:4]), inc="scl")
    score_done = p.val("scl")
    W("tensor", "scl", score_done)
    for c in range(NT):
        pk = ps["pG"] if c % 2 == 0 else ps["pH"]
        W("tensor", "vec")
        E("tensor", lambda e, c=c, pk=pk: e.transpose(pk[:, 0:16], sb["lg"][0:16, 1 + c * 128:1 + (c + 1) * 128],
                                                      sb["ident"][0:16, 0:16]), inc="ten")
        W("vector", "ten")
        E("vector", lambda e, c=c, pk=pk: e.tensor_copy(sb["score_t"][:, c * 16:(c + 1) * 16], pk[:, 0:16]), inc="vec")

    E("scalar", lambda e: e.activation(out=sb["qpbuf"][:, QP["tk0"]:QP["tk0"] + 8],
                                       in_=sb["vt"][:, VT["bfk"]:VT["bfk"] + 8], func=AF.Tanh), inc="scl")
    E("gpsimd", lambda e: e.iota(sb["io1"][0:16, 0:128], pattern=[[0, 128]], base=0, channel_multiplier=1))
    E("gpsimd", lambda e: e.iota(sb["io2"][0:16, 0:128], pattern=[[1, 2], [0, 64]], base=0, channel_multiplier=0))
    E("gpsimd", lambda e: e.iota(sb["ioh"][0:16, 0:8], pattern=[[0, 8]], base=0, channel_multiplier=1))
    E("gpsimd", lambda e: e.iota(sb["ioc"][0:16, 0:8], pattern=[[1, 8]], base=0, channel_multiplier=0), inc="qdma")
    W("vector", "qdma")
    E("vector", lambda e: e.tensor_scalar(sb["io1"][0:16, :], sb["io1"][0:16, :], 1, None, op0=Alu.bitwise_and),
      inc="vec")
    SS()
    E("vector", lambda e: e.tensor_tensor(sb["E1"][0:16, :], sb["io1"][0:16, :], sb["io2"][0:16, :], Alu.is_equal))
    E("vector", lambda e: e.tensor_scalar(sb["ioh"][0:16, 8:16], sb["ioh"][0:16, 0:8], 1, None,
                                          op0=Alu.bitwise_and), inc="vec")
    SS()
    E("vector", lambda e: e.tensor_tensor(sb["ioh"][0:16, 16:24], sb["ioh"][0:16, 0:8], sb["ioh"][0:16, 8:16],
                                          Alu.subtract), inc="vec")
    SS()
    E("vector", lambda e: e.tensor_scalar(sb["ch1"][0:16, :], sb["ioh"][0:16, 16:24], 0.5, None, op0=Alu.mult),
      inc="vec")
    SS()
    E("vector", lambda e: e.tensor_tensor(sb["ch1"][0:16, :], sb["ch1"][0:16, :], sb["ioc"][0:16, 0:8],
                                          Alu.is_equal), inc="vec")
    SS()
    W("vector", "scl", score_done)
    E("vector", lambda e: e.tensor_tensor(sb["s0rhs"][0:16, :], sb["ch1"][0:16, :],
                                          sb["lg"][0:16, 0:1].broadcast_to([16, 8]), Alu.mult), inc="vec")
    W("tensor", "vec")
    E("tensor", lambda e: e.matmul(ps["pA"][:, 0:8], sb["E1"][0:16, :], sb["s0rhs"][0:16, :],
                                   start=True, stop=True), inc="ten")
    W("vector", "ten")
    E("vector", lambda e: e.tensor_copy(sb["s0f"][:, 0:8], ps["pA"][:, 0:8]), inc="vec")

    def wf_load(wname, bname, vcol, gcol):
        W("gpsimd", "ten")
        for k in range(NC):
            E("gpsimd", lambda e, k=k, wname=wname: e.dma_start(
                out=sb["Wfbuf"][:, k * D:(k + 1) * D], in_=g[wname][k * 128:(k + 1) * 128, :]),
              inc="gdma", inc_n=16)
        E("sync", lambda e, bname=bname: e.dma_start(out=sb["rows2"][0:1, D:2 * D], in_=g[bname][0:1, :]),
          inc="sdma", inc_n=16)
        W("tensor", "gdma")
        W("tensor", "vec")
        for n in range(2):
            for c in range(NC):
                E("tensor", lambda e, n=n, c=c, vcol=vcol: e.matmul(
                    [ps["pA"], ps["pB"]][n][0:1, :], sb["vtb"][:, vcol + c:vcol + c + 1],
                    sb["Wfbuf"][:, c * D + n * 512:c * D + (n + 1) * 512],
                    start=(c == 0), stop=(c == NC - 1)), inc="ten" if c == NC - 1 else None)
        W("vector", "ten")
        W("vector", "sdma")
        for n in range(2):
            E("vector", lambda e, n=n: e.tensor_tensor(
                sb["rows2"][0:1, n * 512:(n + 1) * 512], [ps["pA"], ps["pB"]][n][0:1, :],
                sb["rows2"][0:1, D + n * 512:D + (n + 1) * 512], Alu.add), inc="vec")
        SS()
        E("vector", lambda e: e.tensor_copy(sb["bfb"][0:1, 0:D], sb["rows2"][0:1, 0:D]), inc="vec")
        bfb_done = p.val("vec")
        W("scalar", "ten")
        for k in range(NC):
            E("scalar", lambda e, k=k, gcol=gcol: e.activation(
                out=sb["Wfbuf"][:, k * D:(k + 1) * D], in_=sb["Wfbuf"][:, k * D:(k + 1) * D],
                func=AF.Identity, scale=sb["vt"][:, gcol + k:gcol + k + 1]), inc="scl")
        return bfb_done, p.val("scl")

    E("vector", lambda e: e.tensor_copy(sb["vtb"][:, 0:8], sb["vt"][:, VT["betaq"]:VT["betaq"] + 8]))
    E("vector", lambda e: e.tensor_copy(sb["vtb"][:, 8:16], sb["vt"][:, VT["betak"]:VT["betak"] + 8]), inc="vec")

    bfb_done, wf_scaled = wf_load("Wfk", "bfk", 8, VT["gk"])
    PP = [(ps["pA"], ps["pB"]), (ps["pC"], ps["pD"])]
    tanh_done = {}
    for m in range(NT):
        pp = PP[m % 2]
        W("tensor", "scl", wf_scaled)
        W("tensor", "vec", bfb_done)
        if m >= 2:
            W("tensor", "scl", tanh_done[m - 2])
        for k in range(NC):
            for n in range(2):
                E("tensor", lambda e, m=m, k=k, n=n, pp=pp: e.matmul(
                    pp[n][:, :], sb["Kn_fm"][:, k * L + m * 128:k * L + (m + 1) * 128],
                    sb["Wfbuf"][:, k * D + n * 512:k * D + (n + 1) * 512],
                    start=(k == 0), stop=False))
        for n in range(2):
            E("tensor", lambda e, n=n, pp=pp: e.matmul(
                pp[n][:, :], sb["onesrb"][0:1, 0:128], sb["bfb"][0:1, n * 512:(n + 1) * 512],
                start=False, stop=True), inc="ten")
        W("scalar", "ten")
        W("scalar", "vec")
        for n in range(2):
            E("scalar", lambda e, n=n, pp=pp: e.activation(
                out=sb["prescr"][:, n * 512:(n + 1) * 512], in_=pp[n][:, :], func=AF.Tanh), inc="scl")
        tanh_done[m] = p.val("scl")
        W("vector", "scl")
        for h in range(16):
            E("vector", lambda e, m=m, h=h: e.scalar_tensor_tensor(
                out=sb["pKw"][:, m * D + h * 64:m * D + (h + 1) * 64],
                in0=sb["prescr"][:, h * 64:(h + 1) * 64], scalar=1.0,
                in1=sb["score_t"][:, m * 16 + h:m * 16 + h + 1].broadcast_to([128, 64]),
                op0=Alu.add, op1=Alu.mult), inc="vec" if h == 15 else None)
    pkw_all = p.val("vec")

    W("tensor", "vec", pkw_all)
    kvdone = {}
    for hp in range(8):
        bank = ps["pE"] if hp % 2 == 0 else ps["pF"]
        if hp >= 2:
            W("tensor", "vec", kvdone[hp - 2])
        for m in range(NT):
            E("tensor", lambda e, m=m, hp=hp, bank=bank: e.matmul(
                bank[:, 0:129], sb["pKw"][:, m * D + hp * 128:m * D + (hp + 1) * 128],
                sb["VfT"][:, m * 1032 + hp * 129:m * 1032 + (hp + 1) * 129],
                start=(m == 0), stop=(m == NT - 1)), inc="ten" if m == NT - 1 else None)
        W("vector", "ten")
        E("vector", lambda e, hp=hp, bank=bank: e.tensor_copy(sb["kvscr"][:, hp * 129:(hp + 1) * 129],
                                                              bank[:, 0:129]),
          inc="vec")
        kvdone[hp] = p.val("vec")
    W("vector", "vec")
    E("vector", lambda e: e.tensor_tensor(sb["qpbuf"][:, QP["ks0"]:QP["ks0"] + 8],
                                          sb["qpbuf"][:, QP["tk0"]:QP["tk0"] + 8], sb["s0f"][:, 0:8], Alu.mult),
      inc="vec")
    SS()
    E("vector", lambda e: e.tensor_tensor(
        sb["KsumF"][:, 0:8],
        sb["kvscr"][:, :].rearrange("p (h c) -> p h c", c=129)[:, :, 128],
        sb["qpbuf"][:, QP["ks0"]:QP["ks0"] + 8], Alu.add), inc="vec")
    SS()
    E("vector", lambda e: e.tensor_tensor(sb["KsumF"][:, 0:8], sb["KsumF"][:, 0:8], sb["s0f"][:, 0:8], Alu.add),
      inc="vec")
    E("vector", lambda e: e.memset(sb["KVd"][:, :], 0.0), inc="vec")
    SS()
    for hp in range(8):
        E("vector", lambda e, hp=hp: e.tensor_copy(
            sb["KVd"][0:64, hp * 130:hp * 130 + 64], sb["kvscr"][0:64, hp * 129:hp * 129 + 64]))
        E("vector", lambda e, hp=hp: e.tensor_copy(
            sb["KVd"][64:128, hp * 130 + 64:hp * 130 + 128], sb["kvscr"][64:128, hp * 129 + 64:hp * 129 + 128]))
        E("vector", lambda e, hp=hp: e.tensor_copy(
            sb["KVd"][0:64, hp * 130 + 128:hp * 130 + 129], sb["KsumF"][0:64, hp:hp + 1]))
        E("vector", lambda e, hp=hp: e.tensor_copy(
            sb["KVd"][64:128, hp * 130 + 129:hp * 130 + 130], sb["KsumF"][64:128, hp:hp + 1]),
          inc="vec" if hp == 7 else None)
        kvd_ready = p.val("vec")
    if A3_STOP == "kvd":
        return

    bfb_done, wf_scaled = wf_load("Wfq", "bfq", 0, VT["gq"])
    tev_a3 = {}
    mix_dma_done = {}
    evac_done2 = {}
    tanh_done = {}
    num_used = {}
    for m in range(NT):
        pp = PP[m % 2]
        W("tensor", "scl", wf_scaled)
        W("tensor", "vec", max(bfb_done, kvd_ready))
        if m >= 1:
            W("tensor", "vec", evac_done2[m - 1])
        if m >= 2:
            W("tensor", "scl", tanh_done[m - 2])
        for k in range(NC):
            for n in range(2):
                E("tensor", lambda e, m=m, k=k, n=n, pp=pp: e.matmul(
                    pp[n][:, :], sb["Qn_fm"][:, k * L + m * 128:k * L + (m + 1) * 128],
                    sb["Wfbuf"][:, k * D + n * 512:k * D + (n + 1) * 512],
                    start=(k == 0), stop=False))
        for n in range(2):
            E("tensor", lambda e, n=n, pp=pp: e.matmul(
                pp[n][:, :], sb["onesrb"][0:1, 0:128], sb["bfb"][0:1, n * 512:(n + 1) * 512],
                start=False, stop=True), inc="ten")
        W("scalar", "ten")
        for n in range(2):
            E("scalar", lambda e, n=n, pp=pp: e.activation(
                out=sb["phit"][:, n * 512:(n + 1) * 512], in_=pp[n][:, :], func=AF.Tanh), inc="scl")
        tanh_done[m] = p.val("scl")
        slot = m % 2
        W("tensor", "scl", tanh_done[m])
        if m >= 2:
            W("tensor", "ten", num_used[m - 2])
        for k in range(NC):
            pk = ps["pG"] if k % 2 == 0 else ps["pH"]
            gi3 = m * NC + k
            if gi3 < 2:
                W("tensor", "vec")
            if gi3 >= 2:
                W("tensor", "scl", tev_a3.get(gi3 - 2, 0))
            E("tensor", lambda e, k=k, pk=pk: e.transpose(
                pk[:, 0:64].bitcast(BF16), sb["phit"][:, k * 128:(k + 1) * 128], sb["identb"][:, :]), inc="ten")
            W("scalar", "ten")
            E("scalar", lambda e, k=k, pk=pk, slot=slot: e.activation(
                out=sb["pqt"][:, slot * D + k * 128:slot * D + (k + 1) * 128], in_=pk[:, 0:64].bitcast(BF16),
                func=AF.Identity, bias=sb["onef"][:, 0:1]), inc="scl")
            tev_a3[gi3] = p.val("scl")
        pqt_done = p.val("scl")
        W("tensor", "scl", pqt_done)
        NSLOT = [(ps["pC"], 0), (ps["pC"], 130), (ps["pC"], 260), (ps["pD"], 0), (ps["pD"], 130),
                 (ps["pD"], 260), (ps["pE"], 0), (ps["pE"], 130)]
        for hp in range(8):
            bank, off = NSLOT[hp]
            E("tensor", lambda e, hp=hp, bank=bank, off=off, slot=slot: e.matmul(
                bank[:, off:off + 130], sb["pqt"][:, slot * D + hp * 128:slot * D + (hp + 1) * 128],
                sb["KVd"][:, hp * 130:(hp + 1) * 130], start=True, stop=True),
              inc="ten" if hp == 7 else None)
        num_used[m] = p.val("ten")
        W("vector", "ten")
        if m >= 2:
            W("vector", "odma", mix_dma_done[m - 2])
        mix = sb["prescr"][:, slot * D:(slot + 1) * D]
        for hp in range(8):
            bank, off = NSLOT[hp]
            E("vector", lambda e, hp=hp, bank=bank, off=off: e.tensor_scalar(
                sb["dall"][:, 2 * hp:2 * hp + 2], bank[:, off + 128:off + 130], 1e-6, None, op0=Alu.add),
              inc="vec" if hp == 7 else None)
        SS()
        E("vector", lambda e: e.reciprocal(sb["dall"][:, :], sb["dall"][:, :]), inc="vec")
        SS()
        for hp in range(8):
            bank, off = NSLOT[hp]
            for j in range(2):
                E("vector", lambda e, hp=hp, j=j, bank=bank, off=off, mix=mix: e.tensor_tensor(
                    mix[:, hp * 128 + j * 64:hp * 128 + (j + 1) * 64], bank[:, off + j * 64:off + (j + 1) * 64],
                    sb["dall"][:, 2 * hp + j:2 * hp + j + 1].broadcast_to([128, 64]), Alu.mult),
                  inc="vec" if (hp == 7 and j == 1) else None)
        evac_done2[m] = p.val("vec")
        W("scalar", "vec")
        E("scalar", lambda e, m=m, mix=mix: e.dma_start(out=g["mixer_dram"][m * 128:(m + 1) * 128, :], in_=mix),
          inc="odma", inc_n=16)
        mix_dma_done[m] = p.val("odma")
    W("vector", "odma")
    b.sfree("pqt", "kvscr", "pKw", "phit", "dall", "ioc", "ioh", "io2", "io1", "smx", "s0rhs", "ch1", "E1",
            "bfb", "onesrb", "vtb", "s0f", "KsumF", "KVd", "score_t", "qpbkbd", "qpbd", "lg", "rows2", "Wfbuf")
    b.sfree("Qn_fm", "Kn_fm", "VfT")




def stage_a4(b, dev_y=True):
    p, sb, g, ps = b.p, b.sb, b.g, b.ps
    E, W = p.emit, p.wait

    def SS(eng="vector", sem="vec"):
        W(eng, sem)

    b.sbuf("W1s", [128, NC * MLPD], BF16)
    b.sbuf("W2s", [128, 32 * D], BF16)
    b.sbuf("gabat", [128, 2 * D], F32, side="right")
    b.sbuf("aT", [128, 2 * D], F32, side="right")
    b.sbuf("a_fm", [128, NC * 256], BF16, side="right")
    b.sbuf("xbuf", [128, 2 * D], F32, side="right")
    b.sbuf("hscr", [128, D], F32, side="right")
    b.sbuf("yscr", [128, D], F32, side="right")
    b.sbuf("ybf", [128, D], BF16, side="right")
    b.sbuf("yfm", [128, NC * 128], F32, side="right")
    b.sbuf("sTt", [128, 8], F32, side="right")
    b.sbuf("wgs", [128, NC * 8], F32, side="right")
    b.sbuf("b2b", [1, D], BF16, side="right")
    b.sbuf("gslice", [128, 256], BF16, side="right")
    b.sbuf("smg", [128, 4], F32, side="right")
    b.sbuf("rowstage3", [1, D], F32, side="right")
    b.sbuf("onesrb4", [1, 128], BF16, side="right")

    p.fence(("gpsimd", "sync"))
    for k in range(NC):
        E("gpsimd", lambda e, k=k: e.dma_start(out=sb["W1s"][:, k * MLPD:(k + 1) * MLPD],
                                               in_=g["W1"][k * 128:(k + 1) * 128, :]), inc="gdma", inc_n=16)
    for k in range(32):
        E("gpsimd", lambda e, k=k: e.dma_start(out=sb["W2s"][:, k * D:(k + 1) * D],
                                               in_=g["W2"][k * 128:(k + 1) * 128, :]), inc="gdma", inc_n=16)
    for k in range(NC):
        E("sync", lambda e, k=k: e.dma_start(out=sb["wgs"][:, k * 8:(k + 1) * 8],
                                             in_=g["wg"][k * 128:(k + 1) * 128, :]), inc="sdma", inc_n=16)
    W("vector", "vec")
    E("vector", lambda e: e.memset(sb["onesrb4"][0:1, :], 1.0))
    E("vector", lambda e: e.tensor_copy(sb["b2b"][0:1, :], sb["rows"][0:1, 0:D]), inc="vec")
    for i, nm in enumerate(("ga", "ba")):
        W("sync", "ten")
        E("sync", lambda e, nm=nm: e.dma_start(out=sb["rowstage3"][0:1, :], in_=g[nm][0:1, :]), inc="sdma", inc_n=16)
        W("tensor", "sdma")
        for n in range(2):
            E("tensor", lambda e, n=n: e.matmul(ps["pG"][:, :], sb["onesr"][0:1, 0:128],
                                                sb["rowstage3"][0:1, n * 512:(n + 1) * 512],
                                                start=True, stop=True), inc="ten")
            W("vector", "ten")
            E("vector", lambda e, i=i, n=n: e.tensor_copy(
                sb["gabat"][:, i * D + n * 512:i * D + (n + 1) * 512], ps["pG"][:, :]), inc="vec")
            W("tensor", "vec")
    W("tensor", "gdma")
    W("sync", "odma")

    NCH = L // 256
    tev_a4 = {}
    gelu_done = {}
    ydma_done = {}
    mixld_done = {}
    afm_done = {}
    evh_done = {}
    for tc in range(NCH):
        for t2 in range(2):
            ti = tc * 2 + t2
            slot = t2
            pre = sb["prescr"][:, slot * D:(slot + 1) * D]
            W("sync", "vec")
            W("sync", "scl")
            E("sync", lambda e, ti=ti, pre=pre: e.dma_start(out=pre, in_=g["mixer_dram"][ti * 128:(ti + 1) * 128, :]),
              inc="sdma", inc_n=16)
            E("sync", lambda e, ti=ti, slot=slot: e.dma_start(out=sb["xbuf"][:, slot * D:(slot + 1) * D],
                                                              in_=g["x"][ti * 128:(ti + 1) * 128, :]),
              inc="sdma", inc_n=16)
            W("vector", "sdma")
            ln_tile(b, pre, sb["hscr"][:, :], slot)
            W("vector", "scl")
            E("vector", lambda e: e.tensor_tensor(sb["hscr"][:, :], sb["hscr"][:, :],
                                                  sb["gabat"][:, 0:D], Alu.mult), inc="vec")
            SS()
            E("vector", lambda e, slot=slot: e.tensor_tensor(sb["aT"][:, slot * D:(slot + 1) * D],
                                                             sb["hscr"][:, :], sb["gabat"][:, D:2 * D],
                                                             Alu.add), inc="vec")
            at_done = p.val("vec")
            W("tensor", "vec", at_done)
            for k in range(NC):
                pk = ps["pG"] if k % 2 == 0 else ps["pH"]
                gi4 = ti * NC + k
                W("tensor", "vec")
                if gi4 >= 2:
                    W("tensor", "scl", tev_a4.get(gi4 - 2, 0))
                E("tensor", lambda e, k=k, pk=pk, slot=slot: e.transpose(
                    pk[:, 0:128], sb["aT"][:, slot * D + k * 128:slot * D + (k + 1) * 128],
                    sb["ident"][:, :]), inc="ten")
                W("scalar", "ten")
                E("scalar", lambda e, k=k, pk=pk, t2=t2: e.activation(
                    out=sb["a_fm"][:, k * 256 + t2 * 128:k * 256 + (t2 + 1) * 128], in_=pk[:, 0:128],
                    func=AF.Identity), inc="scl")
                tev_a4[gi4] = p.val("scl")
            W("vector", "ten")
            E("vector", lambda e, slot=slot: e.scalar_tensor_tensor(
                out=sb["aT"][:, slot * D:(slot + 1) * D], in0=sb["xbuf"][:, slot * D:(slot + 1) * D],
                scalar=2.0, in1=sb["aT"][:, slot * D:(slot + 1) * D], op0=Alu.mult, op1=Alu.add), inc="vec")
        afm_done[tc] = p.val("vec")

        W("tensor", "vec", afm_done[tc])
        W("tensor", "scl")
        W2B = [ps["pC"], ps["pD"], ps["pE"], ps["pF"]]
        for hc in range(32):
            pw = ps["pA"] if hc % 2 == 0 else ps["pB"]
            gi = tc * 32 + hc
            if gi >= 2:
                W("tensor", "scl", gelu_done[gi - 2])
            for k in range(NC):
                E("tensor", lambda e, hc=hc, k=k, pw=pw: e.matmul(
                    pw[:, 0:256], sb["W1s"][:, k * MLPD + hc * 128:k * MLPD + (hc + 1) * 128],
                    sb["a_fm"][:, k * 256:(k + 1) * 256], start=(k == 0), stop=(k == NC - 1)),
                  inc="ten" if k == NC - 1 else None)
            W("scalar", "ten")
            E("scalar", lambda e, hc=hc, pw=pw: e.activation(
                out=sb["gslice"][:, :], in_=pw[:, 0:256], func=AF.Gelu,
                bias=sb["b1t"][:, hc:hc + 1]), inc="scl")
            gelu_done[gi] = p.val("scl")
            W("tensor", "scl", gelu_done[gi])
            for ms in range(2):
                for n in range(2):
                    E("tensor", lambda e, hc=hc, ms=ms, n=n: e.matmul(
                        W2B[ms * 2 + n][:, :], sb["gslice"][:, ms * 128:(ms + 1) * 128],
                        sb["W2s"][:, hc * D + n * 512:hc * D + (n + 1) * 512],
                        start=(hc == 0), stop=False))
        for ms in range(2):
            for n in range(2):
                E("tensor", lambda e, ms=ms, n=n: e.matmul(
                    W2B[ms * 2 + n][:, :], sb["onesrb4"][0:1, 0:128], sb["b2b"][0:1, n * 512:(n + 1) * 512],
                    start=False, stop=True), inc="ten" if (ms == 1 and n == 1) else None)
        W("vector", "ten")
        for t2 in range(2):
            ti = tc * 2 + t2
            slot = t2
            W("vector", "scl")
            for n in range(2):
                E("vector", lambda e, t2=t2, n=n, slot=slot: e.tensor_tensor(
                    sb["hscr"][:, n * 512:(n + 1) * 512], W2B[t2 * 2 + n][:, :],
                    sb["aT"][:, slot * D + n * 512:slot * D + (n + 1) * 512], Alu.add),
                  inc="vec" if n == 1 else None)
            SS()
            W("scalar", "ten")
            if ti >= 1:
                W("scalar", "odma", ydma_done[ti - 1])
                W("vector", "odma", ydma_done[ti - 1])
            ln_tile(b, sb["hscr"][:, :], sb["yscr"][:, :], slot)
            W("vector", "scl")
            W("scalar", "scl")
            if dev_y:
                E("scalar", lambda e, ti=ti: e.dma_start(out=g["y_out"][ti * 128:(ti + 1) * 128, :],
                                                         in_=sb["yscr"][:, :]), inc="odma", inc_n=16)
            E("scalar", lambda e, ti=ti: e.dma_start(out=g["y_dram"][ti * 128:(ti + 1) * 128, :],
                                                     in_=sb["yscr"][:, :]), inc="odma", inc_n=16)
            E("vector", lambda e: e.tensor_copy(sb["ybf"][:, :], sb["yscr"][:, :]), inc="vec")
            W("scalar", "vec")
            E("scalar", lambda e, ti=ti: e.dma_start(out=g["agy_dram"][ti * 128:(ti + 1) * 128, :],
                                                     in_=sb["ybf"][:, :]), inc="odma", inc_n=16)
            for k in range(NC):
                pk = ps["pG"] if k % 2 == 0 else ps["pH"]
                W("tensor", "vec")
                W("tensor", "scl")
                E("tensor", lambda e, k=k, pk=pk: e.transpose(
                    pk[:, 0:128], sb["yscr"][:, k * 128:(k + 1) * 128], sb["ident"][:, :]), inc="ten")
                W("vector", "ten")
                E("vector", lambda e, k=k, pk=pk: e.tensor_copy(
                    sb["yfm"][:, k * 128:(k + 1) * 128], pk[:, 0:128]), inc="vec")
            yfm_done = p.val("vec")
            W("tensor", "vec", yfm_done)
            for k in range(NC):
                E("tensor", lambda e, k=k: e.matmul(
                    ps["pG"][:, 0:8], sb["yfm"][:, k * 128:(k + 1) * 128], sb["wgs"][:, k * 8:(k + 1) * 8],
                    start=(k == 0), stop=(k == NC - 1)), inc="ten" if k == NC - 1 else None)
            W("vector", "ten")
            E("vector", lambda e: e.tensor_reduce(sb["smg"][:, 0:1], ps["pG"][:, 0:8], axis=AX.X, op=Alu.max),
              inc="vec")
            SS()
            E("vector", lambda e: e.tensor_scalar(sb["smg"][:, 1:2], sb["smg"][:, 0:1], -1.0, None, op0=Alu.mult),
              inc="vec")
            W("scalar", "vec")
            E("scalar", lambda e: e.activation(out=sb["sTt"][:, :], in_=ps["pG"][:, 0:8], func=AF.Exp,
                                               bias=sb["smg"][:, 1:2]), inc="scl")
            W("vector", "scl")
            E("vector", lambda e: e.tensor_reduce(sb["smg"][:, 2:3], sb["sTt"][:, :], axis=AX.X, op=Alu.add),
              inc="vec")
            SS()
            E("vector", lambda e: e.reciprocal(sb["smg"][:, 3:4], sb["smg"][:, 2:3]), inc="vec")
            W("scalar", "vec")
            E("scalar", lambda e: e.activation(out=sb["sTt"][:, :], in_=sb["sTt"][:, :], func=AF.Identity,
                                               scale=sb["smg"][:, 3:4]), inc="scl")
            W("scalar", "scl")
            if dev_y:
                E("scalar", lambda e, ti=ti: e.dma_start(out=g["s_out"][ti * 128:(ti + 1) * 128, :],
                                                         in_=sb["sTt"][:, :]), inc="odma", inc_n=16)
            W("tensor", "scl")
            E("tensor", lambda e: e.transpose(ps["pH"][0:8, 0:128], sb["sTt"][:, :], sb["ident"][:, :]), inc="ten")
            W("vector", "ten")
            E("vector", lambda e: e.tensor_copy(sb["yfm"][0:8, 0:128], ps["pH"][0:8, 0:128]), inc="vec")
            W("scalar", "vec")
            E("scalar", lambda e, ti=ti: e.dma_start(out=g["sag_dram"][ti, :, :], in_=sb["yfm"][0:8, 0:128]),
              inc="odma", inc_n=16)
            ydma_done[ti] = p.val("odma")
    W("vector", "odma")
    W("scalar", "odma")
    b.sfree("onesrb4", "rowstage3", "smg", "gslice", "b2b", "wgs", "sTt", "yfm", "ybf", "yscr", "hscr",
            "xbuf", "a_fm", "aT", "gabat")
    b.sfree("W2s", "W1s")


def stage_r1(b):
    p, sb, g, ps = b.p, b.sb, b.g, b.ps
    E, W = p.emit, p.wait

    def SS(eng="vector", sem="vec"):
        W(eng, sem)

    b.dram_tmp("S2_dram", [128, NE, 128], shared=True)
    b.dram_tmp("Yfull_dram", [NE * L, D], BF16, shared=True)
    b.sbuf("S2", [128, NE * 128], F32)
    b.sbuf("selres", [8, 8], F32)
    b.sbuf("selmask", [128, NE * 128], F32)
    b.sbuf("sint", [8, 16], I32, side="right")
    b.sbuf("sbyte", [8, 4], F32, side="right")
    b.sbuf("ones4", [4, 128], F32, side="right")
    b.sbuf("cmpb", [128, NE * 128], F32, side="right")
    b.sbuf("red", [128, 32], F32, side="right")
    b.sbuf("tbits", [128, 8], I32, side="right")
    b.sbuf("tbscr", [128, 4 * 8], I32, side="right")
    b.sbuf("onescF", [128, 1], F32, side="right")
    b.sbuf("cntf", [8, 4], F32, side="right")
    b.sbuf("tvalf", [128, 128], F32, side="right")
    b.sbuf("eqm", [128, NE * 128], F32, side="right")
    b.sbuf("iot", [128, 128], I32, side="right")
    b.sbuf("mflt", [128, 8], F32, side="right")
    b.sbuf("xsp", [8, 8], F32, side="right")
    b.p.new_sem("cc")

    W("gpsimd", "odma")
    p.fence(("gpsimd", "sync"))
    E("gpsimd", lambda e: e.collective_compute(
        "AllGather", Alu.bypass, replica_groups=[list(range(8))],
        ins=[g["sag_dram"].opt()], outs=[g["S2_dram"].opt()]), inc="cc")
    E("gpsimd", lambda e: e.collective_compute(
        "AllGather", Alu.bypass, replica_groups=[list(range(8))],
        ins=[g["agy_dram"].opt()], outs=[g["Yfull_dram"].opt()]), inc="cc")
    W("sync", "cc", 1)
    E("sync", lambda e: e.dma_start(out=sb["S2"][:, :], in_=g["S2_dram"][:, :, :]), inc="sdma", inc_n=16)
    W("vector", "sdma")
    E("vector", lambda e: e.memset(sb["onescF"][:, :], 1.0))
    E("vector", lambda e: e.memset(sb["ones4"][:, :], 1.0))
    E("gpsimd", lambda e: e.iota(sb["iot"][:, :], pattern=[[1, 128]], base=0, channel_multiplier=128), inc="qdma")
    W("vector", "qdma")
    E("vector", lambda e: e.tensor_copy(sb["tvalf"][:, :], sb["iot"][:, :]), inc="vec")

    def bcast_exact(src_col):
        for j in range(4):
            E("vector", lambda e, j=j: e.tensor_scalar(
                sb["sint"][0:8, 8 + j:9 + j], sb["sint"][0:8, src_col:src_col + 1], 8 * j, None,
                op0=Alu.logical_shift_right), inc="vec" if j == 3 else None)
        SS()
        for j in range(4):
            E("vector", lambda e, j=j: e.tensor_scalar(
                sb["sint"][0:8, 8 + j:9 + j], sb["sint"][0:8, 8 + j:9 + j], 255, None,
                op0=Alu.bitwise_and), inc="vec" if j == 3 else None)
        SS()
        E("vector", lambda e: e.tensor_copy(sb["sbyte"][0:8, 0:4], sb["sint"][0:8, 8:12]), inc="vec")
        W("tensor", "vec")
        for j in range(4):
            E("tensor", lambda e, j=j: e.transpose(ps["pH"][0:1, 40 + j * 8:40 + (j + 1) * 8],
                                                   sb["sbyte"][0:8, j:j + 1], sb["ident"][0:8, 0:8]),
              inc="ten" if j == 3 else None)
        W("vector", "ten")
        E("vector", lambda e: e.tensor_copy(sb["red"][0:1, 0:32],
                                            ps["pH"][0:1, 40:72]), inc="vec")
        W("tensor", "vec")
        for j in range(4):
            E("tensor", lambda e, j=j: e.matmul(ps["pH"][:, j * 8:(j + 1) * 8], sb["onesr"][0:1, 0:128],
                                                sb["red"][0:1, j * 8:(j + 1) * 8], start=True, stop=True),
              inc="ten" if j == 3 else None)
        W("vector", "ten")
        for j in range(4):
            E("vector", lambda e, j=j: e.tensor_copy(sb["tbscr"][:, j * 8:(j + 1) * 8],
                                                     ps["pH"][:, j * 8:(j + 1) * 8]),
              inc="vec" if j == 3 else None)
        SS()
        for j in range(1, 4):
            E("vector", lambda e, j=j: e.tensor_scalar(
                sb["tbscr"][:, j * 8:(j + 1) * 8], sb["tbscr"][:, j * 8:(j + 1) * 8], 8 * j, None,
                op0=Alu.logical_shift_left), inc="vec" if j == 3 else None)
        SS()
        E("vector", lambda e: e.tensor_tensor(sb["tbits"][:, :], sb["tbscr"][:, 0:8], sb["tbscr"][:, 8:16],
                                              Alu.bitwise_or), inc="vec")
        SS()
        E("vector", lambda e: e.tensor_tensor(sb["tbits"][:, :], sb["tbits"][:, :], sb["tbscr"][:, 16:24],
                                              Alu.bitwise_or), inc="vec")
        SS()
        E("vector", lambda e: e.tensor_tensor(sb["tbits"][:, :], sb["tbits"][:, :], sb["tbscr"][:, 24:32],
                                              Alu.bitwise_or), inc="vec")
        SS()

    def count_gt(out_col):
        for e_ in range(NE):
            E("vector", lambda e, e_=e_: e.tensor_scalar(
                sb["cmpb"][:, e_ * 128:(e_ + 1) * 128], sb["S2"][:, e_ * 128:(e_ + 1) * 128],
                sb["tbits"].bitcast(F32)[:, e_:e_ + 1], None, op0=Alu.is_gt),
              inc="vec" if e_ == NE - 1 else None)
        SS()
        E("vector", lambda e: e.tensor_reduce(sb["red"][:, 0:8],
                                              sb["cmpb"][:, :].rearrange("p (e a) -> p e a", a=128),
                                              axis=AX.X, op=Alu.add), inc="vec")
        W("tensor", "vec")
        E("tensor", lambda e: e.matmul(ps["pH"][0:8, 32:33], sb["red"][:, 0:8], sb["onescF"][:, :],
                                       start=True, stop=True), inc="ten")
        W("vector", "ten")
        E("vector", lambda e, out_col=out_col: e.tensor_copy(sb["cntf"][0:8, out_col:out_col + 1],
                                                             ps["pH"][0:8, 32:33]), inc="vec")
        SS()

    E("vector", lambda e: e.memset(sb["xsp"][0:8, 0:8], 0.0), inc="vec")
    SS()
    for bit in range(30, -1, -1):
        hinc = float(1 << (bit - 16)) if bit >= 16 else 0.0
        linc = float(1 << bit) if bit < 16 else 0.0
        E("vector", lambda e, hinc=hinc: e.tensor_scalar(sb["xsp"][0:8, 2:3], sb["xsp"][0:8, 0:1],
                                                         hinc, None, op0=Alu.add))
        E("vector", lambda e, linc=linc: e.tensor_scalar(sb["xsp"][0:8, 3:4], sb["xsp"][0:8, 1:2],
                                                         linc, None, op0=Alu.add), inc="vec")
        SS()
        E("vector", lambda e: e.tensor_copy(sb["sint"][0:8, 2:3], sb["xsp"][0:8, 2:3]))
        E("vector", lambda e: e.tensor_copy(sb["sint"][0:8, 3:4], sb["xsp"][0:8, 3:4]), inc="vec")
        SS()
        E("vector", lambda e: e.tensor_scalar(sb["sint"][0:8, 2:3], sb["sint"][0:8, 2:3], 16, None,
                                              op0=Alu.logical_shift_left), inc="vec")
        SS()
        E("vector", lambda e: e.tensor_tensor(sb["sint"][0:8, 2:3], sb["sint"][0:8, 2:3],
                                              sb["sint"][0:8, 3:4], Alu.bitwise_or), inc="vec")
        SS()
        bcast_exact(2)
        count_gt(0)
        E("vector", lambda e: e.tensor_scalar(sb["xsp"][0:8, 4:5], sb["cntf"][0:8, 0:1], 4096.0, None,
                                              op0=Alu.is_ge), inc="vec")
        SS()
        E("vector", lambda e: e.tensor_copy(sb["sint"][0:8, 9:10], sb["xsp"][0:8, 4:5]), inc="vec")
        SS()
        E("vector", lambda e: e.select(sb["xsp"][0:8, 0:1], sb["sint"][0:8, 9:10],
                                       sb["xsp"][0:8, 2:3], sb["xsp"][0:8, 0:1]))
        E("vector", lambda e: e.select(sb["xsp"][0:8, 1:2], sb["sint"][0:8, 9:10],
                                       sb["xsp"][0:8, 3:4], sb["xsp"][0:8, 1:2]), inc="vec")
        SS()
    E("vector", lambda e: e.tensor_scalar(sb["xsp"][0:8, 1:2], sb["xsp"][0:8, 1:2], 1.0, None, op0=Alu.add),
      inc="vec")
    SS()
    E("vector", lambda e: e.tensor_scalar(sb["xsp"][0:8, 4:5], sb["xsp"][0:8, 1:2], 65535.0, None,
                                          op0=Alu.is_gt), inc="vec")
    SS()
    E("vector", lambda e: e.tensor_scalar(sb["xsp"][0:8, 5:6], sb["xsp"][0:8, 4:5], -65536.0, None,
                                          op0=Alu.mult), inc="vec")
    SS()
    E("vector", lambda e: e.tensor_tensor(sb["xsp"][0:8, 1:2], sb["xsp"][0:8, 1:2],
                                          sb["xsp"][0:8, 5:6], Alu.add))
    E("vector", lambda e: e.tensor_tensor(sb["xsp"][0:8, 0:1], sb["xsp"][0:8, 0:1],
                                          sb["xsp"][0:8, 4:5], Alu.add), inc="vec")
    SS()
    E("vector", lambda e: e.tensor_copy(sb["sint"][0:8, 1:2], sb["xsp"][0:8, 0:1]))
    E("vector", lambda e: e.tensor_copy(sb["sint"][0:8, 3:4], sb["xsp"][0:8, 1:2]), inc="vec")
    SS()
    E("vector", lambda e: e.tensor_scalar(sb["sint"][0:8, 1:2], sb["sint"][0:8, 1:2], 16, None,
                                          op0=Alu.logical_shift_left), inc="vec")
    SS()
    E("vector", lambda e: e.tensor_tensor(sb["sint"][0:8, 1:2], sb["sint"][0:8, 1:2],
                                          sb["sint"][0:8, 3:4], Alu.bitwise_or), inc="vec")
    SS()
    bcast_exact(1)
    count_gt(2)
    W("vector", "vec")
    for e_ in range(NE):
        E("vector", lambda e, e_=e_: e.tensor_scalar(
            sb["eqm"][:, e_ * 128:(e_ + 1) * 128], sb["S2"][:, e_ * 128:(e_ + 1) * 128],
            sb["tbits"].bitcast(F32)[:, e_:e_ + 1], None, op0=Alu.is_equal),
          inc="vec" if e_ == NE - 1 else None)
    SS()
    E("vector", lambda e: e.tensor_copy(sb["selres"][0:8, 0:1].bitcast(I32), sb["sint"][0:8, 1:2]))
    E("vector", lambda e: e.tensor_copy(sb["selres"][0:8, 2:3], sb["cntf"][0:8, 2:3]), inc="vec")
    E("vector", lambda e: e.memset(sb["xsp"][0:8, 5:6], 0.0), inc="vec")
    SS()
    for bit in range(14, -1, -1):
        E("vector", lambda e, v=float(1 << bit): e.tensor_scalar(sb["xsp"][0:8, 6:7], sb["xsp"][0:8, 5:6],
                                                                 v, None, op0=Alu.add), inc="vec")
        SS()
        E("vector", lambda e: e.tensor_copy(sb["sint"][0:8, 6:7], sb["xsp"][0:8, 6:7]), inc="vec")
        SS()
        bcast_exact(6)
        E("vector", lambda e: e.tensor_copy(sb["mflt"][:, :], sb["tbits"][:, :]), inc="vec")
        SS()
        for e_ in range(NE):
            E("vector", lambda e, e_=e_: e.tensor_scalar(
                sb["cmpb"][:, e_ * 128:(e_ + 1) * 128], sb["tvalf"][:, :],
                sb["mflt"][:, e_:e_ + 1], None, op0=Alu.is_lt),
              inc="vec" if e_ == NE - 1 else None)
        SS()
        E("vector", lambda e: e.tensor_tensor(sb["cmpb"][:, :], sb["cmpb"][:, :], sb["eqm"][:, :],
                                              Alu.mult), inc="vec")
        SS()
        E("vector", lambda e: e.tensor_reduce(sb["red"][:, 0:8],
                                              sb["cmpb"][:, :].rearrange("p (e a) -> p e a", a=128),
                                              axis=AX.X, op=Alu.add), inc="vec")
        W("tensor", "vec")
        E("tensor", lambda e: e.matmul(ps["pH"][0:8, 32:33], sb["red"][:, 0:8], sb["onescF"][:, :],
                                       start=True, stop=True), inc="ten")
        W("vector", "ten")
        E("vector", lambda e: e.tensor_copy(sb["cntf"][0:8, 0:1], ps["pH"][0:8, 32:33]), inc="vec")
        SS()
        E("vector", lambda e: e.tensor_tensor(sb["cntf"][0:8, 1:2], sb["cntf"][0:8, 0:1],
                                              sb["cntf"][0:8, 2:3], Alu.add), inc="vec")
        SS()
        E("vector", lambda e: e.tensor_scalar(sb["xsp"][0:8, 4:5], sb["cntf"][0:8, 1:2], 4096.0, None,
                                              op0=Alu.is_lt), inc="vec")
        SS()
        E("vector", lambda e: e.tensor_copy(sb["sint"][0:8, 9:10], sb["xsp"][0:8, 4:5]), inc="vec")
        SS()
        E("vector", lambda e: e.select(sb["xsp"][0:8, 5:6], sb["sint"][0:8, 9:10],
                                       sb["xsp"][0:8, 6:7], sb["xsp"][0:8, 5:6]), inc="vec")
        SS()
    E("vector", lambda e: e.tensor_scalar(sb["xsp"][0:8, 5:6], sb["xsp"][0:8, 5:6], 1.0, None, op0=Alu.add),
      inc="vec")
    SS()
    E("vector", lambda e: e.tensor_copy(sb["selres"][0:8, 1:2], sb["xsp"][0:8, 5:6]), inc="vec")
    SS()
    E("vector", lambda e: e.tensor_copy(sb["sint"][0:8, 6:7], sb["xsp"][0:8, 5:6]), inc="vec")
    SS()
    bcast_exact(6)
    E("vector", lambda e: e.tensor_copy(sb["mflt"][:, :], sb["tbits"][:, :]), inc="vec")
    SS()
    bcast_exact(1)
    for e_ in range(NE):
        E("vector", lambda e, e_=e_: e.tensor_scalar(
            sb["cmpb"][:, e_ * 128:(e_ + 1) * 128], sb["tvalf"][:, :],
            sb["mflt"][:, e_:e_ + 1], None, op0=Alu.is_lt), inc="vec" if e_ == NE - 1 else None)
    SS()
    E("vector", lambda e: e.tensor_tensor(sb["cmpb"][:, :], sb["cmpb"][:, :], sb["eqm"][:, :], Alu.mult),
      inc="vec")
    SS()
    for e_ in range(NE):
        E("vector", lambda e, e_=e_: e.tensor_scalar(
            sb["selmask"][:, e_ * 128:(e_ + 1) * 128], sb["S2"][:, e_ * 128:(e_ + 1) * 128],
            sb["tbits"].bitcast(F32)[:, e_:e_ + 1], None, op0=Alu.is_gt), inc="vec" if e_ == NE - 1 else None)
    SS()
    E("vector", lambda e: e.tensor_tensor(sb["selmask"][:, :], sb["selmask"][:, :], sb["cmpb"][:, :],
                                          Alu.add), inc="vec")
    SS()
    b.sfree("xsp", "mflt", "iot", "eqm", "tvalf", "cntf", "onescF", "tbscr", "tbits", "red", "cmpb",
            "ones4", "sbyte", "sint")


def stage_r2(b):
    p, sb, g, ps = b.p, b.sb, b.g, b.ps
    E, W = p.emit, p.wait
    from concourse.bass_core_rust import IndirectOffsetOnAxis

    def SS(eng="vector", sem="vec"):
        W(eng, sem)

    b.dram_tmp("IL_dram", [NE, 128, 32, 2])
    b.dram_tmp("ILfull_dram", [NE, 128, 32, 2])
    b.sbuf("LTu", [128, NE * 16], U32)
    b.sbuf("scanb", [128, 128], F32, side="right")
    b.sbuf("rank", [128, 128], F32, side="right")
    b.sbuf("rtot", [128, 4], F32, side="right")
    b.sbuf("tri", [128, 128], F32, side="right")
    b.sbuf("onesP", [128, 128], F32, side="right")
    b.sbuf("i1", [128, 128], I32, side="right")
    b.sbuf("i2", [128, 128], I32, side="right")
    b.sbuf("exi", [128, 8 * 16], F32, side="right")
    b.sbuf("exs", [128, 64], F32, side="right")
    b.sbuf("pairs", [128, 16 * 2], F32, side="right")
    b.sbuf("dsti", [128, 16], U32, side="right")
    b.sbuf("iscr", [128, 64], I32, side="right")
    b.sbuf("zz", [128, 512], F32, side="right")

    p.fence(("gpsimd", "sync"))
    E("gpsimd", lambda e: e.iota(sb["i1"][:, :], pattern=[[0, 128]], base=0, channel_multiplier=1))
    E("gpsimd", lambda e: e.iota(sb["i2"][:, :], pattern=[[1, 128]], base=0, channel_multiplier=0), inc="qdma")
    W("vector", "qdma")
    E("vector", lambda e: e.tensor_tensor(sb["tri"][:, :], sb["i1"][:, :], sb["i2"][:, :], Alu.is_lt))
    E("vector", lambda e: e.memset(sb["onesP"][:, :], 1.0))
    E("vector", lambda e: e.memset(sb["zz"][:, :], 0.0), inc="vec")
    W("sync", "vec")
    E("sync", lambda e: e.dma_start(out=g["IL_dram"][:, :, :, :], in_=sb["zz"][:, :]), inc="sdma", inc_n=16)
    zero_done = p.val("sdma")

    bcr = [None]

    def mk_bcr(e):
        if bcr[0] is None:
            bcr[0] = e.alloc_register("bcreg")
            e.reg_mov(bcr[0], 32767)
        return bcr[0]

    scat_cnt = [0]
    scat_after = {}
    for e_ in range(NE):
        if e_ > 0:
            W("vector", "gdma", scat_after[e_ - 1])
        msl = sb["selmask"][:, e_ * 128:(e_ + 1) * 128]
        W("vector", "vec")
        E("vector", lambda e, msl=msl: e.tensor_tensor_scan(sb["scanb"][:, :], sb["onesP"][:, :], msl,
                                                            op0=Alu.mult, op1=Alu.add, initial=0.0), inc="vec")
        SS()
        E("vector", lambda e: e.tensor_copy(sb["rtot"][:, 0:1], sb["scanb"][:, 127:128]), inc="vec")
        W("tensor", "vec")
        E("tensor", lambda e: e.matmul(ps["pG"][:, 0:1], sb["tri"][:, :], sb["rtot"][:, 0:1],
                                       start=True, stop=True), inc="ten")
        W("vector", "ten")
        E("vector", lambda e: e.tensor_copy(sb["rtot"][:, 1:2], ps["pG"][:, 0:1]), inc="vec")
        SS()
        E("vector", lambda e, msl=msl: e.tensor_tensor(sb["rank"][:, :], sb["scanb"][:, :], msl,
                                                       Alu.subtract), inc="vec")
        SS()
        E("vector", lambda e: e.tensor_scalar(sb["rank"][:, :], sb["rank"][:, :], sb["rtot"][:, 1:2], None,
                                              op0=Alu.add), inc="vec")
        SS()
        if "r2dbg" in b.DBG and e_ == 0:
            W("sync", "vec")
            E("sync", lambda e: e.dma_start(out=b.DBG["r2dbg"][0:128, 0:128], in_=sb["rank"][:, :]), inc="odma", inc_n=16)
            E("sync", lambda e: e.dma_start(out=b.DBG["r2dbg"][0:128, 128:132], in_=sb["rtot"][:, 0:4]), inc="odma", inc_n=16)
            E("sync", lambda e: e.dma_start(out=b.DBG["r2dbg"][0:128, 132:260], in_=sb["scanb"][:, :]), inc="odma", inc_n=16)
            E("sync", lambda e: e.dma_start(out=b.DBG["r2dbg"][0:128, 260:264], in_=sb["tri"][:, 40:44]), inc="odma", inc_n=16)
            W("vector", "odma")
        E("vector", lambda e: e.tensor_copy(sb["i1"][:, :], sb["rank"][:, :]), inc="vec")
        SS()
        E("vector", lambda e: e.tensor_scalar(sb["i2"][:, :], sb["i1"][:, :], 6, None,
                                              op0=Alu.logical_shift_right), inc="vec")
        SS()
        E("vector", lambda e: e.tensor_scalar(sb["i1"][:, :], sb["i1"][:, :], 63, None,
                                              op0=Alu.bitwise_and), inc="vec")
        SS()
        E("vector", lambda e: e.tensor_copy(sb["scanb"][:, :], sb["i2"][:, :]))
        E("vector", lambda e: e.tensor_copy(sb["rank"][:, :], sb["i1"][:, :]), inc="vec")
        ex_ready = p.val("vec")
        W("tensor", "vec", ex_ready)
        E("tensor", lambda e: e.matmul(ps["pG"][:, 0:16], sb["scanb"][:, :], g_sel(b), start=True, stop=True))
        E("tensor", lambda e: e.matmul(ps["pG"][:, 16:32], sb["rank"][:, :], g_sel(b), start=True, stop=True))
        E("tensor", lambda e, msl=msl: e.matmul(ps["pG"][:, 32:48], msl, g_sel(b), start=True, stop=True))
        E("tensor", lambda e, e_=e_: e.matmul(ps["pG"][:, 48:64], sb["S2"][:, e_ * 128:(e_ + 1) * 128],
                                              g_sel(b), start=True, stop=True), inc="ten")
        W("vector", "ten")
        E("vector", lambda e: e.tensor_copy(sb["exs"][:, 0:64], ps["pG"][:, 0:64]), inc="vec")
        SS()
        E("vector", lambda e: e.tensor_copy(sb["iscr"][:, 0:16], sb["exs"][:, 0:16]))
        E("vector", lambda e: e.tensor_copy(sb["iscr"][:, 16:32], sb["exs"][:, 16:32]), inc="vec")
        SS()
        E("vector", lambda e: e.tensor_scalar(sb["iscr"][:, 0:16], sb["iscr"][:, 0:16], 6, None,
                                              op0=Alu.logical_shift_left), inc="vec")
        SS()
        E("vector", lambda e: e.tensor_tensor(sb["iscr"][:, 0:16], sb["iscr"][:, 0:16], sb["iscr"][:, 16:32],
                                              Alu.bitwise_or), inc="vec")
        SS()
        E("vector", lambda e: e.tensor_copy(sb["iscr"][:, 32:48], sb["exs"][:, 32:48]), inc="vec")
        SS()
        E("vector", lambda e, e_=e_: e.memset(sb["LTu"][:, e_ * 16:(e_ + 1) * 16], 4096), inc="vec")
        SS()
        E("vector", lambda e, e_=e_: e.select(sb["LTu"][:, e_ * 16:(e_ + 1) * 16].bitcast(I32),
                                              sb["iscr"][:, 32:48], sb["iscr"][:, 0:16],
                                              sb["LTu"][:, e_ * 16:(e_ + 1) * 16].bitcast(I32)), inc="vec")
        SS()
        E("vector", lambda e: e.tensor_scalar(sb["iscr"][:, 48:64], sb["iscr"][:, 0:16], 127, None,
                                              op0=Alu.bitwise_and), inc="vec")
        SS()
        E("vector", lambda e: e.tensor_scalar(sb["iscr"][:, 48:64], sb["iscr"][:, 48:64], 5, None,
                                              op0=Alu.logical_shift_left), inc="vec")
        SS()
        E("vector", lambda e: e.tensor_scalar(sb["iscr"][:, 16:32], sb["iscr"][:, 0:16], 7, None,
                                              op0=Alu.logical_shift_right), inc="vec")
        SS()
        E("vector", lambda e: e.tensor_tensor(sb["iscr"][:, 48:64], sb["iscr"][:, 48:64], sb["iscr"][:, 16:32],
                                              Alu.bitwise_or), inc="vec")
        SS()
        E("vector", lambda e, e_=e_: e.tensor_scalar(sb["iscr"][:, 48:64], sb["iscr"][:, 48:64],
                                                     e_ << 12, None, op0=Alu.bitwise_or), inc="vec")
        SS()
        E("vector", lambda e: e.memset(sb["dsti"][:, :], 0x7FFFFFFF), inc="vec")
        SS()
        E("vector", lambda e: e.select(sb["dsti"][:, :].bitcast(I32), sb["iscr"][:, 32:48],
                                       sb["iscr"][:, 48:64], sb["dsti"][:, :].bitcast(I32)), inc="vec")
        SS()
        E("vector", lambda e: e.tensor_copy(sb["pairs"][:, :].rearrange("p (c t) -> p c t", t=2)[:, :, 0],
                                            g_idrow_s(b)))
        E("vector", lambda e: e.tensor_copy(sb["pairs"][:, :].rearrange("p (c t) -> p c t", t=2)[:, :, 1],
                                            sb["exs"][:, 48:64]), inc="vec")
        pair_done = p.val("vec")
        W("gpsimd", "vec", pair_done)
        if e_ == 0:
            W("gpsimd", "sdma", zero_done)
        for c in range(16):
            E("gpsimd", lambda e, e_=e_, c=c: e.indirect_dma_start(
                out=g["IL_dram"].rearrange("n p b t -> (n p b) t"),
                out_offset=IndirectOffsetOnAxis(sb["dsti"][:, c:c + 1], 0),
                in_=sb["pairs"][:, 2 * c:2 * c + 2], in_offset=None,
                bounds_check=mk_bcr(e), oob_is_err=False), inc="gdma", inc_n=16)
            scat_cnt[0] += 16
        scat_after[e_] = p.val("gdma")
    if "ILown" in b.DBG:
        W("sync", "gdma")
        E("sync", lambda e: e.dma_start(out=b.DBG["ILown"][:, :, :, :], in_=g["IL_dram"][:, :, :, :]),
          inc="odma", inc_n=16)
        W("sync", "vec")
        E("sync", lambda e: e.dma_start(out=b.DBG["dst0"][:, :], in_=sb["dsti"][:, :]), inc="odma", inc_n=16)
    W("gpsimd", "gdma")
    E("gpsimd", lambda e: e.collective_compute(
        "AllReduce", Alu.add, replica_groups=[list(range(8))],
        ins=[g["IL_dram"].opt()], outs=[g["ILfull_dram"].opt()]), inc="cc")
    b.sfree("zz", "iscr", "dsti", "pairs", "exs", "exi", "i2", "i1", "onesP", "tri", "rtot", "rank", "scanb")


def g_sel(b):
    return b.sb["selin"][:, :]


def g_idrow_s(b):
    return b.sb["idrowin"][:, :]


def stage_b(b):
    p, sb, g, ps = b.p, b.sb, b.g, b.ps
    E, W = p.emit, p.wait
    from concourse.bass_core_rust import IndirectOffsetOnAxis

    def SS(eng="vector", sem="vec"):
        W(eng, sem)

    b.dram_tmp("Zloc", [CAP + 1, D], BF16)
    b.dram_tmp("Zfull", [NE * (CAP + 1), D], BF16, shared=True)
    b.sbuf("w13", [128, NC * 2 * EH], BF16)
    b.sbuf("w2s", [128, 16 * D], BF16)
    b.sbuf("ILs", [128, 64], F32, side="right")
    b.sbuf("idu", [128, 32], U32, side="right")
    b.sbuf("sclb", [128, 32], F32, side="right")
    b.sbuf("eoffs", [128, 1], U32, side="right")
    b.sbuf("yt", [128, 2 * D], BF16, side="right")
    b.sbuf("rt", [128, D], BF16, side="right")
    b.sbuf("r_fm", [128, NC * 512], BF16, side="right")
    b.sbuf("hh", [128, 16 * 512], BF16, side="right")
    b.sbuf("silb", [128, 512], BF16, side="right")
    b.sbuf("ztile", [128, 1024], BF16, side="right")
    b.sbuf("zrow", [1, D], BF16, side="right")

    p.fence(("gpsimd", "sync"))
    for k in range(NC):
        E("gpsimd", lambda e, k=k: e.dma_start(out=sb["w13"][:, k * EH:(k + 1) * EH],
                                               in_=g["ew1"][k * 128:(k + 1) * 128, :]), inc="gdma", inc_n=16)
    for k in range(NC):
        E("gpsimd", lambda e, k=k: e.dma_start(out=sb["w13"][:, NC * EH + k * EH:NC * EH + (k + 1) * EH],
                                               in_=g["ew3"][k * 128:(k + 1) * 128, :]), inc="gdma", inc_n=16)
    for k in range(16):
        E("gpsimd", lambda e, k=k: e.dma_start(out=sb["w2s"][:, k * D:(k + 1) * D],
                                               in_=g["ew2"][k * 128:(k + 1) * 128, :]), inc="gdma", inc_n=16)
    E("sync", lambda e: e.dma_start(out=sb["eoffs"][:, :], in_=g["eoff"][:, :]), inc="sdma", inc_n=16)
    W("vector", "vec")
    E("vector", lambda e: e.memset(sb["zrow"][0:1, :], 0.0), inc="vec")
    W("sync", "vec")
    E("sync", lambda e: e.dma_start(out=g["Zloc"][CAP:CAP + 1, :], in_=sb["zrow"][0:1, :]), inc="sdma", inc_n=16)
    W("gpsimd", "cc", 3)
    W("gpsimd", "sdma")
    E("gpsimd", lambda e: e.indirect_dma_start(
        out=sb["ILs"][:, :], out_offset=None,
        in_=g["ILfull_dram"].rearrange("n p b t -> (n p) (b t)"),
        in_offset=IndirectOffsetOnAxis(sb["eoffs"][:, 0:1], 0)), inc="gdma", inc_n=16)
    W("vector", "gdma")
    E("vector", lambda e: e.tensor_copy(sb["idu"][:, :],
                                        sb["ILs"][:, :].rearrange("p (c t) -> p c t", t=2)[:, :, 0]))
    E("vector", lambda e: e.tensor_copy(sb["sclb"][:, :],
                                        sb["ILs"][:, :].rearrange("p (c t) -> p c t", t=2)[:, :, 1]), inc="vec")
    idu_done = p.val("vec")
    if "bdump" in b.DBG:
        W("sync", "vec")
        E("sync", lambda e: e.dma_start(out=b.DBG["bdump"][0:128, 0:32], in_=sb["idu"][:, :].bitcast(F32)),
          inc="odma", inc_n=16)
        E("sync", lambda e: e.dma_start(out=b.DBG["bdump"][0:128, 32:96], in_=sb["ILs"][:, :]), inc="odma", inc_n=16)
        W("vector", "odma")

    NCH = CAP // 512
    gat_done = {}
    rt_used = {}
    tev_b = {}
    zev = {}
    zcopy_done = {}
    rfm_done = {}
    uv_done = {}
    z_dma_done = {}
    for ch in range(NCH):
        for j in range(4):
            bslot = ch * 4 + j
            ybuf = (bslot % 2) * D
            W("gpsimd", "vec", idu_done)
            if bslot >= 2:
                W("gpsimd", "scl", gat_done[bslot - 2])
            E("gpsimd", lambda e, bslot=bslot, ybuf=ybuf: e.indirect_dma_start(
                out=sb["yt"][:, ybuf:ybuf + D], out_offset=None,
                in_=g["Yfull_dram"][:, :],
                in_offset=IndirectOffsetOnAxis(sb["idu"][:, bslot:bslot + 1], 0)), inc="gdma", inc_n=16)
            W("scalar", "gdma")
            if bslot >= 1:
                W("scalar", "ten", rt_used[bslot - 1])
            E("scalar", lambda e, bslot=bslot, ybuf=ybuf: e.activation(
                out=sb["rt"][:, :], in_=sb["yt"][:, ybuf:ybuf + D], func=AF.Identity,
                scale=sb["sclb"][:, bslot:bslot + 1]), inc="scl")
            gat_done[bslot] = p.val("scl")
            if "bdump" in b.DBG and bslot == 0:
                W("sync", "scl")
                E("sync", lambda e, ybuf=ybuf: e.dma_start(out=b.DBG["bdump"][0:128, 96:96 + 512],
                                                           in_=sb["yt"][:, ybuf:ybuf + D].bitcast(F32)),
                  inc="odma", inc_n=16)
                W("vector", "odma")
                W("scalar", "odma")
            W("tensor", "scl", gat_done[bslot])
            for k in range(NC):
                pk = ps["pG"] if k % 2 == 0 else ps["pH"]
                gib = bslot * NC + k
                W("tensor", "vec")
                if gib >= 2:
                    W("tensor", "scl", tev_b.get(gib - 2, 0))
                E("tensor", lambda e, k=k, pk=pk: e.transpose(
                    pk[:, 0:64].bitcast(BF16), sb["rt"][:, k * 128:(k + 1) * 128], sb["identb"][:, :]),
                  inc="ten")
                W("scalar", "ten")
                E("scalar", lambda e, k=k, j=j, pk=pk: e.activation(
                    out=sb["r_fm"][:, k * 512 + j * 128:k * 512 + (j + 1) * 128],
                    in_=pk[:, 0:64].bitcast(BF16), func=AF.Identity), inc="scl")
                tev_b[gib] = p.val("scl")
            rt_used[bslot] = p.val("ten")
        rfm_done[ch] = p.val("scl")
        W("tensor", "scl", rfm_done[ch])
        if ch >= 1:
            W("tensor", "vec", zcopy_done[ch - 1])
        for hc in range(16):
            pu = ps["pA"] if hc % 2 == 0 else ps["pB"]
            pv = ps["pC"] if hc % 2 == 0 else ps["pD"]
            if hc >= 2:
                W("tensor", "vec", uv_done[(ch, hc - 2)])
            for k in range(NC):
                E("tensor", lambda e, hc=hc, k=k, pu=pu: e.matmul(
                    pu[:, :], sb["w13"][:, k * EH + hc * 128:k * EH + (hc + 1) * 128],
                    sb["r_fm"][:, k * 512:(k + 1) * 512], start=(k == 0), stop=(k == NC - 1)))
            for k in range(NC):
                E("tensor", lambda e, hc=hc, k=k, pv=pv: e.matmul(
                    pv[:, :], sb["w13"][:, NC * EH + k * EH + hc * 128:NC * EH + k * EH + (hc + 1) * 128],
                    sb["r_fm"][:, k * 512:(k + 1) * 512], start=(k == 0), stop=(k == NC - 1)), 
                  inc="ten" if k == NC - 1 else None)
            W("scalar", "ten")
            if hc >= 1:
                W("scalar", "vec", uv_done[(ch, hc - 1)])
            E("scalar", lambda e, pu=pu: e.activation(out=sb["silb"][:, :], in_=pu[:, :], func=AF.Silu),
              inc="scl")
            W("vector", "scl")
            E("vector", lambda e, hc=hc, pv=pv: e.tensor_tensor(
                sb["hh"][:, hc * 512:(hc + 1) * 512], sb["silb"][:, :], pv[:, :], Alu.mult), inc="vec")
            uv_done[(ch, hc)] = p.val("vec")
        W2B = [ps["pA"], ps["pB"], ps["pC"], ps["pD"], ps["pE"], ps["pF"], ps["pG"], ps["pH"]]
        W("tensor", "vec", uv_done[(ch, 15)])
        for ms in range(4):
            for n in range(2):
                bank = W2B[ms * 2 + n]
                for hc in range(16):
                    E("tensor", lambda e, ms=ms, n=n, hc=hc, bank=bank: e.matmul(
                        bank[:, :], sb["hh"][:, hc * 512 + ms * 128:hc * 512 + (ms + 1) * 128],
                        sb["w2s"][:, hc * D + n * 512:hc * D + (n + 1) * 512],
                        start=(hc == 0), stop=(hc == 15)),
                      inc="ten" if (ms == 3 and n == 1 and hc == 15) else None)
        W("vector", "ten")
        if ch >= 1:
            W("vector", "odma", z_dma_done[ch - 1])
        for ms in range(4):
            for n in range(2):
                zi = ch * 8 + ms * 2 + n
                zslot = (zi % 2) * 512
                if zi >= 2:
                    W("vector", "odma", zev.get(zi - 2, 0))
                E("vector", lambda e, ms=ms, n=n, zslot=zslot: e.tensor_copy(
                    sb["ztile"][:, zslot:zslot + 512], W2B[ms * 2 + n][:, :]), inc="vec")
                W("scalar", "vec")
                E("scalar", lambda e, ch=ch, ms=ms, n=n, zslot=zslot: e.dma_start(
                    out=g["Zloc"][ch * 512 + ms * 128:ch * 512 + (ms + 1) * 128, n * 512:(n + 1) * 512],
                    in_=sb["ztile"][:, zslot:zslot + 512]), inc="odma", inc_n=16)
                zev[zi] = p.val("odma")
        z_dma_done[ch] = p.val("odma")
        zcopy_done[ch] = p.val("vec")
    W("gpsimd", "odma")
    E("gpsimd", lambda e: e.collective_compute(
        "AllGather", Alu.bypass, replica_groups=[list(range(8))],
        ins=[g["Zloc"].opt()], outs=[g["Zfull"].opt()]), inc="cc")
    b.sfree("zrow", "ztile", "silb", "hh", "r_fm", "rt", "yt", "eoffs", "sclb", "idu", "ILs")
    b.sfree("w2s", "w13")


def stage_c(b):
    p, sb, g, ps = b.p, b.sb, b.g, b.ps
    E, W = p.emit, p.wait
    from concourse.bass_core_rust import IndirectOffsetOnAxis

    def SS(eng="vector", sem="vec"):
        W(eng, sem)

    b.sbuf("acc", [128, NT * D], F32)
    b.sbuf("zt2", [128, 4 * D], BF16, side="right")
    b.sbuf("cidx", [128, 16], U32, side="right")
    b.sbuf("oscr", [128, D], F32, side="right")
    p.fence(("gpsimd", "sync"))
    W("sync", "odma")
    E("sync", lambda e: e.dma_start(out=sb["acc"][:, :].rearrange("p (c d) -> p c d", d=D),
                                    in_=g["y_dram"].rearrange("(c p) d -> p c d", p=128)), inc="sdma", inc_n=16)
    W("vector", "sdma")
    W("gpsimd", "cc", 4)
    add_done = {}
    gath_done = {}
    for e_ in range(NE):
        W("vector", "gdma")
        E("vector", lambda e, e_=e_: e.tensor_scalar(sb["cidx"][:, :], sb["LTu"][:, e_ * 16:(e_ + 1) * 16],
                                                     float(e_ * (CAP + 1)), None, op0=Alu.add), inc="vec")
        cidx_done = p.val("vec")
        for c in range(NT):
            slot = ((e_ * NT + c) % 2) * D
            W("gpsimd", "vec", cidx_done)
            if e_ * NT + c >= 2:
                W("gpsimd", "vec", add_done[e_ * NT + c - 2])
            E("gpsimd", lambda e, c=c, slot=slot: e.indirect_dma_start(
                out=sb["zt2"][:, slot:slot + D], out_offset=None,
                in_=g["Zfull"][:, :],
                in_offset=IndirectOffsetOnAxis(sb["cidx"][:, c:c + 1], 0)), inc="gdma", inc_n=16)
            W("vector", "gdma")
            E("vector", lambda e, c=c, slot=slot: e.tensor_tensor(
                sb["acc"][:, c * D:(c + 1) * D], sb["acc"][:, c * D:(c + 1) * D],
                sb["zt2"][:, slot:slot + D], Alu.add), inc="vec")
            add_done[e_ * NT + c] = p.val("vec")
    out_dma = {}
    for c in range(NT):
        W("vector", "vec", add_done[7 * NT + c])
        slotl = c % 2
        if c >= 1:
            W("scalar", "odma", out_dma[c - 1])
        ln_tile(b, sb["acc"][:, c * D:(c + 1) * D], sb["oscr"][:, :], slotl)
        W("scalar", "scl")
        E("scalar", lambda e, c=c: e.dma_start(out=g["y_out"][c * 128:(c + 1) * 128, :],
                                               in_=sb["oscr"][:, :]), inc="odma", inc_n=16)
        out_dma[c] = p.val("odma")
    W("scalar", "odma")
    W("vector", "odma")
    W("sync", "odma")


from concourse.bass_utils import run_bass_kernel_spmd

_KEY = {
    "Wq": "L0_attn_Wq", "Wk": "L0_attn_Wk", "Wv": "L0_attn_Wv", "Wfq": "L0_attn_Wfq", "Wfk": "L0_attn_Wfk",
    "bq": "L0_attn_bq", "gq": "L0_attn_gq", "betaq": "L0_attn_betaq", "bk": "L0_attn_bk", "gk": "L0_attn_gk",
    "betak": "L0_attn_betak", "bv": "L0_attn_bv", "gv": "L0_attn_gv", "betav": "L0_attn_betav",
    "ga": "L0_attn_ga", "ba": "L0_attn_ba", "bfq": "L0_attn_bfq", "bfk": "L0_attn_bfk", "b2": "L0_attn_b2",
    "W1": "L0_attn_W1", "b1": "L0_attn_b1", "W2": "L0_attn_W2", "wg": "L0_ffn_wg",
}

_NC_CACHE = {}


def _build():
    if "nc" not in _NC_CACHE:
        b = B()
        declare(b)
        stage_consts(b)
        stage_a0(b)
        stage_a1(b)
        stage_a2(b)
        stage_a3(b)
        stage_a4(b, dev_y=False)
        stage_r1(b)
        stage_r2(b)
        stage_b(b)
        stage_c(b)
        _NC_CACHE["nc"] = finish(b)
    return _NC_CACHE["nc"]


def kernel(**inputs):
    nc = _build()
    inp = {k: np.ascontiguousarray(np.asarray(v)) for k, v in inputs.items()}
    shared = {}
    for mine, theirs in _KEY.items():
        a = inp[theirs]
        shared[mine] = a.reshape(1, -1).astype(np.float32) if a.ndim == 1 else a.astype(np.float32)
    x = inp["L0_attn_x"].astype(np.float32)
    w1f, w3f, w2f = inp["L0_ffn_w1"], inp["L0_ffn_w3"], inp["L0_ffn_w2"]
    in_maps = []
    for i in range(8):
        sel = np.zeros((128, 16), np.float32)
        for j in range(16):
            sel[16 * i + j, j] = 1.0
        idrow = ((16 * i + np.arange(16))[None, :] * 128 + np.arange(128)[:, None]).astype(np.float32)
        eoff = (i * 128 + np.arange(128))[:, None].astype(np.uint32)
        in_maps.append(dict(shared, x=np.ascontiguousarray(x[i]), sel=sel, idrow=idrow, eoff=eoff,
                            ew1=np.ascontiguousarray(w1f[i]).astype(np.float32),
                            ew3=np.ascontiguousarray(w3f[i]).astype(np.float32),
                            ew2=np.ascontiguousarray(w2f[i]).astype(np.float32)))
    res = run_bass_kernel_spmd(nc, in_maps, core_ids=list(range(8)))
    return np.stack([np.asarray(res.results[i]["y_out"]) for i in range(8)], 0)
